# version 05
# baseline (speedup 1.0000x reference)
import numpy as np
import concourse.bass as bass
import concourse.mybir as mybir
from concourse.bass_utils import run_bass_kernel_spmd
from contextlib import ExitStack
import ml_dtypes

F32 = mybir.dt.float32
BF16 = mybir.dt.bfloat16
AF = mybir.ActivationFunctionType
ALU = mybir.AluOpType
AXX = mybir.AxisListType.X
BF = ml_dtypes.bfloat16

ENGINES = ("tensor", "vector", "scalar", "gpsimd", "sync")

D = 1024
T = 4096
NB = 2
NH = 16
HD = 64
DFF = 4096
EPS = 1e-5


class Sem:
    __slots__ = ("h", "n", "name")

    def __init__(self, h, name):
        self.h = h
        self.n = 0
        self.name = name

    def bump(self, k):
        self.n += k
        return self.n


class KB:

    def __init__(self, nc, ctx):
        self.nc = nc
        self.ctx = ctx
        self.q = {e: [] for e in ENGINES}
        self._sems = {}
        self.dma_sems = set()
        self.barrier_sem = self.sem("kb_barrier")

    def sem(self, name):
        if name not in self._sems:
            h = self.ctx.enter_context(self.nc.semaphore(name))
            self._sems[name] = Sem(h, name)
        return self._sems[name]

    def op(self, eng, fn, wait=(), inc=(), drain=False):
        wait = tuple((s, v) for s, v in wait)
        planned = []
        inc2 = []
        for s, k in inc:
            planned.append(s.bump(k))
            inc2.append((s, k))

        def thunk(e, fn=fn, wait=wait, inc2=tuple(inc2), drain=drain):
            for s, v in wait:
                e.wait_ge(s.h, v)
            inst = fn(e)
            if drain:
                e.drain()
                for s, k in inc2:
                    e.sem_inc(s.h, k)
            else:
                for s, k in inc2:
                    assert inst is not None, "inc attached to op returning None"
                    inst = inst.then_inc(s.h, k)
            return inst

        self.q[eng].append(thunk)
        return planned

    def wait(self, eng, sem, val):
        self.op(eng, lambda e: None, wait=[(sem, val)])

    def chain(self, eng, fn, s, extra_wait=(), drain=False):
        v = s.n
        self.op(eng, fn, wait=[(s, v)] + list(extra_wait), inc=[(s, 1)], drain=drain)
        return s.n

    def dma(self, queue, out, in_, sem=None, wait=(), new_group=True, **kw):
        s = sem if sem is not None else self.sem("dma_" + queue)
        self.dma_sems.add(s)
        w = list(wait)
        if new_group:
            w.append((s, s.n))
        v = self.op(queue, lambda e: e.dma_start(out=out, in_=in_, **kw),
                    wait=w, inc=[(s, 16)])
        return s, v[0]

    def barrier(self, engines=ENGINES, name=None):
        arr = self.barrier_sem
        active = [e for e in engines]
        dma_waits = [(s, s.n) for s in self.dma_sems if s.n > 0]
        for e in active:
            def arrive(eng, e=e):
                if e != "sync":
                    eng.drain()
                return eng.sem_inc(arr.h, 1)
            self.op(e, arrive)
            arr.bump(1)
        tgt = arr.n
        for e in active:
            def waitall(eng, tgt=tgt, dma_waits=tuple(dma_waits)):
                eng.wait_ge(arr.h, tgt)
                for s, v in dma_waits:
                    eng.wait_ge(s.h, v)
            self.q[e].append(waitall)

    def finalize(self):
        block = self.ctx.enter_context(self.nc.Block())
        for e in ENGINES:
            ops = self.q[e]
            if not ops:
                continue
            def runner(eng, ops=ops):
                for t in ops:
                    t(eng)
            getattr(block, e)(runner)



def prep_weights(inp):
    w = {}
    w["WqT"] = np.ascontiguousarray(inp["L0_attn_Wq"].T)
    w["WkT"] = np.ascontiguousarray(inp["L0_attn_Wk"].T)
    w["WvT"] = np.ascontiguousarray(inp["L0_attn_Wv"].T)
    w["WoT"] = np.ascontiguousarray(inp["L0_attn_Wo"].T)
    w["wiT"] = np.ascontiguousarray(inp["L0_ffn_wi"].T)
    w["woT"] = np.ascontiguousarray(inp["L0_ffn_wo"].T)
    w["bi"] = np.ascontiguousarray(inp["L0_ffn_bi"].reshape(32, 128))
    w["bo_row"] = np.ascontiguousarray(inp["L0_ffn_bo"].reshape(1, 1024))
    w["loraAT"] = np.ascontiguousarray(inp["L0_ffn_lora_A"].reshape(128, 1024).T)
    w["loraBT"] = np.ascontiguousarray(
        inp["L0_ffn_lora_B"].transpose(0, 2, 1).reshape(128, DFF))
    w["gate_wT"] = np.ascontiguousarray(inp["L0_ffn_gate_w"].T)
    w["gate_b_t"] = np.ascontiguousarray(np.tile(inp["L0_ffn_gate_b"].reshape(1, 8), (128, 1)))
    Wxp = inp["L1_attn_Wxproj"]
    w["WdtpT"] = np.ascontiguousarray(Wxp[:64].T)
    w["WBT"] = np.ascontiguousarray(Wxp[64:80].T)
    w["WCT"] = np.ascontiguousarray(Wxp[80:96].T)
    w["WxsT"] = np.ascontiguousarray(inp["L1_attn_Wxs"].T)
    w["WdtT"] = np.ascontiguousarray(inp["L1_attn_Wdt"].T)
    w["bdt"] = np.ascontiguousarray(inp["L1_attn_bdt"].reshape(1, 1))
    w["A_log"] = np.ascontiguousarray(inp["L1_attn_A_log"].reshape(1, 16))
    w["Dp_col"] = np.ascontiguousarray(inp["L1_attn_Dp"].reshape(8, 128))
    w["WoutT"] = np.ascontiguousarray(inp["L1_attn_Wout"].T)
    w["WQT"] = np.ascontiguousarray(inp["L1_ffn_W_Q"].T)
    w["WKT"] = np.ascontiguousarray(inp["L1_ffn_W_K"].T)
    w["WOT"] = np.ascontiguousarray(inp["L1_ffn_W_O"].T)
    w["WinvT"] = np.ascontiguousarray(inp["L1_ffn_W_inv"].T)
    w["Vf_T"] = np.ascontiguousarray(inp["L1_ffn_V_fwd"].transpose(1, 0, 2).reshape(D, 512))
    w["Wf_T"] = np.ascontiguousarray(inp["L1_ffn_W_fwd"].transpose(1, 0, 2).reshape(D, 512))
    w["Uf_flat"] = np.ascontiguousarray(inp["L1_ffn_U_fwd"].transpose(0, 2, 1).reshape(512, D))
    w["bf_k"] = np.ascontiguousarray(inp["L1_ffn_b_fwd"])
    w["Vi_T"] = np.ascontiguousarray(inp["L1_ffn_W_inv_exp"].transpose(1, 0, 2).reshape(D, 512))
    w["Wi_T"] = np.ascontiguousarray(inp["L1_ffn_V_inv"].transpose(1, 0, 2).reshape(D, 512))
    w["Ui_flat"] = np.ascontiguousarray(inp["L1_ffn_U_inv"].transpose(0, 2, 1).reshape(512, D))
    w["bi_k"] = np.ascontiguousarray(inp["L1_ffn_b_inv"])
    w["rw1T"] = np.ascontiguousarray(inp["L1_ffn_router_w1"].T)
    w["rb1"] = np.ascontiguousarray(inp["L1_ffn_router_b1"].reshape(8, 128))
    w["rw2T"] = np.ascontiguousarray(inp["L1_ffn_router_w2"].T)
    w["rb2"] = np.ascontiguousarray(inp["L1_ffn_router_b2"].reshape(1, 8))
    w["expb"] = np.ascontiguousarray(inp["L1_ffn_expert_bias"].reshape(1, 8))
    w["rb2_t"] = np.ascontiguousarray(np.tile(inp["L1_ffn_router_b2"].reshape(1, 8), (128, 1)))
    w["expb_t"] = np.ascontiguousarray(np.tile(inp["L1_ffn_expert_bias"].reshape(1, 8), (128, 1)))
    w["alpha"] = np.ascontiguousarray(np.asarray(inp["L1_ffn_alpha_bi"]).reshape(1, 1))
    return w


def host_constants():
    c = {}
    c["identity_bf"] = np.eye(128, dtype=BF)
    c["ones_col"] = np.ones((128, 1), np.float32)
    c["ones_col_bf"] = np.ones((128, 1), BF)
    c["ones_row"] = np.ones((1, 128), np.float32)
    c["inv_n"] = (1.0 / np.arange(1, T + 1, dtype=np.float64)).astype(np.float32).reshape(1, T)
    c["tie_eps"] = np.tile((np.arange(8, dtype=np.float32) * -1e-5).reshape(1, 8), (128, 1))
    R = np.zeros((8, 128), np.float32)
    for e in range(8):
        R[e, e * 16:(e + 1) * 16] = 1.0
    c["R_lora"] = R
    Rk = np.zeros((8, 512), np.float32)
    for k in range(8):
        Rk[k, k * 64:(k + 1) * 64] = 1.0
    c["R_kr"] = Rk
    return c



LN_CH = 512


def make_ln_bufs(ctx, nc, name, with_default_fill=None):
    CH = LN_CH
    bufs = {"name": name}
    if with_default_fill is not None:
        bufs["rin"] = [ctx.enter_context(nc.sbuf_tensor(f"{name}_rin{i}", [128, 8 * CH], F32)) for i in range(2)]
        if with_default_fill != "none":
            bufs["ain"] = [ctx.enter_context(nc.sbuf_tensor(f"{name}_ain{i}", [128, 8 * CH], with_default_fill)) for i in range(2)]
    bufs["zb"] = [ctx.enter_context(nc.sbuf_tensor(f"{name}_z{i}", [128, 8 * CH], F32)) for i in range(2)]
    bufs["zbf"] = ctx.enter_context(nc.sbuf_tensor(name + "_zbf", [128, 8 * CH], BF16))
    bufs["zsq"] = ctx.enter_context(nc.sbuf_tensor(name + "_zsq", [128, 8 * CH], BF16))
    bufs["rows"] = ctx.enter_context(nc.sbuf_tensor(name + "_rows", [1, 8 * CH], F32))
    bufs["xo"] = [ctx.enter_context(nc.sbuf_tensor(f"{name}_xo{i}", [128, 8 * CH], F32)) for i in range(2)]
    bufs["xob"] = [ctx.enter_context(nc.sbuf_tensor(f"{name}_xob{i}", [128, 8 * CH], BF16)) for i in range(2)]
    return bufs


def make_psum_pool(ctx, nc):
    ps = {}
    ps["ps_sz"] = ctx.enter_context(nc.psum_tensor("g_psz", [1, LN_CH], F32))
    ps["ps_sq"] = ctx.enter_context(nc.psum_tensor("g_psq", [1, LN_CH], F32))
    ps["psA"] = ctx.enter_context(nc.psum_tensor("g_psA", [128, LN_CH], F32))
    ps["psB"] = ctx.enter_context(nc.psum_tensor("g_psB", [128, LN_CH], F32))
    psP2 = ctx.enter_context(nc.psum_tensor("g_psP2", [128, 2 * LN_CH], F32))
    ps["psP2"] = psP2
    ps["psW"] = [psP2[:, 0:LN_CH], psP2[:, LN_CH:2 * LN_CH],
                 ctx.enter_context(nc.psum_tensor("g_psW2", [128, LN_CH], F32)),
                 ctx.enter_context(nc.psum_tensor("g_psW3", [128, LN_CH], F32))]
    return ps


def hbm_chunk_view(hbm, b, j, CH=LN_CH):
    return hbm[b].rearrange("(k p) t -> p k t", p=128)[:, :, j * CH:(j + 1) * CH]


def sb_chunk3(t):
    return t[:].rearrange("p (k t) -> p k t", k=8)


def emit_ln_stream(kb, bufs, consts_sb, nchunks, fill_z, out_view, name):
    CH = LN_CH
    zb = bufs["zb"]; zbf = bufs["zbf"]; zsq = bufs["zsq"]; rows = bufs["rows"]
    xo = bufs["xo"]; xob = bufs["xob"]
    ps = bufs["ps"]
    ps_sz = ps["ps_sz"]; ps_sq = ps["ps_sq"]; psA = ps["psA"]; psB = ps["psB"]
    s_out = kb.sem("LN_out")
    s_cg = kb.sem("LN_cg")
    s_cs = kb.sem("LN_cs")
    s_st = kb.sem("LN_st")
    s_rv = kb.sem("LN_rv")
    s_rs = kb.sem("LN_rs")
    s_bc = kb.sem("LN_bc")
    s_av = kb.sem("LN_av")
    s_ag = kb.sem("LN_ag")
    ones_bf = consts_sb["ones_col_bf"]
    ones_row = consts_sb["ones_row"]
    R = [dict() for _ in range(nchunks)]

    for j in range(nchunks):
        pj = j % 2
        r = R[j]
        wait_free = [(s_av, R[j - 2]["tt_done"])] if j >= 2 else []
        zsem, zval = fill_z(j, pj, wait_free)
        r["zready"] = zval

        wc = [(zsem, zval)]
        if j >= 1:
            wc = wc + [(s_st, R[j - 1]["stats"])]
        kb.op("gpsimd", lambda e, pj=pj: e.tensor_copy(out=zbf[:], in_=zb[pj][:]),
              wait=wc, inc=[(s_cg, 1)])
        r["cg"] = s_cg.n
        kb.op("scalar", lambda e, pj=pj: e.activation(out=zsq[:], in_=zb[pj][:], func=AF.Square),
              wait=wc, inc=[(s_cs, 1)])
        r["cs"] = s_cs.n

        wmm = [(s_cg, r["cg"]), (s_cs, r["cs"])]
        if j >= 1:
            wmm.append((s_rv, R[j - 1]["row1"]))
        for k in range(8):
            kb.op("tensor", lambda e, k=k: e.matmul(ps_sz[:], ones_bf[:],
                  zbf[:, k * CH:(k + 1) * CH], start=(k == 0), stop=(k == 7)),
                  wait=(wmm if k == 0 else []))
        for k in range(8):
            kb.op("tensor", lambda e, k=k: e.matmul(ps_sq[:], ones_bf[:],
                  zsq[:, k * CH:(k + 1) * CH], start=(k == 0), stop=(k == 7)),
                  inc=([(s_st, 1)] if k == 7 else []))
        r["stats"] = s_st.n

        mu = rows[:, 0:CH]; var = rows[:, CH:2 * CH]; a_r = rows[:, 2 * CH:3 * CH]
        b_r = rows[:, 3 * CH:4 * CH]; t0 = rows[:, 4 * CH:5 * CH]
        wrow = [(s_st, r["stats"])]
        if j >= 1:
            wrow.append((s_bc, R[j - 1]["bc"]))

        def rowmath(e):
            e.tensor_scalar_mul(out=mu, in0=ps_sz[:], scalar1=1.0 / D)
            e.tensor_scalar_mul(out=var, in0=ps_sq[:], scalar1=1.0 / D)
            e.tensor_mul(out=t0, in0=mu, in1=mu)
            return e.tensor_sub(out=var, in0=var, in1=t0)
        kb.op("vector", rowmath, wait=wrow, inc=[(s_rv, 1)])
        r["row1"] = s_rv.n

        def rowexp(e):
            e.activation(out=t0, in_=var, func=AF.Ln, bias=consts_sb["eps11"][:, 0:1])
            return e.activation(out=a_r, in_=t0, func=AF.Exp, scale=-0.5)
        kb.op("scalar", rowexp, wait=[(s_rv, r["row1"])], inc=[(s_rs, 1)])
        r["row2"] = s_rs.n

        def rowb(e):
            e.tensor_mul(out=b_r, in0=mu, in1=a_r)
            return e.tensor_scalar_mul(out=b_r, in0=b_r, scalar1=-1.0)
        kb.op("vector", rowb, wait=[(s_rs, r["row2"])], inc=[(s_rv, 1)])
        r["row3"] = s_rv.n

        wbc = [(s_rv, r["row3"])]
        if j >= 1:
            wbc.append((s_av, R[j - 1]["tt_done"]))
        kb.op("tensor", lambda e: e.matmul(psA[:], ones_row[:], a_r, start=True, stop=True),
              wait=wbc)
        kb.op("tensor", lambda e: e.matmul(psB[:], ones_row[:], b_r, start=True, stop=True),
              inc=[(s_bc, 1)])
        r["bc"] = s_bc.n

        wap = [(s_bc, r["bc"])]
        if j >= 2:
            wap.append((s_out, R[j - 2]["outdma"]))
        for k in range(8):
            sl = slice(k * CH, (k + 1) * CH)
            kb.op("vector", lambda e, pj=pj, sl=sl: e.tensor_mul(out=xo[pj][:, sl], in0=zb[pj][:, sl], in1=psA[:]),
                  wait=(wap if k == 0 else []))
            kb.op("vector", lambda e, pj=pj, sl=sl: e.tensor_add(out=xo[pj][:, sl], in0=xo[pj][:, sl], in1=psB[:]),
                  inc=([(s_av, 1)] if k == 7 else []))
        r["tt_done"] = s_av.n
        kb.op("gpsimd", lambda e, pj=pj: e.tensor_copy(out=xob[pj][:], in_=xo[pj][:]),
              wait=[(s_av, r["tt_done"])], inc=[(s_ag, 1)])
        r["ag"] = s_ag.n

        of, ob = out_view(j)
        kb.dma("sync", of, sb_chunk3(xo[pj]), sem=s_out, wait=[(s_av, r["tt_done"])])
        _, vo = kb.dma("sync", ob, sb_chunk3(xob[pj]), sem=s_out, new_group=False,
                       wait=[(s_ag, r["ag"])])
        r["outdma"] = vo
    return R


def emit_ln_phase(kb, ctx, nc, consts_sb, ps, res_hbm, addin_hbm, addin_dtype,
                  out_f32_hbm, out_bf_hbm, b_list, name, bufs=None):
    CH = LN_CH
    NCH = T // CH
    if bufs is None:
        bufs = make_ln_bufs(ctx, nc, name, with_default_fill=(addin_dtype if addin_hbm is not None else "none"))
        bufs["ps"] = ps
    rin = bufs["rin"]; ain = bufs.get("ain"); zb = bufs["zb"]
    s_in = [kb.sem("LN_in0"), kb.sem("LN_in1")]
    s_zc = kb.sem("LN_zc")
    nch_total = NCH * len(b_list)

    def fill_z(g, pj, wait_free):
        b = b_list[g // NCH]; j = g % NCH
        w = [(s_zc, fill_z.consumed[g - 2])] if g >= 2 else []
        sS = s_in[pj]
        _, v1 = kb.dma("sync", sb_chunk3(rin[pj]), hbm_chunk_view(res_hbm, b, j), sem=sS, wait=w)
        v2 = v1
        if ain is not None:
            _, v2 = kb.dma("sync", sb_chunk3(ain[pj]), hbm_chunk_view(addin_hbm, b, j), sem=sS, new_group=False)
        wz = [(sS, max(v1, v2))] + list(wait_free)
        if ain is not None:
            kb.op("vector", lambda e: e.tensor_add(out=zb[pj][:], in0=rin[pj][:], in1=ain[pj][:]),
                  wait=wz, inc=[(s_zc, 1)])
        else:
            kb.op("vector", lambda e: e.tensor_copy(out=zb[pj][:], in_=rin[pj][:]),
                  wait=wz, inc=[(s_zc, 1)])
        fill_z.consumed[g] = s_zc.n
        return (s_zc, s_zc.n)
    fill_z.consumed = [0] * nch_total

    def out_view(g):
        b = b_list[g // NCH]; j = g % NCH
        return (hbm_chunk_view(out_f32_hbm, b, j), hbm_chunk_view(out_bf_hbm, b, j))

    return emit_ln_stream(kb, bufs, consts_sb, nch_total, fill_z, out_view, name)



def emit_ssm_phase(kb, ctx, nc, consts_sb, ps, whbm, x2bf_hbm, x2f32_hbm,
                   out_f32_hbm, out_bf_hbm, name="ssm"):
    CH = LN_CH
    NCH = T // CH
    s = kb.sem(name + "_chain")
    s_xin = kb.sem(name + "_xin")
    s_rin = kb.sem(name + "_resin")
    s_gp = kb.sem("dma_gpsimd")
    psW = ps["psW"]

    Wdtp_bf = ctx.enter_context(nc.sbuf_tensor(name + "_wdtp", [128, 8 * 64], BF16))
    WB_bf = ctx.enter_context(nc.sbuf_tensor(name + "_wB", [128, 8 * 16], BF16))
    WC_bf = ctx.enter_context(nc.sbuf_tensor(name + "_wC", [128, 8 * 16], BF16))
    Wxs_bf = ctx.enter_context(nc.sbuf_tensor(name + "_wxs", [128, 8 * 16], BF16))
    WdtT_sb = ctx.enter_context(nc.sbuf_tensor(name + "_wdt", [64, 1], F32))
    sc = ctx.enter_context(nc.sbuf_tensor(name + "_sc", [1, 64], F32))
    Dp1 = ctx.enter_context(nc.sbuf_tensor(name + "_dp1", [128, 8], F32))
    WoutT_bf = ctx.enter_context(nc.sbuf_tensor(name + "_wout", [16, 1024], BF16))

    for wn, wsb, cc in (("WdtpT", Wdtp_bf, 64), ("WBT", WB_bf, 16), ("WCT", WC_bf, 16), ("WxsT", Wxs_bf, 16)):
        kb.dma("gpsimd", wsb[:].rearrange("p (k c) -> p k c", c=cc),
               whbm[wn].rearrange("(k p) c -> p k c", p=128))
    kb.dma("sync", WdtT_sb[:], whbm["WdtT"])
    kb.dma("sync", sc[0:1, 0:1], whbm["bdt"])
    kb.dma("sync", sc[0:1, 16:32], whbm["A_log"])
    kb.dma("sync", Dp1[:], whbm["Dp_col"].rearrange("k p -> p k"), allow_slow_non_contiguous=True)
    _, gv = kb.dma("gpsimd", WoutT_bf[:], whbm["WoutT"])
    ldv = kb.sem("dma_sync").n
    s_dma0 = kb.sem("dma_sync")
    kb.chain("vector", lambda e: e.tensor_scalar_mul(out=sc[0:1, 1:2], in0=sc[0:1, 0:1], scalar1=-1.0),
             s, extra_wait=[(s_dma0, ldv), (s_gp, gv)], drain=True)
    kb.chain("scalar", lambda e: e.activation(out=sc[0:1, 32:48], in_=sc[0:1, 16:32], func=AF.Exp), s, drain=True)
    kb.chain("vector", lambda e: e.tensor_scalar_add(out=Dp1[:], in0=Dp1[:], scalar1=1.0), s, drain=True)
    negbdt = sc[0:1, 1:2]
    E16 = sc[0:1, 32:48]

    x2bf = [ctx.enter_context(nc.sbuf_tensor(f"{name}_x2bf{i}", [128, 8 * CH], BF16)) for i in range(2)]
    dtp_sb = ctx.enter_context(nc.sbuf_tensor(name + "_dtp", [64, CH], F32))
    Bp = ctx.enter_context(nc.sbuf_tensor(name + "_Bp", [16, CH], F32))
    xsp = ctx.enter_context(nc.sbuf_tensor(name + "_xsp", [16, CH], F32))
    Cp = ctx.enter_context(nc.sbuf_tensor(name + "_Cp", [16, CH], F32))
    rowsS = ctx.enter_context(nc.sbuf_tensor(name + "_rows", [1, 2 * CH], F32))
    Ab = ctx.enter_context(nc.sbuf_tensor(name + "_Ab", [16, CH], F32))
    BxP = ctx.enter_context(nc.sbuf_tensor(name + "_Bx", [16, CH], F32))
    hs = ctx.enter_context(nc.sbuf_tensor(name + "_hs", [16, T], F32))
    hcbf = ctx.enter_context(nc.sbuf_tensor(name + "_hc", [16, T], BF16))

    bufs = make_ln_bufs(ctx, nc, name + "_ln", with_default_fill="none")
    bufs["ps"] = ps
    rin = bufs["rin"]; zb = bufs["zb"]
    s_ymm = kb.sem(name + "_ymm")
    s_yst = kb.sem(name + "_yst")

    RR = []
    proj_done = {}
    for b in range(NB):
        for j in range(NCH):
            pj = j % 2
            g = b * NCH + j
            wx = [(s, proj_done[g - 2])] if g >= 2 else []
            _, vx = kb.dma("sync", x2bf[pj][:].rearrange("p (k t) -> p k t", k=8),
                           hbm_chunk_view(x2bf_hbm, b, j), sem=s_xin, wait=wx)
            fw = [(s_xin, vx)]
            for (proj_sb, wbf, CC, pw) in ((dtp_sb, Wdtp_bf, 64, psW[0]), (Bp, WB_bf, 16, psW[1]),
                                           (xsp, Wxs_bf, 16, psW[2]), (Cp, WC_bf, 16, psW[3])):
                for k in range(8):
                    kb.chain("tensor", lambda e, pw=pw, k=k, wbf=wbf, CC=CC, pj=pj: e.matmul(
                        pw[0:CC, :], wbf[:, k * CC:(k + 1) * CC],
                        x2bf[pj][:, k * CH:(k + 1) * CH],
                        start=(k == 0), stop=(k == 7)), s, extra_wait=fw)
                    fw = []
                kb.chain("scalar", lambda e, pw=pw, proj_sb=proj_sb, CC=CC: e.copy(
                    out=proj_sb[:], in_=pw[0:CC, :]), s)
            proj_done[g] = s.n
            t1 = rowsS[:, 0:CH]; dtn = rowsS[:, CH:2 * CH]
            kb.chain("tensor", lambda e: e.matmul(psW[0][0:1, :], WdtT_sb[:], dtp_sb[:],
                                                  start=True, stop=True), s)
            kb.chain("scalar", lambda e: e.activation(out=t1, in_=psW[0][0:1, :], func=AF.Sigmoid,
                                                      scale=-1.0, bias=negbdt), s)
            kb.chain("scalar", lambda e: e.activation(out=dtn, in_=t1, func=AF.Ln), s)
            kb.chain("tensor", lambda e: e.matmul(psW[0][0:16, :], E16, dtn, start=True, stop=True), s)
            kb.chain("scalar", lambda e: e.activation(out=Ab[:], in_=psW[0][0:16, :], func=AF.Exp), s)
            kb.chain("vector", lambda e: e.tensor_scalar_mul(out=t1, in0=dtn, scalar1=-1.0), s)
            kb.chain("tensor", lambda e: e.matmul(psW[1][0:16, :], consts_sb["ones_row"][:, 0:16],
                                                  t1, start=True, stop=True), s)
            kb.chain("vector", lambda e: e.tensor_mul(out=BxP[:], in0=Bp[:], in1=xsp[:]), s)
            kb.chain("vector", lambda e: e.tensor_mul(out=BxP[:], in0=BxP[:], in1=psW[1][0:16, :]), s)
            if j == 0:
                kb.chain("vector", lambda e, j=j: e.tensor_tensor_scan(
                    out=hs[:, j * CH:(j + 1) * CH], data0=Ab[:], data1=BxP[:],
                    initial=0.0, op0=ALU.mult, op1=ALU.add), s)
            else:
                kb.chain("vector", lambda e, j=j: e.tensor_tensor_scan(
                    out=hs[:, j * CH:(j + 1) * CH], data0=Ab[:], data1=BxP[:],
                    initial=hs[:, j * CH - 1:j * CH], op0=ALU.mult, op1=ALU.add), s)
            kb.chain("vector", lambda e, j=j: e.tensor_mul(
                out=Ab[:], in0=hs[:, j * CH:(j + 1) * CH], in1=Cp[:]), s)
            kb.chain("scalar", lambda e, j=j: e.copy(out=hcbf[:, j * CH:(j + 1) * CH],
                                                     in_=Ab[:]), s)
        kb.barrier()

        def fill_z(j, pj, wait_free, b=b):
            w = [(s_yst, fill_z.consumed[j - 2])] if j >= 2 else []
            _, v1 = kb.dma("sync", sb_chunk3(rin[pj]), hbm_chunk_view(x2f32_hbm, b, j),
                           sem=s_rin, wait=w)
            for k in range(8):
                par = 2 + (k % 2)
                wmm = [(s_yst, fill_z.last_stt[par])] if fill_z.last_stt[par] else []
                kb.op("tensor", lambda e, k=k, j=j, par=par: e.matmul(
                    psW[par][:], WoutT_bf[:, k * 128:(k + 1) * 128],
                    hcbf[:, j * CH:(j + 1) * CH], start=True, stop=True),
                    wait=wmm, inc=[(s_ymm, 1)])
                vmm = s_ymm.n
                sl = slice(k * CH, (k + 1) * CH)
                wst = [(s_ymm, vmm), (s_rin, v1)] + (list(wait_free) if k == 0 else [])
                kb.op("vector", lambda e, pj=pj, sl=sl, k=k, par=par: e.scalar_tensor_tensor(
                    out=zb[pj][:, sl], in0=rin[pj][:, sl], scalar=Dp1[:, k:k + 1],
                    in1=psW[par][:], op0=ALU.mult, op1=ALU.add),
                    wait=wst, inc=[(s_yst, 1)])
                fill_z.last_stt[par] = s_yst.n
            fill_z.consumed[j] = s_yst.n
            return (s_yst, s_yst.n)
        fill_z.consumed = [0] * NCH
        fill_z.last_stt = {2: 0, 3: 0}

        def out_view(j, b=b):
            return (hbm_chunk_view(out_f32_hbm, b, j), hbm_chunk_view(out_bf_hbm, b, j))

        RR.append(emit_ln_stream(kb, bufs, consts_sb, NCH, fill_z, out_view, f"{name}_ln{b}"))
        kb.barrier()
    return RR


def sim_run(nc, in_map, out_names):
    from concourse.bass_interp import CoreSim
    nc.finalize()
    sim = CoreSim(nc, require_finite=False, require_nnan=False)
    for k, v in in_map.items():
        view = sim.tensor(k)
        view[:] = v
    sim.simulate()
    return {k: np.array(sim.tensor(k)) for k in out_names}



def make_matmul_fill(kb, ctx, nc, ps, name, WT_bf, act_hbm, res_hbm, bufs, b):
    CH = LN_CH
    psW = ps["psW"]
    rin = bufs["rin"]
    ain = bufs["ain"]
    zb = bufs["zb"]
    s_in = [kb.sem("FI_in0"), kb.sem("FI_in1")]
    s_mm = kb.sem("FI_mm")
    s_v = kb.sem("FI_v")

    def fill_z(j, pj, wait_free):
        w = [(s_v, fill_z.consumed[j - 2])] if j >= 2 else []
        sS = s_in[pj]
        _, v1 = kb.dma("sync", sb_chunk3(rin[pj]), hbm_chunk_view(res_hbm, b, j), sem=sS, wait=w)
        _, v2 = kb.dma("sync", sb_chunk3(ain[pj]), hbm_chunk_view(act_hbm, b, j), sem=sS, new_group=False)
        for m in range(8):
            par = 2 + (m % 2)
            wmm = [(sS, v2)] if m == 0 else []
            if fill_z.last_tt[par]:
                wmm.append((s_v, fill_z.last_tt[par]))
            for k in range(8):
                kb.op("tensor", lambda e, m=m, k=k, par=par, pj=pj: e.matmul(
                    psW[par][:], WT_bf[:, k * 1024 + m * 128:k * 1024 + (m + 1) * 128],
                    ain[pj][:, k * CH:(k + 1) * CH],
                    start=(k == 0), stop=(k == 7)),
                    wait=(wmm if k == 0 else []),
                    inc=([(s_mm, 1)] if k == 7 else []))
            vmm = s_mm.n
            sl = slice(m * CH, (m + 1) * CH)
            wst = [(s_mm, vmm)] + (list(wait_free) if m == 0 else [])
            kb.op("vector", lambda e, pj=pj, sl=sl, par=par: e.tensor_add(
                out=zb[pj][:, sl], in0=rin[pj][:, sl], in1=psW[par][:]),
                wait=wst, inc=[(s_v, 1)])
            fill_z.last_tt[par] = s_v.n
        fill_z.consumed[j] = s_v.n
        return (s_v, s_v.n)
    fill_z.consumed = [0] * (T // CH)
    fill_z.last_tt = {2: 0, 3: 0}
    return fill_z



def emit_attn_phase(kb, ctx, nc, consts_sb, ps, whbm, x_res_hbm, attnpre_hbm,
                    out_f32_hbm, out_bf_hbm, name="att"):
    CH = 512
    NCH = T // CH
    NTT = T // 128
    psW = ps["psW"]
    s_mm = kb.sem(name + "_mm")
    s_e = kb.sem(name + "_e")
    s_g = kb.sem(name + "_g")
    s_v = kb.sem(name + "_v")
    s_wq = kb.sem(name + "_wq")
    s_x = kb.sem(name + "_x")
    s_st = kb.sem(name + "_stdma")

    with ExitStack() as actx:
        x_bf = actx.enter_context(nc.sbuf_tensor(name + "_xbf", [128, 8 * T], BF16))
        V_aug = actx.enter_context(nc.sbuf_tensor(name + "_vaug", [128, NTT * NH * 65], BF16))
        WvT_bf = actx.enter_context(nc.sbuf_tensor(name + "_wv", [128, 8 * CH], BF16))
        wqk = actx.enter_context(nc.sbuf_tensor(name + "_wqk", [128, 2 * 8 * 64], BF16))
        Qa = actx.enter_context(nc.sbuf_tensor(name + "_qa", [66, T], BF16))
        Ka = actx.enter_context(nc.sbuf_tensor(name + "_ka", [66, T], BF16))
        qsq = actx.enter_context(nc.sbuf_tensor(name + "_qsq", [64, CH], BF16))
        Pt = actx.enter_context(nc.sbuf_tensor(name + "_pt", [128, NTT * CH], BF16))
        stage = [actx.enter_context(nc.sbuf_tensor(f"{name}_stage{i}", [64, CH], BF16)) for i in range(2)]
        arow = actx.enter_context(nc.sbuf_tensor(name + "_arow", [66, 16 + 2 * CH], F32))
        ones66 = actx.enter_context(nc.sbuf_tensor(name + "_o66", [66, 128], F32))
        recbc = actx.enter_context(nc.sbuf_tensor(name + "_recbc", [64, CH], F32))

        kn1 = arow[64:65, 0:8]
        mkcell = arow[64:65, 8:9]
        qnrow = arow[64:65, 16:16 + CH]
        rec = arow[64:65, 16 + CH:16 + 2 * CH]

        kb.op("vector", lambda e: e.memset(ones66[:], 1.0))
        kb.op("vector", lambda e: e.memset(Ka[64:65, :], 1.0))
        kb.op("vector", lambda e: e.memset(
            V_aug[:].rearrange("p (s c) -> p s c", c=65)[:, :, 64:65], 1.0), inc=[(s_v, 1)])

        last_STmm_h = 0
        last_PVmm = {}
        stage_dma = {0: 0, 1: 0}
        stage_tt_bank = {0: 0, 1: 0}
        rec_val = {0: 0, 1: 0}
        stage_tt = 0

        for b in range(NB):
            _, vx = kb.dma("gpsimd", x_bf[:].rearrange("p (k t) -> p k t", k=8),
                           hbm_chunk_bview(x_res_hbm, b), sem=s_x)
            vau_evict = [0] * (2 * NTT)
            for half in range(2):
                wload = [] if (b == 0 and half == 0) else [(s_mm, s_mm.n)]
                _, vwv = kb.dma("gpsimd", WvT_bf[:].rearrange("p (k c) -> p k c", c=CH),
                                whbm["WvT"].rearrange("(k p) c -> p k c", p=128)[:, :, half * CH:(half + 1) * CH],
                                sem=s_wq, wait=wload)
                for i in range(NTT):
                    u = half * NTT + i
                    par = 2 + (u % 2)
                    wv = [(s_x, vx), (s_wq, vwv)] if i == 0 else []
                    if i == 0 and half == 0:
                        wv = wv + [(s_v, s_v.n)]
                    if u >= 2:
                        wv = wv + [(s_e, vau_evict[u - 2])]
                    if b > 0 and u < 2:
                        wv = wv + [(s_v, stage_tt)]
                    for k in range(8):
                        kb.op("tensor", lambda e, i=i, k=k, par=par: e.matmul(
                            psW[par][:],
                            x_bf[:, k * T + i * 128:k * T + (i + 1) * 128],
                            WvT_bf[:, k * CH:(k + 1) * CH],
                            start=(k == 0), stop=(k == 7)),
                            wait=(wv if k == 0 else []),
                            inc=([(s_mm, 1)] if k == 7 else []))
                    vmi = s_mm.n
                    kb.op("scalar", lambda e, i=i, half=half, par=par: e.copy(
                        out=V_aug[:, i * NH * 65 + half * 8 * 65:i * NH * 65 + (half * 8 + 8) * 65].rearrange(
                            "p (h c) -> p h c", c=65)[:, :, 0:64],
                        in_=psW[par][:].rearrange("p (h c) -> p h c", c=64)),
                        wait=[(s_mm, vmi)], inc=[(s_e, 1)])
                    vau_evict[u] = s_e.n
            kb.barrier()
            for h in range(NH):
                _, vw = kb.dma("gpsimd", wqk[:, 0:512].rearrange("p (k c) -> p k c", c=64),
                               whbm["WqT"].rearrange("(k p) c -> p k c", p=128)[:, :, h * 64:(h + 1) * 64],
                               sem=s_wq)
                _, vw2 = kb.dma("gpsimd", wqk[:, 512:1024].rearrange("p (k c) -> p k c", c=64),
                                whbm["WkT"].rearrange("(k p) c -> p k c", p=128)[:, :, h * 64:(h + 1) * 64],
                                sem=s_wq, new_group=False)
                for (dst, off, scale) in ((Qa, 0, 0.125), (Ka, 512, 1.0)):
                    for j in range(NCH):
                        par = j % 2
                        wpr = []
                        if j == 0 and off == 0:
                            wpr = [(s_wq, vw2), (s_mm, last_STmm_h), (s_e, s_e.n)]
                        for k in range(8):
                            kb.op("tensor", lambda e, j=j, k=k, par=par, off=off: e.matmul(
                                psW[par][0:64, :], wqk[:, off + k * 64:off + (k + 1) * 64],
                                x_bf[:, k * T + j * CH:k * T + (j + 1) * CH],
                                start=(k == 0), stop=(k == 7)),
                                wait=(wpr if k == 0 else []),
                                inc=([(s_mm, 1)] if k == 7 else []))
                        vpj = s_mm.n
                        kb.op("scalar", lambda e, j=j, par=par, dst=dst, scale=scale: e.activation(
                            out=dst[0:64, j * CH:(j + 1) * CH], in_=psW[par][0:64, :],
                            func=AF.Identity, scale=scale),
                            wait=[(s_mm, vpj)], inc=[(s_e, 1)])
                vqk = s_e.n
                last_cons = ("v", s_v.n)
                prev_vns = 0
                for j in range(NCH):
                    wsq = [(s_e, vqk)] if j == 0 else [(s_mm, prev_vns)]
                    kb.op("scalar", lambda e, j=j: e.activation(
                        out=qsq[:], in_=Ka[0:64, j * CH:(j + 1) * CH], func=AF.Square),
                        wait=wsq, inc=[(s_e, 1)])
                    vsq = s_e.n
                    wnm = [(s_e, vsq), (s_v if last_cons[0] == "v" else s_e, last_cons[1])]
                    kb.op("tensor", lambda e: e.matmul(
                        psW[2][64:65, :], consts_sb["ones_col_bf"][0:64, :],
                        qsq[:], start=True, stop=True),
                        wait=wnm, inc=[(s_mm, 1)])
                    vns = s_mm.n
                    prev_vns = vns
                    kb.op("vector", lambda e, j=j: e.tensor_reduce(
                        out=kn1[:, j:j + 1], in_=psW[2][64:65, :], axis=AXX, op=ALU.max),
                        wait=[(s_mm, vns)], inc=[(s_v, 1)])
                    last_cons = ("v", s_v.n)
                kb.op("vector", lambda e: e.tensor_reduce(
                    out=mkcell, in_=kn1, axis=AXX, op=ALU.max), inc=[(s_v, 1)])
                vmk = s_v.n
                kb.op("scalar", lambda e: e.activation(out=mkcell, in_=mkcell, func=AF.Sqrt),
                      wait=[(s_v, vmk)], inc=[(s_e, 1)], drain=True)
                vmk2 = s_e.n
                for j in range(NCH):
                    kb.op("scalar", lambda e, j=j: e.activation(
                        out=qsq[:], in_=Qa[0:64, j * CH:(j + 1) * CH], func=AF.Square),
                        wait=([(s_v, s_v.n)] if j == 0 else []),
                        inc=[(s_e, 1)])
                    vsq = s_e.n
                    wnm = [(s_e, vsq), (s_v if last_cons[0] == "v" else s_e, last_cons[1])]
                    kb.op("tensor", lambda e: e.matmul(
                        psW[2][64:65, :], consts_sb["ones_col_bf"][0:64, :],
                        qsq[:], start=True, stop=True),
                        wait=wnm, inc=[(s_mm, 1)])
                    vns = s_mm.n
                    kb.op("scalar", lambda e: e.activation(
                        out=qnrow, in_=psW[2][64:65, :], func=AF.Sqrt),
                        wait=[(s_mm, vns)], inc=[(s_e, 1)])
                    vqn = s_e.n
                    kb.op("vector", lambda e, j=j: e.tensor_scalar(
                        out=Qa[64:65, j * CH:(j + 1) * CH], in0=qnrow, scalar1=mkcell, scalar2=-1.0,
                        op0=ALU.mult, op1=ALU.mult),
                        wait=[(s_e, vqn)], inc=[(s_v, 1)], drain=True)
                    last_cons = ("v", s_v.n)
                vaug = s_v.n

                psP2 = ps["psP2"]
                psA = ps["psA"]
                pend = None
                exp_pair_prev = 0

                def emit_norm(nblk, nbank):
                    kb.op("tensor", lambda e: e.matmul(
                        psA[0:64, :], ones66[64:65, 0:64], rec, start=True, stop=True),
                        wait=[(s_v, rec_val[nblk % 2])], inc=[(s_mm, 1)])
                    vbc = s_mm.n
                    kb.op("vector", lambda e: e.tensor_copy(out=recbc[:], in_=psA[0:64, :]),
                          wait=[(s_mm, vbc)], inc=[(s_v, 1)])
                    vrecbc = s_v.n
                    wstg = [(s_v, vrecbc)]
                    if stage_dma[nblk % 2]:
                        wstg.append((s_st, stage_dma[nblk % 2]))
                    kb.op("vector", lambda e, nbank=nbank: e.tensor_mul(
                        out=stage[nblk % 2][:], in0=nbank[0:64, :], in1=recbc[:]),
                        wait=wstg, inc=[(s_v, 1)])
                    stage_tt_bank[nblk % 2] = s_v.n
                    _, sdv = kb.dma("sync", attnpre_hbm[b][h * 64:(h + 1) * 64, nblk * CH:(nblk + 1) * CH],
                                    stage[nblk % 2][:], sem=s_st, wait=[(s_v, stage_tt_bank[nblk % 2])])
                    stage_dma[nblk % 2] = sdv

                for blk in range(8):
                    ntk = 4 * (blk + 1)
                    pvw = []
                    exp_done = [0] * NTT
                    for p in range(ntk // 2):
                        tA, tB = 2 * p, 2 * p + 1
                        wst = [(s_v, vaug)] if (blk == 0 and p == 0) else []
                        if p >= 1:
                            wst.append((s_e, exp_done[2 * p - 2]))
                        elif blk > 0:
                            wst.append((s_e, exp_pair_prev))
                        kb.op("tensor", lambda e, tA=tA, blk=blk: e.matmul(
                            psP2[:, 0:CH], Ka[0:65, tA * 128:(tA + 1) * 128],
                            Qa[0:65, blk * CH:(blk + 1) * CH], start=True, stop=True),
                            wait=wst)
                        kb.op("tensor", lambda e, tB=tB, blk=blk: e.matmul(
                            psP2[:, CH:2 * CH], Ka[0:65, tB * 128:(tB + 1) * 128],
                            Qa[0:65, blk * CH:(blk + 1) * CH], start=True, stop=True),
                            inc=[(s_mm, 1)])
                        vstm = s_mm.n
                        wexp = [(s_mm, vstm)]
                        for tx in (tA, tB):
                            if last_PVmm.get(tx):
                                wexp.append((s_mm, last_PVmm[tx]))
                        kb.op("scalar", lambda e, tA=tA: e.activation(
                            out=Pt[:, tA * CH:(tA + 2) * CH], in_=psP2[:, 0:2 * CH], func=AF.Exp),
                            wait=wexp, inc=[(s_e, 1)])
                        exp_done[tA] = exp_done[tB] = s_e.n
                        for tx in (tA, tB):
                            if tx * 128 >= blk * CH:
                                def mask(e, tx=tx, q0=blk * CH):
                                    i0 = tx * CH
                                    lo = tx * 128 - q0
                                    if lo > 0:
                                        e.memset(Pt[:, i0:i0 + lo], 0.0)
                                    return e.affine_select(
                                        out=Pt[:, i0 + lo:i0 + lo + 128], in_=Pt[:, i0 + lo:i0 + lo + 128],
                                        pattern=[[1, 128]], compare_op=ALU.is_ge, fill=0.0,
                                        base=0, channel_multiplier=-1)
                                kb.op("gpsimd", mask, wait=[(s_e, exp_done[tx])], inc=[(s_g, 1)])
                                pvw.append((tx, s_g, s_g.n))
                            else:
                                pvw.append((tx, s_e, exp_done[tx]))
                    exp_pair_prev = exp_done[ntk - 1]
                    if blk == 7:
                        last_STmm_h = s_mm.n
                    if pend is not None:
                        emit_norm(pend[0], pend[1])
                    pvbank = psW[2] if blk % 2 == 0 else psW[3]
                    for idx, (tkt, sem_, val) in enumerate(pvw):
                        wpv = [(sem_, val)]
                        if idx == 0 and stage_tt_bank[blk % 2]:
                            wpv.append((s_v, stage_tt_bank[blk % 2]))
                        kb.op("tensor", lambda e, tkt=tkt, ntk=ntk, h=h, pvbank=pvbank: e.matmul(
                            pvbank[0:65, :],
                            V_aug[:, tkt * NH * 65 + h * 65:tkt * NH * 65 + (h + 1) * 65],
                            Pt[:, tkt * CH:(tkt + 1) * CH],
                            start=(tkt == 0), stop=(tkt == ntk - 1)),
                            wait=wpv, inc=[(s_mm, 1)])
                        last_PVmm[tkt] = s_mm.n
                    vpv = s_mm.n
                    kb.op("vector", lambda e, pvbank=pvbank: e.reciprocal(
                        out=rec, in_=pvbank[64:65, :]),
                        wait=[(s_mm, vpv)], inc=[(s_v, 1)], drain=True)
                    rec_val[blk % 2] = s_v.n
                    pend = (blk, pvbank)
                emit_norm(pend[0], pend[1])
                stage_tt = stage_tt_bank[pend[0] % 2]
            kb.barrier()
        kb.barrier()

    WoT_bf = ctx.enter_context(nc.sbuf_tensor(name + "_wo", [128, 8 * 1024], BF16))
    _, vwo = kb.dma("gpsimd", WoT_bf[:].rearrange("p (k c) -> p k c", c=1024),
                    whbm["WoT"].rearrange("(k p) c -> p k c", p=128))
    bufs = make_ln_bufs(ctx, nc, name + "_ln", with_default_fill=BF16)
    bufs["ps"] = ps
    kb.wait("tensor", kb.sem("dma_gpsimd"), vwo)
    kb.barrier()
    RR = []
    for b in range(NB):
        fill = make_matmul_fill(kb, ctx, nc, ps, f"{name}_f{b}", WoT_bf, attnpre_hbm,
                                x_res_hbm, bufs, b)

        def out_view(j, b=b):
            return (hbm_chunk_view(out_f32_hbm, b, j), hbm_chunk_view(out_bf_hbm, b, j))

        RR.append(emit_ln_stream(kb, bufs, consts_sb, NCH, fill, out_view, f"{name}_ln{b}"))
        kb.barrier()
    return RR


def hbm_chunk_bview(hbm, b):
    return hbm[b].rearrange("(k p) t -> p k t", p=128)



def emit_ffn0_phase(kb, ctx, nc, consts_sb, ps, whbm, x1bf_hbm, x1f_hbm,
                    hsum_hbm, ffnout_hbm, out_f32_hbm, out_bf_hbm, name="ffn0", cut=None):
    CH = 512
    NCH = T // CH
    NT = T // 128
    psW = ps["psW"]
    s_mm = kb.sem(name + "_mm")
    s_e = kb.sem(name + "_e")
    s_v = kb.sem(name + "_v")
    s_c = kb.sem(name + "_c")
    s_wi = kb.sem(name + "_wi")
    s_hs = kb.sem(name + "_hs")
    s_ho = kb.sem(name + "_ho")
    s_fo = kb.sem(name + "_fo")
    s_xb = kb.sem(name + "_xb")

    val_rows = ctx.enter_context(nc.sbuf_tensor(name + "_valr", [1, 2 * T], F32))

    def _subA(b, nb_):
        with ExitStack() as actx:
            x1_bf = actx.enter_context(nc.sbuf_tensor(nb_ + "_x1bf", [128, 8 * T], BF16))
            gwt = actx.enter_context(nc.sbuf_tensor(nb_ + "_gwt", [128, 8 * 8], BF16))
            gbt = actx.enter_context(nc.sbuf_tensor(nb_ + "_gbt", [128, 8], F32))
            tie = actx.enter_context(nc.sbuf_tensor(nb_ + "_tie", [128, 8], F32))
            laT = actx.enter_context(nc.sbuf_tensor(nb_ + "_laT", [128, 8 * 128], BF16))
            lbT = actx.enter_context(nc.sbuf_tensor(nb_ + "_lbT", [128, DFF], BF16))
            lbneg = actx.enter_context(nc.sbuf_tensor(nb_ + "_lbn", [128, 128], BF16))
            Rl = actx.enter_context(nc.sbuf_tensor(nb_ + "_Rl", [8, 128], BF16))
            bi_sb = actx.enter_context(nc.sbuf_tensor(nb_ + "_bi", [128, 32], F32))
            garr = actx.enter_context(nc.sbuf_tensor(nb_ + "_gr", [128, 1544], F32))
            indc = actx.enter_context(nc.sbuf_tensor(nb_ + "_indc", [128, 2 * NT * 8], BF16))
            valc = actx.enter_context(nc.sbuf_tensor(nb_ + "_valc", [128, NT], F32))
            ind1T = actx.enter_context(nc.sbuf_tensor(nb_ + "_i1T", [8, T], BF16))
            ind2T = actx.enter_context(nc.sbuf_tensor(nb_ + "_i2T", [8, T], BF16))
            vct = actx.enter_context(nc.sbuf_tensor(nb_ + "_vct", [32, 128], F32))
            z1 = actx.enter_context(nc.sbuf_tensor(nb_ + "_z1", [128, T], BF16))
            z2 = actx.enter_context(nc.sbuf_tensor(nb_ + "_z2", [128, T], BF16))
            iexp = actx.enter_context(nc.sbuf_tensor(nb_ + "_iexp", [128, CH], BF16))
            valbc = actx.enter_context(nc.sbuf_tensor(nb_ + "_vbc", [128, 8 * CH], F32))
            wic = [actx.enter_context(nc.sbuf_tensor(f"{nb_}_wic{i}", [128, 8 * 128], BF16)) for i in range(2)]
            hst = [actx.enter_context(nc.sbuf_tensor(f"{nb_}_hst{i}", [128, 2 * CH], BF16)) for i in range(4)]
            h2p = [actx.enter_context(nc.sbuf_tensor(f"{nb_}_h2p{i}", [128, CH], F32)) for i in range(2)]
            ident = actx.enter_context(nc.sbuf_tensor(nb_ + "_id", [128, 128], BF16))
            identf = actx.enter_context(nc.sbuf_tensor(nb_ + "_idf", [128, 128], F32))

            kb.dma("sync", gbt[:], whbm["gate_b_t"])
            kb.dma("sync", tie[:], consts_sb["tie_hbm"], new_group=False)
            kb.dma("sync", Rl[:], consts_sb["Rl_hbm"], new_group=False)
            kb.dma("sync", bi_sb[:], whbm["bi"].rearrange("m p -> p m"),
                   new_group=False, allow_slow_non_contiguous=True)
            kb.dma("sync", ident[:], consts_sb["ident_hbm"], new_group=False)
            _, vld = kb.dma("sync", identf[:], consts_sb["identf_hbm"], new_group=False)
            kb.dma("gpsimd", laT[:].rearrange("p (k c) -> p k c", c=128),
                   whbm["loraAT"].rearrange("(k p) c -> p k c", p=128))
            kb.dma("gpsimd", lbT[:], whbm["loraBT"], new_group=False)
            _, vld2 = kb.dma("gpsimd", gwt[:].rearrange("p (k c) -> p k c", c=8),
                             whbm["gate_wT"].rearrange("(k p) c -> p k c", p=128), new_group=False)
            _, vxb = kb.dma("gpsimd", x1_bf[:].rearrange("p (k t) -> p k t", k=8),
                            hbm_chunk_bview(x1bf_hbm, b), sem=s_xb)
            kb.wait("tensor", kb.sem("dma_sync"), vld)
            kb.wait("tensor", kb.sem("dma_gpsimd"), vld2)
            kb.wait("tensor", s_xb, vxb)
            kb.barrier()

            kb.chain("vector", lambda e: e.memset(garr[:], 0.0), s_c)
            lgA = garr[:, 0:256]; lselA = garr[:, 256:512]; maskedA = garr[:, 512:768]
            emA = garr[:, 768:1024]
            m1nA = garr[:, 1024:1152]
            mm1A = garr[:, 1152:1280]
            m2A = garr[:, 1280:1408]
            rrA = garr[:, 1408:1536]
            sc8 = garr[:, 1536:1544]
            for i in range(NT):
                for k in range(8):
                    kb.chain("tensor", lambda e, i=i, k=k: e.matmul(
                        psW[0][:, 0:8], x1_bf[:, k * T + i * 128:k * T + (i + 1) * 128],
                        gwt[:, k * 8:(k + 1) * 8], start=(k == 0), stop=(k == 7)), s_c)
                def p1(e, i=i):
                    e.tensor_add(out=lgA[:, i * 8:(i + 1) * 8], in0=psW[0][:, 0:8], in1=gbt[:])
                    e.drain()
                    e.tensor_add(out=lselA[:, i * 8:(i + 1) * 8], in0=lgA[:, i * 8:(i + 1) * 8], in1=tie[:])
                    e.drain()
                    e.tensor_reduce(out=m1nA[:, 4 * i:4 * i + 1], in_=lselA[:, i * 8:(i + 1) * 8],
                                    axis=AXX, op=ALU.max, negate=True)
                    e.drain()
                    return e.tensor_scalar_mul(out=mm1A[:, 4 * i:4 * i + 1],
                                               in0=m1nA[:, 4 * i:4 * i + 1], scalar1=-1.0)
                kb.chain("vector", p1, s_c, drain=True)
            for i in range(NT):
                def p2(e, i=i):
                    e.tensor_scalar(out=indc[:, i * 8:(i + 1) * 8], in0=lselA[:, i * 8:(i + 1) * 8],
                                    scalar1=mm1A[:, 4 * i:4 * i + 1], scalar2=None, op0=ALU.is_ge)
                    e.tensor_scalar(out=sc8, in0=lselA[:, i * 8:(i + 1) * 8],
                                    scalar1=mm1A[:, 4 * i:4 * i + 1], scalar2=None, op0=ALU.is_lt)
                    e.drain()
                    e.tensor_scalar(out=sc8, in0=sc8, scalar1=1e30, scalar2=-1e30,
                                    op0=ALU.mult, op1=ALU.add)
                    e.drain()
                    e.tensor_add(out=maskedA[:, i * 8:(i + 1) * 8], in0=lselA[:, i * 8:(i + 1) * 8], in1=sc8)
                    e.drain()
                    return e.tensor_reduce(out=m2A[:, 4 * i:4 * i + 1],
                                           in_=maskedA[:, i * 8:(i + 1) * 8], axis=AXX, op=ALU.max)
                kb.chain("vector", p2, s_c, drain=True)
            for i in range(NT):
                kb.chain("vector", lambda e, i=i: e.tensor_scalar(
                    out=indc[:, NT * 8 + i * 8:NT * 8 + (i + 1) * 8],
                    in0=maskedA[:, i * 8:(i + 1) * 8],
                    scalar1=m2A[:, 4 * i:4 * i + 1], scalar2=None, op0=ALU.is_ge), s_c, drain=True)
                kb.chain("scalar", lambda e, i=i: e.activation(
                    out=emA[:, i * 8:(i + 1) * 8], in_=lgA[:, i * 8:(i + 1) * 8],
                    func=AF.Exp, bias=m1nA[:, 4 * i:4 * i + 1]), s_c, drain=True)
            for i in range(NT):
                def p4(e, i=i):
                    e.tensor_reduce(out=rrA[:, 4 * i:4 * i + 1], in_=emA[:, i * 8:(i + 1) * 8],
                                    axis=AXX, op=ALU.add)
                    e.drain()
                    e.reciprocal(out=rrA[:, 4 * i + 1:4 * i + 2], in_=rrA[:, 4 * i:4 * i + 1])
                    e.tensor_add(out=sc8, in0=indc[:, i * 8:(i + 1) * 8],
                                 in1=indc[:, NT * 8 + i * 8:NT * 8 + (i + 1) * 8])
                    e.drain()
                    e.tensor_mul(out=sc8, in0=emA[:, i * 8:(i + 1) * 8], in1=sc8)
                    e.drain()
                    e.tensor_reduce(out=valc[:, i:i + 1], in_=sc8, axis=AXX, op=ALU.add)
                    e.drain()
                    return e.tensor_mul(out=valc[:, i:i + 1], in0=valc[:, i:i + 1],
                                        in1=rrA[:, 4 * i + 1:4 * i + 2])
                kb.chain("vector", p4, s_c, drain=True)
            for i in range(NT):
                kb.chain("tensor", lambda e, i=i: e.transpose(
                    psW[0][0:8, 0:128].bitcast(BF16)[:, 0:128],
                    indc[:, i * 8:(i + 1) * 8], ident[:]), s_c)
                kb.chain("vector", lambda e, i=i: e.tensor_copy(
                    out=ind1T[:, i * 128:(i + 1) * 128], in_=psW[0][0:8, 0:128].bitcast(BF16)[:, 0:128]), s_c)
                kb.chain("tensor", lambda e, i=i: e.transpose(
                    psW[0][0:8, 0:128].bitcast(BF16)[:, 0:128],
                    indc[:, NT * 8 + i * 8:NT * 8 + (i + 1) * 8], ident[:]), s_c)
                kb.chain("vector", lambda e, i=i: e.tensor_copy(
                    out=ind2T[:, i * 128:(i + 1) * 128], in_=psW[0][0:8, 0:128].bitcast(BF16)[:, 0:128]), s_c)
            kb.chain("tensor", lambda e: e.transpose(psW[0][0:32, 0:128], valc[:], identf[:]), s_c)
            kb.chain("vector", lambda e: e.tensor_copy(out=vct[:], in_=psW[0][0:32, 0:128]), s_c)
            vchain = s_c.n
            s_vr = kb.sem(name + "_vr")
            _, vvr = kb.dma("sync", val_rows[0:1, b * T:(b + 1) * T], vct[:], sem=s_vr,
                            wait=[(s_c, vchain)])
            for j in range(NCH):
                kb.chain("tensor", lambda e, j=j, b=b: e.matmul(
                    psW[0][:, 0:CH], consts_sb["ones_row"][:],
                    val_rows[0:1, b * T + j * CH:b * T + (j + 1) * CH], start=True, stop=True), s_c,
                    extra_wait=([(s_vr, vvr)] if j == 0 else []))
                kb.chain("vector", lambda e, j=j: e.tensor_copy(out=valbc[:, j * CH:(j + 1) * CH],
                                                                in_=psW[0][:, 0:CH]), s_c)
            if cut == "gate":
                if "dbg_garr" in whbm:
                    kb.dma("sync", whbm["dbg_garr"], garr[:, :], wait=[(s_c, s_c.n)])
                    kb.dma("gpsimd", whbm["dbg_indc"], indc[:, :], wait=[(s_c, s_c.n)])
                kb.barrier()
                return
            for j in range(NCH):
                for k in range(8):
                    kb.chain("tensor", lambda e, j=j, k=k: e.matmul(
                        psW[0][:, 0:CH], laT[:, k * 128:(k + 1) * 128],
                        x1_bf[:, k * T + j * CH:k * T + (j + 1) * CH],
                        start=(k == 0), stop=(k == 7)), s_c)
                kb.chain("scalar", lambda e, j=j: e.copy(out=z1[:, j * CH:(j + 1) * CH],
                                                         in_=psW[0][:, 0:CH]), s_c)
                kb.chain("tensor", lambda e, j=j: e.matmul(
                    psW[0][:, 0:CH], Rl[:], ind2T[:, j * CH:(j + 1) * CH], start=True, stop=True), s_c)
                kb.chain("vector", lambda e, j=j: e.tensor_copy(out=iexp[:], in_=psW[0][:, 0:CH]), s_c)
                kb.chain("vector", lambda e, j=j: e.tensor_mul(
                    out=z2[:, j * CH:(j + 1) * CH], in0=z1[:, j * CH:(j + 1) * CH], in1=iexp[:]), s_c)
                kb.chain("tensor", lambda e, j=j: e.matmul(
                    psW[0][:, 0:CH], Rl[:], ind1T[:, j * CH:(j + 1) * CH], start=True, stop=True), s_c)
                kb.chain("vector", lambda e, j=j: e.tensor_copy(out=iexp[:], in_=psW[0][:, 0:CH]), s_c)
                kb.chain("vector", lambda e, j=j: e.tensor_mul(
                    out=z1[:, j * CH:(j + 1) * CH], in0=z1[:, j * CH:(j + 1) * CH], in1=iexp[:]), s_c)
            kb.barrier()
            if cut == "z":
                return

            relu_done = {2: 0, 3: 0}
            uB_done = {0: 0, 1: 0}
            h2p_free = [0, 0]
            hst_free = [0, 0, 0, 0]
            wiv = [0, 0]
            for m in range(32):
                pw = m % 2
                wwi = [(s_mm, wiv[pw])] if wiv[pw] else []
                _, vwi = kb.dma("gpsimd", wic[pw][:].rearrange("p (k c) -> p k c", c=128),
                                whbm["wiT"].rearrange("(k p) c -> p k c", p=128)[:, :, m * 128:(m + 1) * 128],
                                sem=s_wi, wait=wwi)
                kb.op("vector", lambda e, pw=pw, m=m: e.tensor_scalar_mul(
                    out=lbneg[:], in0=lbT[:, m * 128:(m + 1) * 128], scalar1=-1.0),
                    wait=[(s_v, s_v.n), (s_mm, s_mm.n)], inc=[(s_v, 1)], drain=True)
                vneg = s_v.n
                for j in range(NCH):
                    par = 2 + (j % 2)
                    parB = j % 2
                    hb = (2 * m + j) % 4
                    wsh = [(s_wi, vwi), (s_v, vneg)] if j == 0 else []
                    if relu_done[par]:
                        wsh.append((s_e, relu_done[par]))
                    for k in range(8):
                        kb.op("tensor", lambda e, j=j, k=k, par=par, pw=pw: e.matmul(
                            psW[par][:], wic[pw][:, k * 128:(k + 1) * 128],
                            x1_bf[:, k * T + j * CH:k * T + (j + 1) * CH],
                            start=(k == 0), stop=False),
                            wait=(wsh if k == 0 else []))
                    kb.op("tensor", lambda e, j=j, par=par, m=m: e.matmul(
                        psW[par][:], lbT[:, m * 128:(m + 1) * 128],
                        z1[:, j * CH:(j + 1) * CH], start=False, stop=True),
                        inc=[(s_mm, 1)])
                    v1 = s_mm.n
                    wB = [(s_v, uB_done[parB])] if uB_done[parB] else []
                    kb.op("tensor", lambda e, j=j, parB=parB, m=m: e.matmul(
                        psW[parB][:], lbT[:, m * 128:(m + 1) * 128],
                        z2[:, j * CH:(j + 1) * CH], start=True, stop=False),
                        wait=wB)
                    kb.op("tensor", lambda e, j=j, parB=parB, m=m: e.matmul(
                        psW[parB][:], lbneg[:], z1[:, j * CH:(j + 1) * CH],
                        start=False, stop=True), inc=[(s_mm, 1)])
                    v2 = s_mm.n
                    whst = [(s_mm, v1)]
                    if hst_free[hb]:
                        whst.append((s_hs, hst_free[hb]))
                    kb.op("scalar", lambda e, par=par, m=m, hb=hb: e.activation(
                        out=hst[hb][:, 0:CH], in_=psW[par][:], func=AF.Relu,
                        bias=bi_sb[:, m:m + 1]),
                        wait=whst, inc=[(s_e, 1)])
                    ve1 = s_e.n
                    kb.op("scalar", lambda e, parB=parB, hb=hb: e.copy(
                        out=hst[hb][:, CH:2 * CH], in_=psW[parB][:]),
                        wait=[(s_mm, v2)], inc=[(s_e, 1)])
                    veB = s_e.n
                    kb.op("vector", lambda e, par=par, hb=hb: e.tensor_add(
                        out=h2p[hb % 2][:], in0=psW[par][:], in1=hst[hb][:, CH:2 * CH]),
                        wait=[(s_e, veB), (s_v, h2p_free[hb % 2])], inc=[(s_v, 1)])
                    vh2p = s_v.n
                    uB_done[parB] = vh2p
                    kb.op("scalar", lambda e, m=m, hb=hb: e.activation(
                        out=hst[hb][:, CH:2 * CH], in_=h2p[hb % 2][:], func=AF.Relu,
                        bias=bi_sb[:, m:m + 1]),
                        wait=[(s_v, vh2p)], inc=[(s_e, 1)])
                    ve2 = s_e.n
                    relu_done[par] = ve2
                    def hsum_tt(e, hb=hb, j=j):
                        e.tensor_add(out=hst[hb][:, 0:CH], in0=hst[hb][:, 0:CH], in1=hst[hb][:, CH:2 * CH])
                        return e.tensor_mul(out=hst[hb][:, 0:CH], in0=hst[hb][:, 0:CH],
                                            in1=valbc[:, j * CH:(j + 1) * CH])
                    kb.op("vector", hsum_tt, wait=[(s_e, ve2)], inc=[(s_v, 1)])
                    vtt = s_v.n
                    h2p_free[hb % 2] = vtt
                    _, vhs = kb.dma("sync", hsum_hbm[b][m * 128:(m + 1) * 128, j * CH:(j + 1) * CH],
                                    hst[hb][:, 0:CH], sem=s_hs, wait=[(s_v, vtt)])
                    hst_free[hb] = vhs
                wiv[pw] = s_mm.n
            kb.barrier()


    def _subB(b, nb_):
        with ExitStack() as bctx:
            woT_bf = bctx.enter_context(nc.sbuf_tensor(nb_ + "_wo", [128, 32 * 1024], BF16))
            hin = [bctx.enter_context(nc.sbuf_tensor(f"{nb_}_hin{i}", [128, 32 * CH], BF16)) for i in range(2)]
            bo_sb = bctx.enter_context(nc.sbuf_tensor(nb_ + "_bo", [1, 1024], F32))
            fst = [bctx.enter_context(nc.sbuf_tensor(f"{nb_}_fst{i}", [128, CH], BF16)) for i in range(4)]
            _, vwo = kb.dma("gpsimd", woT_bf[:].rearrange("p (k c) -> p k c", c=1024),
                            whbm["woT"].rearrange("(k p) c -> p k c", p=128))
            _, vbo = kb.dma("sync", bo_sb[:], whbm["bo_row"])
            kb.op("vector", lambda e, b=b: e.tensor_scalar_mul(
                out=val_rows[0:1, b * T:(b + 1) * T], in0=val_rows[0:1, b * T:(b + 1) * T],
                scalar1=2.0), wait=[(kb.sem("dma_sync"), kb.sem("dma_sync").n)], inc=[(s_v, 1)])
            vv2 = s_v.n
            kb.wait("tensor", kb.sem("dma_gpsimd"), vwo)
            kb.wait("tensor", kb.sem("dma_sync"), vbo)
            kb.wait("tensor", s_v, vv2)
            kb.barrier()
            evict_done = {2: 0, 3: 0}
            fst_free = [0, 0, 0, 0]
            hin_free = [0, 0]
            for j in range(NCH):
                pj = j % 2
                whi = [(s_mm, hin_free[pj])] if hin_free[pj] else []
                _, vhi = kb.dma("sync", hin[pj][:].rearrange("p (k t) -> p k t", k=32),
                                hsum_hbm[b].rearrange("(k p) t -> p k t", p=128)[:, :, j * CH:(j + 1) * CH],
                                sem=s_ho, wait=whi)
                for mo in range(8):
                    par = 2 + (mo % 2)
                    fb = (8 * j + mo) % 4
                    wmm = [(s_ho, vhi)] if mo == 0 else []
                    if evict_done[par]:
                        wmm.append((s_e, evict_done[par]))
                    for k in range(32):
                        kb.op("tensor", lambda e, mo=mo, k=k, par=par, pj=pj: e.matmul(
                            psW[par][:], woT_bf[:, k * 1024 + mo * 128:k * 1024 + (mo + 1) * 128],
                            hin[pj][:, k * CH:(k + 1) * CH],
                            start=(k == 0), stop=False),
                            wait=(wmm if k == 0 else []))
                    kb.op("tensor", lambda e, mo=mo, j=j, b=b, par=par: e.matmul(
                        psW[par][:], bo_sb[0:1, mo * 128:(mo + 1) * 128],
                        val_rows[0:1, b * T + j * CH:b * T + (j + 1) * CH],
                        start=False, stop=True), inc=[(s_mm, 1)])
                    vmm = s_mm.n
                    wev = [(s_mm, vmm)]
                    if fst_free[fb]:
                        wev.append((s_fo, fst_free[fb]))
                    kb.op("scalar", lambda e, par=par, fb=fb: e.copy(
                        out=fst[fb][:], in_=psW[par][:]),
                        wait=wev, inc=[(s_e, 1)])
                    evict_done[par] = s_e.n
                    _, vfo = kb.dma("sync", ffnout_hbm[b][mo * 128:(mo + 1) * 128, j * CH:(j + 1) * CH],
                                    fst[fb][:], sem=s_fo, wait=[(s_e, evict_done[par])])
                    fst_free[fb] = vfo
                hin_free[pj] = s_mm.n
            kb.barrier()


    for b in range(NB):
        _subA(b, f"{name}{b}")
        if cut is None:
            _subB(b, f"{name}{b}")

    if "dbg_val" in whbm:
        kb.dma("sync", whbm["dbg_val"], val_rows[0:1, :])
        kb.barrier()

    if cut is not None:
        return None
    return emit_ln_phase(kb, ctx, nc, consts_sb, ps, x1f_hbm, ffnout_hbm, BF16,
                         out_f32_hbm, out_bf_hbm, list(range(NB)), name + "_ln")



def emit_ffn1_phase(kb, ctx, nc, consts_sb, ps, whbm, x3bf_hbm, x3f_hbm,
                    qk_hbm, g_hbm, out_f32_hbm, out_bf_hbm, name="ffn1"):
    CH = 512
    NCH = T // CH
    NT = T // 128
    psW = ps["psW"]
    s_mm = kb.sem(name + "_mm")
    s_e = kb.sem(name + "_e")
    s_v = kb.sem(name + "_v")
    s_c = kb.sem(name + "_c")
    s_ld = kb.sem(name + "_ld")
    s_qs = kb.sem(name + "_qs")
    s_ks = kb.sem(name + "_ks")
    s_go = kb.sem(name + "_go")
    s_gi = kb.sem(name + "_gi")

    with ExitStack() as s1:
        x3bf = s1.enter_context(nc.sbuf_tensor(name + "_x3", [128, 8 * T], BF16))
        WQb = s1.enter_context(nc.sbuf_tensor(name + "_wq", [128, 8 * 1024], BF16))
        WKb = s1.enter_context(nc.sbuf_tensor(name + "_wk", [128, 8 * 1024], BF16))
        WIb = s1.enter_context(nc.sbuf_tensor(name + "_wi", [128, 8 * 1024], BF16))
        qc = [s1.enter_context(nc.sbuf_tensor(f"{name}_qc{i}", [128, 8 * CH], BF16)) for i in range(2)]
        for wn, wsb in (("WQT", WQb), ("WKT", WKb), ("WinvT", WIb)):
            kb.dma("gpsimd", wsb[:].rearrange("p (k c) -> p k c", c=1024),
                   whbm[wn].rearrange("(k p) c -> p k c", p=128))
        vw = kb.sem("dma_gpsimd").n
        kb.wait("tensor", kb.sem("dma_gpsimd"), vw)
        kb.barrier()

        def s1_emit(b):
            _, vx = kb.dma("gpsimd", x3bf[:].rearrange("p (k t) -> p k t", k=8),
                           hbm_chunk_bview(x3bf_hbm, b), sem=s_ld)
            evd = {2: 0, 3: 0}
            qcfree = [0, 0]
            for src_i, (wsb, dst_i) in enumerate(((WQb, 0), (WKb, 1))):
                for j in range(NCH):
                    pj = j % 2
                    wq0 = [(s_ld, vx)] if (j == 0 and src_i == 0) else []
                    if qcfree[pj]:
                        wq0 = wq0 + [(s_mm, qcfree[pj])]
                    for m in range(8):
                        par = 2 + (m % 2)
                        wmm = list(wq0) if m == 0 else []
                        if evd[par]:
                            wmm.append((s_e, evd[par]))
                        for k in range(8):
                            kb.op("tensor", lambda e, m=m, k=k, j=j, par=par, wsb=wsb: e.matmul(
                                psW[par][:], wsb[:, k * 1024 + m * 128:k * 1024 + (m + 1) * 128],
                                x3bf[:, k * T + j * CH:k * T + (j + 1) * CH],
                                start=(k == 0), stop=(k == 7)),
                                wait=(wmm if k == 0 else []),
                                inc=([(s_mm, 1)] if k == 7 else []))
                        vm = s_mm.n
                        kb.op("scalar", lambda e, m=m, par=par, pj=pj: e.copy(
                            out=qc[pj][:, m * CH:(m + 1) * CH], in_=psW[par][:]),
                            wait=([(s_mm, vm), (s_qs, s_qs.n)] if m == 0 else [(s_mm, vm)]),
                            inc=[(s_e, 1)])
                        evd[par] = s_e.n
                    vch = s_e.n
                    _, vq = kb.dma("sync", hbm_chunk_view(qk_hbm[dst_i], b, j),
                                   sb_chunk3(qc[pj]), sem=s_qs, wait=[(s_e, vch)])
                    for m in range(8):
                        par = 2 + (m % 2)
                        wmm = [(s_e, evd[par])]
                        for k in range(8):
                            kb.op("tensor", lambda e, m=m, k=k, par=par, pj=pj: e.matmul(
                                psW[par][:], WIb[:, k * 1024 + m * 128:k * 1024 + (m + 1) * 128],
                                qc[pj][:, k * CH:(k + 1) * CH],
                                start=(k == 0), stop=(k == 7)),
                                wait=(wmm if k == 0 else []),
                                inc=([(s_mm, 1)] if k == 7 else []))
                        vm = s_mm.n
                        kb.op("scalar", lambda e, m=m, par=par, pj=pj: e.copy(
                            out=qc[pj][:, m * CH:(m + 1) * CH], in_=psW[par][:]),
                            wait=[(s_mm, vm), (s_qs, vq)] if m == 0 else [(s_mm, vm)],
                            inc=[(s_e, 1)])
                        evd[par] = s_e.n
                    vch = s_e.n
                    _, vq2 = kb.dma("sync", hbm_chunk_view(qk_hbm[2 + dst_i], b, j),
                                    sb_chunk3(qc[pj]), sem=s_qs, wait=[(s_e, vch)])
                    qcfree[pj] = s_mm.n
            kb.barrier()
        for b in range(NB):
            s1_emit(b)
    kb.barrier()

    with ExitStack() as s2:
        Vb = s2.enter_context(nc.sbuf_tensor(name + "_Vb", [128, 8 * 512], BF16))
        Wb = s2.enter_context(nc.sbuf_tensor(name + "_Wb", [128, 8 * 512], BF16))
        rwU = s2.enter_context(nc.sbuf_tensor(name + "_rwU", [128, 8 * 1024], BF16))
        rw1 = rwU
        Ub = rwU
        rw2 = s2.enter_context(nc.sbuf_tensor(name + "_rw2", [128, 8 * 8], BF16))
        rb1 = s2.enter_context(nc.sbuf_tensor(name + "_rb1", [128, 8], F32))
        comb = s2.enter_context(nc.sbuf_tensor(name + "_cmb", [128, 8], F32))
        bexp = s2.enter_context(nc.sbuf_tensor(name + "_bex", [128, 8], F32))
        bk = s2.enter_context(nc.sbuf_tensor(name + "_bk", [8, 1024], BF16))
        Rk = s2.enter_context(nc.sbuf_tensor(name + "_Rk", [8, 512], BF16))
        alr = s2.enter_context(nc.sbuf_tensor(name + "_al", [1, 4], F32))
        acol = s2.enter_context(nc.sbuf_tensor(name + "_ac", [8, 4], F32))
        identf = s2.enter_context(nc.sbuf_tensor(name + "_idf", [128, 128], F32))
        xV = s2.enter_context(nc.sbuf_tensor(name + "_xV", [128, 4 * T], BF16))
        yW = s2.enter_context(nc.sbuf_tensor(name + "_yW", [128, 4 * T], BF16))
        prodw = s2.enter_context(nc.sbuf_tensor(name + "_pw", [128, 4 * T], BF16))
        hc = s2.enter_context(nc.sbuf_tensor(name + "_hc", [128, 8 * CH], BF16))
        qsb = [s2.enter_context(nc.sbuf_tensor(f"{name}_qsb{i}", [128, 8 * CH], BF16)) for i in range(2)]
        warr = s2.enter_context(nc.sbuf_tensor(name + "_wa", [128, 512], F32))
        wT = s2.enter_context(nc.sbuf_tensor(name + "_wT", [8, T], BF16))
        invn = s2.enter_context(nc.sbuf_tensor(name + "_in", [128, CH], F32))
        if "invn_sb" not in consts_sb:
            c_invn = s2.enter_context(nc.sbuf_tensor(name + "_cinvn", [1, T], F32))
            kb.dma("sync", c_invn[:], consts_sb["invn_hbm"])
            consts_sb["invn_sb"] = c_invn
        gst = [s2.enter_context(nc.sbuf_tensor(f"{name}_gst{i}", [128, CH], BF16)) for i in range(4)]
        gfin = [s2.enter_context(nc.sbuf_tensor(f"{name}_gf{i}", [128, 8 * CH], BF16)) for i in range(2)]

        kb.dma("sync", rb1[:], whbm["rb1"].rearrange("k p -> p k"), allow_slow_non_contiguous=True)
        kb.dma("sync", comb[:], whbm["rb2_t"], new_group=False)
        kb.dma("sync", bexp[:], whbm["expb_t"], new_group=False)
        kb.dma("sync", Rk[:], consts_sb["Rk_hbm"], new_group=False)
        kb.dma("sync", alr[0:1, 0:1], whbm["alpha"], new_group=False)
        _, vld0 = kb.dma("sync", identf[:], consts_sb["identf_hbm"], new_group=False)
        _, vld1 = kb.dma("gpsimd", rw2[:].rearrange("p (k c) -> p k c", c=8),
                         whbm["rw2T"].rearrange("(k p) c -> p k c", p=128))
        kb.wait("vector", kb.sem("dma_sync"), vld0)
        kb.wait("vector", kb.sem("dma_gpsimd"), vld1)
        kb.chain("vector", lambda e: e.tensor_add(out=comb[:], in0=comb[:], in1=bexp[:]), s_c, drain=True)
        kb.chain("tensor", lambda e: e.matmul(psW[0][0:8, 0:1], consts_sb["ones_row"][:, 0:8],
                                              alr[0:1, 0:1], start=True, stop=True), s_c)
        kb.chain("vector", lambda e: e.tensor_copy(out=acol[:, 0:1], in_=psW[0][0:8, 0:1]), s_c, drain=True)
        kb.barrier()

        def branch_emit(b, br):
            qsrc = qk_hbm[0 if br == 0 else 2]
            ksrc = qk_hbm[1 if br == 0 else 3]
            VT = "Vf_T" if br == 0 else "Vi_T"
            WT = "Wf_T" if br == 0 else "Wi_T"
            UF = "Uf_flat" if br == 0 else "Ui_flat"
            BK = "bf_k" if br == 0 else "bi_k"
            kb.dma("gpsimd", Vb[:].rearrange("p (k c) -> p k c", c=512),
                   whbm[VT].rearrange("(k p) c -> p k c", p=128))
            kb.dma("gpsimd", Wb[:].rearrange("p (k c) -> p k c", c=512),
                   whbm[WT].rearrange("(k p) c -> p k c", p=128), new_group=False)
            kb.dma("gpsimd", rw1[:].rearrange("p (k c) -> p k c", c=1024),
                   whbm["rw1T"].rearrange("(k p) c -> p k c", p=128), new_group=False)
            _, vwl = kb.dma("gpsimd", bk[:], whbm[BK], new_group=False)
            kb.wait("tensor", kb.sem("dma_gpsimd"), vwl)
            kb.barrier()
            evd = {2: 0, 3: 0}
            for j in range(NCH):
                pj = j % 2
                _, vq = kb.dma("sync", sb_chunk3(qsb[pj]), hbm_chunk_view(qsrc, b, j), sem=s_qs,
                               wait=[(s_mm, s_mm.n)])
                for m in range(4):
                    par = 2 + (m % 2)
                    wmm = [(s_qs, vq)] if m == 0 else []
                    if evd[par]:
                        wmm.append((s_e, evd[par]))
                    for k in range(8):
                        kb.op("tensor", lambda e, m=m, k=k, par=par, pj=pj: e.matmul(
                            psW[par][:], Vb[:, k * 512 + m * 128:k * 512 + (m + 1) * 128],
                            qsb[pj][:, k * CH:(k + 1) * CH],
                            start=(k == 0), stop=(k == 7)),
                            wait=(wmm if k == 0 else []),
                            inc=([(s_mm, 1)] if k == 7 else []))
                    vm = s_mm.n
                    kb.op("scalar", lambda e, m=m, j=j, par=par: e.copy(
                        out=xV[:, m * T + j * CH:m * T + (j + 1) * CH], in_=psW[par][:]),
                        wait=[(s_mm, vm)], inc=[(s_e, 1)])
                    evd[par] = s_e.n
                for m in range(8):
                    par = 2 + (m % 2)
                    wmm = [(s_e, evd[par])] + ([(s_c, s_c.n)] if m == 0 else [])
                    for k in range(8):
                        kb.op("tensor", lambda e, m=m, k=k, par=par, pj=pj: e.matmul(
                            psW[par][:], rw1[:, k * 1024 + m * 128:k * 1024 + (m + 1) * 128],
                            qsb[pj][:, k * CH:(k + 1) * CH],
                            start=(k == 0), stop=(k == 7)),
                            wait=(wmm if k == 0 else []),
                            inc=([(s_mm, 1)] if k == 7 else []))
                    vm = s_mm.n
                    kb.op("scalar", lambda e, m=m, par=par: e.activation(
                        out=hc[:, m * CH:(m + 1) * CH], in_=psW[par][:], func=AF.Gelu,
                        bias=rb1[:, m:m + 1]),
                        wait=[(s_mm, vm)], inc=[(s_e, 1)])
                    evd[par] = s_e.n
                vh = s_e.n
                for tt in range(4):
                    ti = j * 4 + tt
                    for k in range(8):
                        kb.chain("tensor", lambda e, tt=tt, k=k: e.matmul(
                            psW[0][:, 0:8], hc[:, k * CH + tt * 128:k * CH + (tt + 1) * 128],
                            rw2[:, k * 8:(k + 1) * 8], start=(k == 0), stop=(k == 7)), s_c,
                            extra_wait=([(s_e, vh)] if (tt == 0 and k == 0) else []))
                    def smax(e, ti=ti):
                        e.tensor_add(out=warr[:, 0:8], in0=psW[0][:, 0:8], in1=comb[:])
                        e.drain()
                        e.tensor_reduce(out=warr[:, 12:13], in_=warr[:, 0:8], axis=AXX,
                                        op=ALU.max, negate=True)
                        return e.drain()
                    kb.chain("vector", smax, s_c, drain=True)
                    kb.chain("scalar", lambda e: e.activation(
                        out=warr[:, 16:24], in_=warr[:, 0:8], func=AF.Exp,
                        bias=warr[:, 12:13]), s_c, drain=True)
                    def smax2(e, ti=ti):
                        e.tensor_reduce(out=warr[:, 24:25], in_=warr[:, 16:24], axis=AXX, op=ALU.add)
                        e.drain()
                        e.reciprocal(out=warr[:, 28:29], in_=warr[:, 24:25])
                        e.drain()
                        return e.tensor_scalar(out=warr[:, 32 + (ti % 32) * 8:32 + (ti % 32) * 8 + 8]
                                               if False else warr[:, 32:40],
                                               in0=warr[:, 16:24], scalar1=warr[:, 28:29],
                                               scalar2=None, op0=ALU.mult)
                    kb.chain("vector", smax2, s_c, drain=True)
                    kb.chain("tensor", lambda e, ti=ti: e.transpose(
                        psW[0][0:8, 0:128], warr[:, 32:40], identf[:]), s_c)
                    kb.chain("vector", lambda e, ti=ti: e.tensor_copy(
                        out=wT[:, ti * 128:(ti + 1) * 128], in_=psW[0][0:8, 0:128]), s_c, drain=True)
            evd = {2: 0, 3: 0}
            for j in range(NCH):
                pj = j % 2
                _, vk = kb.dma("sync", sb_chunk3(qsb[pj]), hbm_chunk_view(ksrc, b, j), sem=s_ks,
                               wait=[(s_mm, s_mm.n)])
                for m in range(4):
                    par = 2 + (m % 2)
                    wmm = [(s_ks, vk)] if m == 0 else []
                    if evd[par]:
                        wmm.append((s_e, evd[par]))
                    for k in range(8):
                        kb.op("tensor", lambda e, m=m, k=k, par=par, pj=pj: e.matmul(
                            psW[par][:], Wb[:, k * 512 + m * 128:k * 512 + (m + 1) * 128],
                            qsb[pj][:, k * CH:(k + 1) * CH],
                            start=(k == 0), stop=(k == 7)),
                            wait=(wmm if k == 0 else []),
                            inc=([(s_mm, 1)] if k == 7 else []))
                    vm = s_mm.n
                    kb.op("scalar", lambda e, m=m, j=j, par=par: e.copy(
                        out=yW[:, m * T + j * CH:m * T + (j + 1) * CH], in_=psW[par][:]),
                        wait=[(s_mm, vm)], inc=[(s_e, 1)])
                    evd[par] = s_e.n
            vyw = s_e.n
            for m in range(4):
                kb.chain("vector", lambda e, m=m: e.tensor_tensor_scan(
                    out=yW[:, m * T:(m + 1) * T], data0=yW[:, m * T:(m + 1) * T],
                    data1=yW[:, m * T:(m + 1) * T], initial=0.0,
                    op0=ALU.add, op1=ALU.bypass), s_c,
                    extra_wait=([(s_e, vyw)] if m == 0 else []))
            if br == 1:
                kb.chain("vector", lambda e: e.tensor_scalar(
                    out=wT[:], in0=wT[:], scalar1=acol[:, 0:1], scalar2=None,
                    op0=ALU.mult), s_c, drain=True)
            for m in range(4):
                for j in range(NCH):
                    par = 2 + (j % 2)
                    kb.chain("tensor", lambda e, m=m, j=j, par=par: e.matmul(
                        psW[par][:], Rk[:, m * 128:(m + 1) * 128],
                        wT[:, j * CH:(j + 1) * CH], start=True, stop=True), s_c)
                    def prod_tt(e, m=m, j=j, par=par):
                        e.tensor_mul(out=prodw[:, m * T + j * CH:m * T + (j + 1) * CH],
                                     in0=xV[:, m * T + j * CH:m * T + (j + 1) * CH],
                                     in1=yW[:, m * T + j * CH:m * T + (j + 1) * CH])
                        return e.tensor_mul(out=prodw[:, m * T + j * CH:m * T + (j + 1) * CH],
                                            in0=prodw[:, m * T + j * CH:m * T + (j + 1) * CH],
                                            in1=psW[par][:])
                    kb.chain("vector", prod_tt, s_c)
            vprod = s_c.n
            _, vub = kb.dma("gpsimd", Ub[:, 0:4 * 1024].rearrange("p (k c) -> p k c", c=1024),
                            whbm[UF].rearrange("(k p) c -> p k c", p=128),
                            wait=[(s_c, vprod), (s_mm, s_mm.n)])
            kb.wait("tensor", kb.sem("dma_gpsimd"), vub)
            evd = {2: 0, 3: 0}
            gfree = [0, 0, 0, 0]
            for j in range(NCH):
                pj = j % 2
                kb.chain("tensor", lambda e, j=j: e.matmul(
                    psW[0][:, 0:CH], consts_sb["ones_row"][:],
                    consts_sb["invn_sb"][0:1, j * CH:(j + 1) * CH], start=True, stop=True), s_c)
                kb.chain("vector", lambda e: e.tensor_copy(out=invn[:], in_=psW[0][:, 0:CH]), s_c,
                         extra_wait=[(s_v, s_v.n)])
                vin = s_c.n
                if br == 1:
                    _, vgf = kb.dma("sync", sb_chunk3(gfin[pj]), hbm_chunk_view(g_hbm, b, j),
                                    sem=s_go, wait=[(s_v, s_v.n)])
                    branch_emit.gf_dma[j % 2] = vgf
                for m in range(8):
                    par = 2 + (m % 2)
                    gb = (8 * j + m) % 4
                    wmm = [(s_c, vin)] if m == 0 else []
                    if evd[par]:
                        wmm.append((s_v, evd[par]))
                    for k in range(4):
                        kb.op("tensor", lambda e, m=m, k=k, j=j, par=par: e.matmul(
                            psW[par][:], Ub[:, k * 1024 + m * 128:k * 1024 + (m + 1) * 128],
                            prodw[:, k * T + j * CH:k * T + (j + 1) * CH],
                            start=(k == 0), stop=False),
                            wait=(wmm if k == 0 else []))
                    kb.op("tensor", lambda e, m=m, j=j, par=par: e.matmul(
                        psW[par][:], bk[:, m * 128:(m + 1) * 128],
                        wT[:, j * CH:(j + 1) * CH], start=False, stop=True),
                        inc=[(s_mm, 1)])
                    vm = s_mm.n
                    wtv = [(s_mm, vm)]
                    if gfree[gb]:
                        wtv.append((s_gi, gfree[gb]))
                    if br == 0:
                        kb.op("vector", lambda e, par=par, gb=gb: e.tensor_mul(
                            out=gst[gb][:], in0=psW[par][:], in1=invn[:]),
                            wait=wtv, inc=[(s_v, 1)])
                    else:
                        def gcomb(e, par=par, gb=gb, m=m, pj=pj):
                            e.tensor_mul(out=gst[gb][:], in0=psW[par][:], in1=invn[:])
                            return e.tensor_add(out=gst[gb][:], in0=gst[gb][:],
                                                in1=gfin[pj][:, m * CH:(m + 1) * CH])
                        wtv2 = wtv + ([(s_go, branch_emit.gf_dma[j % 2])] if m == 0 else [])
                        kb.op("vector", gcomb, wait=wtv2, inc=[(s_v, 1)])
                    evd[par] = s_v.n
                    _, vg = kb.dma("sync", g_hbm[b][m * 128:(m + 1) * 128, j * CH:(j + 1) * CH],
                                   gst[gb][:], sem=s_gi, wait=[(s_v, evd[par])])
                    gfree[gb] = vg
            kb.barrier()
        branch_emit.gf_dma = [0, 0]

        for b in range(NB):
            branch_emit.gf_dma = [0, 0]
            branch_emit(b, 0)
            branch_emit(b, 1)
    kb.barrier()

    WOb = ctx.enter_context(nc.sbuf_tensor(name + "_wo", [128, 8 * 1024], BF16))
    _, vwo = kb.dma("gpsimd", WOb[:].rearrange("p (k c) -> p k c", c=1024),
                    whbm["WOT"].rearrange("(k p) c -> p k c", p=128))
    bufs = make_ln_bufs(ctx, nc, name + "_ln", with_default_fill=BF16)
    bufs["ps"] = ps
    kb.wait("tensor", kb.sem("dma_gpsimd"), vwo)
    kb.barrier()
    RR = []
    for b in range(NB):
        fill = make_matmul_fill(kb, ctx, nc, ps, f"{name}_f{b}", WOb, g_hbm,
                                x3f_hbm, bufs, b)

        def out_view(j, b=b):
            return (hbm_chunk_view(out_f32_hbm, b, j), hbm_chunk_view(out_bf_hbm, b, j))

        RR.append(emit_ln_stream(kb, bufs, consts_sb, NCH, fill, out_view, f"{name}_ln{b}"))
        kb.barrier()
    return RR



W1 = ["WqT", "WkT", "WvT", "WoT"]
W3 = ["gate_wT", "gate_b_t", "bi", "loraAT", "loraBT", "wiT", "woT", "bo_row"]
W4 = ["WdtpT", "WBT", "WCT", "WxsT", "WdtT", "bdt", "A_log", "Dp_col", "WoutT"]
W5 = ["WQT", "WKT", "WinvT", "WOT", "Vf_T", "Wf_T", "Uf_flat", "bf_k",
      "Vi_T", "Wi_T", "Ui_flat", "bi_k", "rw1T", "rb1", "rw2T", "rb2_t",
      "expb_t", "alpha"]
WALL = W1 + W3 + W4 + W5


def _setup_consts(nc, ctx, kb, hc):
    cs = {}
    c_onesb = ctx.enter_context(nc.sbuf_tensor("c_onesb", [128, 1], BF16))
    c_onesr = ctx.enter_context(nc.sbuf_tensor("c_onesr", [1, 128], F32))
    c_eps = ctx.enter_context(nc.sbuf_tensor("c_eps", [1, 1], F32))
    kb.dma("sync", c_onesb[:], nc.inline_tensor(hc["ones_col_bf"], "cc1").ap())
    kb.dma("sync", c_onesr[:], nc.inline_tensor(hc["ones_row"], "cc2").ap())
    kb.dma("sync", c_eps[:], nc.inline_tensor(np.full((1, 1), EPS, np.float32), "cc3").ap())
    cs["ones_col_bf"] = c_onesb
    cs["ones_row"] = c_onesr
    cs["eps11"] = c_eps
    cs["invn_hbm"] = nc.inline_tensor(hc["inv_n"], "cc4").ap()
    cs["tie_hbm"] = nc.inline_tensor(hc["tie_eps"], "ctie").ap()
    cs["Rl_hbm"] = nc.inline_tensor(hc["R_lora"].astype(BF), "cRl").ap()
    cs["Rk_hbm"] = nc.inline_tensor(hc["R_kr"].astype(BF), "cRk").ap()
    cs["ident_hbm"] = nc.inline_tensor(hc["identity_bf"], "cid").ap()
    cs["identf_hbm"] = nc.inline_tensor(np.eye(128, dtype=np.float32), "cidf").ap()
    return cs


def _build_fused(wshapes, hc):
    nc = bass.Bass("TRN2", target_bir_lowering=False, debug=False, detect_race_conditions=False)
    xres = nc.dram_tensor("xres", [NB, D, T], F32, kind="ExternalInput").ap()
    x4 = nc.dram_tensor("x4", [NB, D, T], F32, kind="ExternalOutput").ap()
    whbm = {n: nc.dram_tensor(n, list(wshapes[n]), F32, kind="ExternalInput").ap() for n in WALL}
    dr = lambda nm, sh, dt: nc.dram_tensor(nm, sh, dt, kind="Internal").ap()
    attnpre = dr("attnpre", [NB, D, T], BF16)
    x1f = dr("x1f", [NB, D, T], F32)
    x1b = dr("x1b", [NB, D, T], BF16)
    hsum = dr("hsum", [NB, DFF, T], BF16)
    ffnout = dr("ffnout", [NB, D, T], BF16)
    x2f = dr("x2f", [NB, D, T], F32)
    x2b = dr("x2b", [NB, D, T], BF16)
    x3f = dr("x3f", [NB, D, T], F32)
    x3b = dr("x3b", [NB, D, T], BF16)
    qk = [dr(f"qk{i}", [NB, D, T], BF16) for i in range(4)]
    g = dr("gcomb", [NB, D, T], BF16)
    x4b = dr("x4b", [NB, D, T], BF16)
    with ExitStack() as ctx:
        kb = KB(nc, ctx)
        cs = _setup_consts(nc, ctx, kb, hc)
        ps = make_psum_pool(ctx, nc)
        kb.barrier()
        with ExitStack() as pctx:
            emit_attn_phase(kb, pctx, nc, cs, ps, whbm, xres, attnpre, x1f, x1b)
            kb.barrier()
        with ExitStack() as pctx:
            emit_ffn0_phase(kb, pctx, nc, cs, ps, whbm, x1b, x1f, hsum, ffnout, x2f, x2b)
            kb.barrier()
        with ExitStack() as pctx:
            emit_ssm_phase(kb, pctx, nc, cs, ps, whbm, x2b, x2f, x3f, x3b)
            kb.barrier()
        with ExitStack() as pctx:
            emit_ffn1_phase(kb, pctx, nc, cs, ps, whbm, x3b, x3f, qk, g, x4, x4b)
            kb.barrier()
        kb.finalize()
    return nc


def kernel(**inputs):
    inputs = {k: np.asarray(v) for k, v in inputs.items()}
    w = prep_weights(inputs)
    hc = host_constants()
    NCORES = 8
    x = inputs["L0_attn_x"]
    shards = [np.ascontiguousarray(x[2 * c:2 * c + 2].transpose(0, 2, 1)) for c in range(NCORES)]
    wsh = {k: v.shape for k, v in w.items()}
    nc = _build_fused(wsh, hc)
    maps = [{"xres": shards[c], **{n: w[n] for n in WALL}} for c in range(NCORES)]
    rr = run_bass_kernel_spmd(nc, maps, core_ids=list(range(NCORES))).results
    out = np.concatenate([np.asarray(rr[c]["x4"]).transpose(0, 2, 1) for c in range(NCORES)], axis=0)
    return np.ascontiguousarray(out.astype(np.float32))


# version 06
# speedup vs baseline: 1.0377x; 1.0377x over previous
import numpy as np
import concourse.bass as bass
import concourse.mybir as mybir
from concourse.bass_utils import run_bass_kernel_spmd
from contextlib import ExitStack
import ml_dtypes

F32 = mybir.dt.float32
BF16 = mybir.dt.bfloat16
AF = mybir.ActivationFunctionType
ALU = mybir.AluOpType
AXX = mybir.AxisListType.X
BF = ml_dtypes.bfloat16

ENGINES = ("tensor", "vector", "scalar", "gpsimd", "sync")

D = 1024
T = 4096
NB = 2
NH = 16
HD = 64
DFF = 4096
EPS = 1e-5


class Sem:
    __slots__ = ("h", "n", "name")

    def __init__(self, h, name):
        self.h = h
        self.n = 0
        self.name = name

    def bump(self, k):
        self.n += k
        return self.n


class KB:

    def __init__(self, nc, ctx):
        self.nc = nc
        self.ctx = ctx
        self.q = {e: [] for e in ENGINES}
        self._sems = {}
        self.dma_sems = set()
        self.barrier_sem = self.sem("kb_barrier")

    def sem(self, name):
        if name not in self._sems:
            h = self.ctx.enter_context(self.nc.semaphore(name))
            self._sems[name] = Sem(h, name)
        return self._sems[name]

    def op(self, eng, fn, wait=(), inc=(), drain=False):
        wait = tuple((s, v) for s, v in wait)
        planned = []
        inc2 = []
        for s, k in inc:
            planned.append(s.bump(k))
            inc2.append((s, k))

        def thunk(e, fn=fn, wait=wait, inc2=tuple(inc2), drain=drain):
            for s, v in wait:
                e.wait_ge(s.h, v)
            inst = fn(e)
            if drain:
                e.drain()
                for s, k in inc2:
                    e.sem_inc(s.h, k)
            else:
                for s, k in inc2:
                    assert inst is not None, "inc attached to op returning None"
                    inst = inst.then_inc(s.h, k)
            return inst

        self.q[eng].append(thunk)
        return planned

    def wait(self, eng, sem, val):
        self.op(eng, lambda e: None, wait=[(sem, val)])

    def chain(self, eng, fn, s, extra_wait=(), drain=False):
        v = s.n
        self.op(eng, fn, wait=[(s, v)] + list(extra_wait), inc=[(s, 1)], drain=drain)
        return s.n

    def dma(self, queue, out, in_, sem=None, wait=(), new_group=True, **kw):
        s = sem if sem is not None else self.sem("dma_" + queue)
        self.dma_sems.add(s)
        w = list(wait)
        if new_group:
            w.append((s, s.n))
        v = self.op(queue, lambda e: e.dma_start(out=out, in_=in_, **kw),
                    wait=w, inc=[(s, 16)])
        return s, v[0]

    def barrier(self, engines=ENGINES, name=None):
        arr = self.barrier_sem
        active = [e for e in engines]
        dma_waits = [(s, s.n) for s in self.dma_sems if s.n > 0]
        for e in active:
            def arrive(eng, e=e):
                if e != "sync":
                    eng.drain()
                return eng.sem_inc(arr.h, 1)
            self.op(e, arrive)
            arr.bump(1)
        tgt = arr.n
        for e in active:
            def waitall(eng, tgt=tgt, dma_waits=tuple(dma_waits)):
                eng.wait_ge(arr.h, tgt)
                for s, v in dma_waits:
                    eng.wait_ge(s.h, v)
            self.q[e].append(waitall)

    def finalize(self):
        block = self.ctx.enter_context(self.nc.Block())
        for e in ENGINES:
            ops = self.q[e]
            if not ops:
                continue
            def runner(eng, ops=ops):
                for t in ops:
                    t(eng)
            getattr(block, e)(runner)



def prep_weights(inp):
    w = {}
    w["WqT"] = np.ascontiguousarray(inp["L0_attn_Wq"].T)
    w["WkT"] = np.ascontiguousarray(inp["L0_attn_Wk"].T)
    w["WvT"] = np.ascontiguousarray(inp["L0_attn_Wv"].T)
    w["WoT"] = np.ascontiguousarray(inp["L0_attn_Wo"].T)
    w["wiT"] = np.ascontiguousarray(inp["L0_ffn_wi"].T)
    w["woT"] = np.ascontiguousarray(inp["L0_ffn_wo"].T)
    w["bi"] = np.ascontiguousarray(inp["L0_ffn_bi"].reshape(32, 128))
    w["bo_row"] = np.ascontiguousarray(inp["L0_ffn_bo"].reshape(1, 1024))
    w["loraAT"] = np.ascontiguousarray(inp["L0_ffn_lora_A"].reshape(128, 1024).T)
    w["loraBT"] = np.ascontiguousarray(
        inp["L0_ffn_lora_B"].transpose(0, 2, 1).reshape(128, DFF))
    w["gate_wT"] = np.ascontiguousarray(inp["L0_ffn_gate_w"].T)
    w["gate_b_t"] = np.ascontiguousarray(np.tile(inp["L0_ffn_gate_b"].reshape(1, 8), (128, 1)))
    Wxp = inp["L1_attn_Wxproj"]
    w["WdtpT"] = np.ascontiguousarray(Wxp[:64].T)
    w["WBT"] = np.ascontiguousarray(Wxp[64:80].T)
    w["WCT"] = np.ascontiguousarray(Wxp[80:96].T)
    w["WxsT"] = np.ascontiguousarray(inp["L1_attn_Wxs"].T)
    w["WdtT"] = np.ascontiguousarray(inp["L1_attn_Wdt"].T)
    w["bdt"] = np.ascontiguousarray(inp["L1_attn_bdt"].reshape(1, 1))
    w["A_log"] = np.ascontiguousarray(inp["L1_attn_A_log"].reshape(1, 16))
    w["Dp_col"] = np.ascontiguousarray(inp["L1_attn_Dp"].reshape(8, 128))
    w["WoutT"] = np.ascontiguousarray(inp["L1_attn_Wout"].T)
    w["WQT"] = np.ascontiguousarray(inp["L1_ffn_W_Q"].T)
    w["WKT"] = np.ascontiguousarray(inp["L1_ffn_W_K"].T)
    w["WOT"] = np.ascontiguousarray(inp["L1_ffn_W_O"].T)
    w["WinvT"] = np.ascontiguousarray(inp["L1_ffn_W_inv"].T)
    w["Vf_T"] = np.ascontiguousarray(inp["L1_ffn_V_fwd"].transpose(1, 0, 2).reshape(D, 512))
    w["Wf_T"] = np.ascontiguousarray(inp["L1_ffn_W_fwd"].transpose(1, 0, 2).reshape(D, 512))
    w["Uf_flat"] = np.ascontiguousarray(inp["L1_ffn_U_fwd"].transpose(0, 2, 1).reshape(512, D))
    w["bf_k"] = np.ascontiguousarray(inp["L1_ffn_b_fwd"])
    w["Vi_T"] = np.ascontiguousarray(inp["L1_ffn_W_inv_exp"].transpose(1, 0, 2).reshape(D, 512))
    w["Wi_T"] = np.ascontiguousarray(inp["L1_ffn_V_inv"].transpose(1, 0, 2).reshape(D, 512))
    w["Ui_flat"] = np.ascontiguousarray(inp["L1_ffn_U_inv"].transpose(0, 2, 1).reshape(512, D))
    w["bi_k"] = np.ascontiguousarray(inp["L1_ffn_b_inv"])
    w["rw1T"] = np.ascontiguousarray(inp["L1_ffn_router_w1"].T)
    w["rb1"] = np.ascontiguousarray(inp["L1_ffn_router_b1"].reshape(8, 128))
    w["rw2T"] = np.ascontiguousarray(inp["L1_ffn_router_w2"].T)
    w["rb2"] = np.ascontiguousarray(inp["L1_ffn_router_b2"].reshape(1, 8))
    w["expb"] = np.ascontiguousarray(inp["L1_ffn_expert_bias"].reshape(1, 8))
    w["rb2_t"] = np.ascontiguousarray(np.tile(inp["L1_ffn_router_b2"].reshape(1, 8), (128, 1)))
    w["expb_t"] = np.ascontiguousarray(np.tile(inp["L1_ffn_expert_bias"].reshape(1, 8), (128, 1)))
    w["alpha"] = np.ascontiguousarray(np.asarray(inp["L1_ffn_alpha_bi"]).reshape(1, 1))
    return w


def host_constants():
    c = {}
    c["identity_bf"] = np.eye(128, dtype=BF)
    c["ones_col"] = np.ones((128, 1), np.float32)
    c["ones_col_bf"] = np.ones((128, 1), BF)
    c["ones_row"] = np.ones((1, 128), np.float32)
    c["inv_n"] = (1.0 / np.arange(1, T + 1, dtype=np.float64)).astype(np.float32).reshape(1, T)
    c["tie_eps"] = np.tile((np.arange(8, dtype=np.float32) * -1e-5).reshape(1, 8), (128, 1))
    R = np.zeros((8, 128), np.float32)
    for e in range(8):
        R[e, e * 16:(e + 1) * 16] = 1.0
    c["R_lora"] = R
    Rk = np.zeros((8, 512), np.float32)
    for k in range(8):
        Rk[k, k * 64:(k + 1) * 64] = 1.0
    c["R_kr"] = Rk
    return c



LN_CH = 512


def make_ln_bufs(ctx, nc, name, with_default_fill=None):
    CH = LN_CH
    bufs = {"name": name}
    if with_default_fill is not None:
        bufs["rin"] = [ctx.enter_context(nc.sbuf_tensor(f"{name}_rin{i}", [128, 8 * CH], F32)) for i in range(2)]
        if with_default_fill != "none":
            bufs["ain"] = [ctx.enter_context(nc.sbuf_tensor(f"{name}_ain{i}", [128, 8 * CH], with_default_fill)) for i in range(2)]
    bufs["zb"] = [ctx.enter_context(nc.sbuf_tensor(f"{name}_z{i}", [128, 8 * CH], F32)) for i in range(2)]
    bufs["zbf"] = ctx.enter_context(nc.sbuf_tensor(name + "_zbf", [128, 8 * CH], BF16))
    bufs["zsq"] = ctx.enter_context(nc.sbuf_tensor(name + "_zsq", [128, 8 * CH], BF16))
    bufs["rows"] = ctx.enter_context(nc.sbuf_tensor(name + "_rows", [1, 8 * CH], F32))
    bufs["xo"] = [ctx.enter_context(nc.sbuf_tensor(f"{name}_xo{i}", [128, 8 * CH], F32)) for i in range(2)]
    bufs["xob"] = [ctx.enter_context(nc.sbuf_tensor(f"{name}_xob{i}", [128, 8 * CH], BF16)) for i in range(2)]
    return bufs


def make_psum_pool(ctx, nc):
    ps = {}
    ps["ps_sz"] = ctx.enter_context(nc.psum_tensor("g_psz", [1, LN_CH], F32))
    ps["ps_sq"] = ctx.enter_context(nc.psum_tensor("g_psq", [1, LN_CH], F32))
    ps["psA"] = ctx.enter_context(nc.psum_tensor("g_psA", [128, LN_CH], F32))
    ps["psB"] = ctx.enter_context(nc.psum_tensor("g_psB", [128, LN_CH], F32))
    psP2 = ctx.enter_context(nc.psum_tensor("g_psP2", [128, 2 * LN_CH], F32))
    ps["psP2"] = psP2
    ps["psW"] = [psP2[:, 0:LN_CH], psP2[:, LN_CH:2 * LN_CH],
                 ctx.enter_context(nc.psum_tensor("g_psW2", [128, LN_CH], F32)),
                 ctx.enter_context(nc.psum_tensor("g_psW3", [128, LN_CH], F32))]
    return ps


def hbm_chunk_view(hbm, b, j, CH=LN_CH):
    return hbm[b].rearrange("(k p) t -> p k t", p=128)[:, :, j * CH:(j + 1) * CH]


def sb_chunk3(t):
    return t[:].rearrange("p (k t) -> p k t", k=8)


def emit_ln_stream(kb, bufs, consts_sb, nchunks, fill_z, out_view, name):
    CH = LN_CH
    zb = bufs["zb"]; zbf = bufs["zbf"]; zsq = bufs["zsq"]; rows = bufs["rows"]
    xo = bufs["xo"]; xob = bufs["xob"]
    ps = bufs["ps"]
    ps_sz = ps["ps_sz"]; ps_sq = ps["ps_sq"]; psA = ps["psA"]; psB = ps["psB"]
    s_out = kb.sem("LN_out")
    s_cg = kb.sem("LN_cg")
    s_cs = kb.sem("LN_cs")
    s_st = kb.sem("LN_st")
    s_rv = kb.sem("LN_rv")
    s_rs = kb.sem("LN_rs")
    s_bc = kb.sem("LN_bc")
    s_av = kb.sem("LN_av")
    s_ag = kb.sem("LN_ag")
    ones_bf = consts_sb["ones_col_bf"]
    ones_row = consts_sb["ones_row"]
    R = [dict() for _ in range(nchunks)]

    for j in range(nchunks):
        pj = j % 2
        r = R[j]
        wait_free = [(s_av, R[j - 2]["tt_done"])] if j >= 2 else []
        zsem, zval = fill_z(j, pj, wait_free)
        r["zready"] = zval

        wc = [(zsem, zval)]
        if j >= 1:
            wc = wc + [(s_st, R[j - 1]["stats"])]
        kb.op("gpsimd", lambda e, pj=pj: e.tensor_copy(out=zbf[:], in_=zb[pj][:]),
              wait=wc, inc=[(s_cg, 1)])
        r["cg"] = s_cg.n
        kb.op("scalar", lambda e, pj=pj: e.activation(out=zsq[:], in_=zb[pj][:], func=AF.Square),
              wait=wc, inc=[(s_cs, 1)])
        r["cs"] = s_cs.n

        wmm = [(s_cg, r["cg"]), (s_cs, r["cs"])]
        if j >= 1:
            wmm.append((s_rv, R[j - 1]["row1"]))
        for k in range(8):
            kb.op("tensor", lambda e, k=k: e.matmul(ps_sz[:], ones_bf[:],
                  zbf[:, k * CH:(k + 1) * CH], start=(k == 0), stop=(k == 7)),
                  wait=(wmm if k == 0 else []))
        for k in range(8):
            kb.op("tensor", lambda e, k=k: e.matmul(ps_sq[:], ones_bf[:],
                  zsq[:, k * CH:(k + 1) * CH], start=(k == 0), stop=(k == 7)),
                  inc=([(s_st, 1)] if k == 7 else []))
        r["stats"] = s_st.n

        mu = rows[:, 0:CH]; var = rows[:, CH:2 * CH]; a_r = rows[:, 2 * CH:3 * CH]
        b_r = rows[:, 3 * CH:4 * CH]; t0 = rows[:, 4 * CH:5 * CH]
        wrow = [(s_st, r["stats"])]
        if j >= 1:
            wrow.append((s_bc, R[j - 1]["bc"]))

        def rowmath(e):
            e.tensor_scalar_mul(out=mu, in0=ps_sz[:], scalar1=1.0 / D)
            e.tensor_scalar_mul(out=var, in0=ps_sq[:], scalar1=1.0 / D)
            e.tensor_mul(out=t0, in0=mu, in1=mu)
            return e.tensor_sub(out=var, in0=var, in1=t0)
        kb.op("vector", rowmath, wait=wrow, inc=[(s_rv, 1)])
        r["row1"] = s_rv.n

        def rowexp(e):
            e.activation(out=t0, in_=var, func=AF.Ln, bias=consts_sb["eps11"][:, 0:1])
            return e.activation(out=a_r, in_=t0, func=AF.Exp, scale=-0.5)
        kb.op("scalar", rowexp, wait=[(s_rv, r["row1"])], inc=[(s_rs, 1)])
        r["row2"] = s_rs.n

        def rowb(e):
            e.tensor_mul(out=b_r, in0=mu, in1=a_r)
            return e.tensor_scalar_mul(out=b_r, in0=b_r, scalar1=-1.0)
        kb.op("vector", rowb, wait=[(s_rs, r["row2"])], inc=[(s_rv, 1)])
        r["row3"] = s_rv.n

        wbc = [(s_rv, r["row3"])]
        if j >= 1:
            wbc.append((s_av, R[j - 1]["tt_done"]))
        kb.op("tensor", lambda e: e.matmul(psA[:], ones_row[:], a_r, start=True, stop=True),
              wait=wbc)
        kb.op("tensor", lambda e: e.matmul(psB[:], ones_row[:], b_r, start=True, stop=True),
              inc=[(s_bc, 1)])
        r["bc"] = s_bc.n

        wap = [(s_bc, r["bc"])]
        if j >= 2:
            wap.append((s_out, R[j - 2]["outdma"]))
        for k in range(8):
            sl = slice(k * CH, (k + 1) * CH)
            kb.op("vector", lambda e, pj=pj, sl=sl: e.tensor_mul(out=xo[pj][:, sl], in0=zb[pj][:, sl], in1=psA[:]),
                  wait=(wap if k == 0 else []))
            kb.op("vector", lambda e, pj=pj, sl=sl: e.tensor_add(out=xo[pj][:, sl], in0=xo[pj][:, sl], in1=psB[:]),
                  inc=([(s_av, 1)] if k == 7 else []))
        r["tt_done"] = s_av.n
        kb.op("gpsimd", lambda e, pj=pj: e.tensor_copy(out=xob[pj][:], in_=xo[pj][:]),
              wait=[(s_av, r["tt_done"])], inc=[(s_ag, 1)])
        r["ag"] = s_ag.n

        of, ob = out_view(j)
        kb.dma("sync", of, sb_chunk3(xo[pj]), sem=s_out, wait=[(s_av, r["tt_done"])])
        _, vo = kb.dma("sync", ob, sb_chunk3(xob[pj]), sem=s_out, new_group=False,
                       wait=[(s_ag, r["ag"])])
        r["outdma"] = vo
    return R


def emit_ln_phase(kb, ctx, nc, consts_sb, ps, res_hbm, addin_hbm, addin_dtype,
                  out_f32_hbm, out_bf_hbm, b_list, name, bufs=None):
    CH = LN_CH
    NCH = T // CH
    if bufs is None:
        bufs = make_ln_bufs(ctx, nc, name, with_default_fill=(addin_dtype if addin_hbm is not None else "none"))
        bufs["ps"] = ps
    rin = bufs["rin"]; ain = bufs.get("ain"); zb = bufs["zb"]
    s_in = [kb.sem("LN_in0"), kb.sem("LN_in1")]
    s_zc = kb.sem("LN_zc")
    nch_total = NCH * len(b_list)

    def fill_z(g, pj, wait_free):
        b = b_list[g // NCH]; j = g % NCH
        w = [(s_zc, fill_z.consumed[g - 2])] if g >= 2 else []
        sS = s_in[pj]
        _, v1 = kb.dma("sync", sb_chunk3(rin[pj]), hbm_chunk_view(res_hbm, b, j), sem=sS, wait=w)
        v2 = v1
        if ain is not None:
            _, v2 = kb.dma("sync", sb_chunk3(ain[pj]), hbm_chunk_view(addin_hbm, b, j), sem=sS, new_group=False)
        wz = [(sS, max(v1, v2))] + list(wait_free)
        if ain is not None:
            kb.op("vector", lambda e: e.tensor_add(out=zb[pj][:], in0=rin[pj][:], in1=ain[pj][:]),
                  wait=wz, inc=[(s_zc, 1)])
        else:
            kb.op("vector", lambda e: e.tensor_copy(out=zb[pj][:], in_=rin[pj][:]),
                  wait=wz, inc=[(s_zc, 1)])
        fill_z.consumed[g] = s_zc.n
        return (s_zc, s_zc.n)
    fill_z.consumed = [0] * nch_total

    def out_view(g):
        b = b_list[g // NCH]; j = g % NCH
        return (hbm_chunk_view(out_f32_hbm, b, j), hbm_chunk_view(out_bf_hbm, b, j))

    return emit_ln_stream(kb, bufs, consts_sb, nch_total, fill_z, out_view, name)



def emit_ssm_phase(kb, ctx, nc, consts_sb, ps, whbm, x2bf_hbm, x2f32_hbm,
                   out_f32_hbm, out_bf_hbm, name="ssm"):
    CH = LN_CH
    NCH = T // CH
    s = kb.sem(name + "_chain")
    s_xin = kb.sem(name + "_xin")
    s_rin = kb.sem(name + "_resin")
    s_gp = kb.sem("dma_gpsimd")
    psW = ps["psW"]

    Wdtp_bf = ctx.enter_context(nc.sbuf_tensor(name + "_wdtp", [128, 8 * 64], BF16))
    WB_bf = ctx.enter_context(nc.sbuf_tensor(name + "_wB", [128, 8 * 16], BF16))
    WC_bf = ctx.enter_context(nc.sbuf_tensor(name + "_wC", [128, 8 * 16], BF16))
    Wxs_bf = ctx.enter_context(nc.sbuf_tensor(name + "_wxs", [128, 8 * 16], BF16))
    WdtT_sb = ctx.enter_context(nc.sbuf_tensor(name + "_wdt", [64, 1], F32))
    sc = ctx.enter_context(nc.sbuf_tensor(name + "_sc", [1, 64], F32))
    Dp1 = ctx.enter_context(nc.sbuf_tensor(name + "_dp1", [128, 8], F32))
    WoutT_bf = ctx.enter_context(nc.sbuf_tensor(name + "_wout", [16, 1024], BF16))

    for wn, wsb, cc in (("WdtpT", Wdtp_bf, 64), ("WBT", WB_bf, 16), ("WCT", WC_bf, 16), ("WxsT", Wxs_bf, 16)):
        kb.dma("gpsimd", wsb[:].rearrange("p (k c) -> p k c", c=cc),
               whbm[wn].rearrange("(k p) c -> p k c", p=128))
    kb.dma("sync", WdtT_sb[:], whbm["WdtT"])
    kb.dma("sync", sc[0:1, 0:1], whbm["bdt"])
    kb.dma("sync", sc[0:1, 16:32], whbm["A_log"])
    kb.dma("sync", Dp1[:], whbm["Dp_col"].rearrange("k p -> p k"), allow_slow_non_contiguous=True)
    _, gv = kb.dma("gpsimd", WoutT_bf[:], whbm["WoutT"])
    ldv = kb.sem("dma_sync").n
    s_dma0 = kb.sem("dma_sync")
    kb.chain("vector", lambda e: e.tensor_scalar_mul(out=sc[0:1, 1:2], in0=sc[0:1, 0:1], scalar1=-1.0),
             s, extra_wait=[(s_dma0, ldv), (s_gp, gv)], drain=True)
    kb.chain("scalar", lambda e: e.activation(out=sc[0:1, 32:48], in_=sc[0:1, 16:32], func=AF.Exp), s, drain=True)
    kb.chain("vector", lambda e: e.tensor_scalar_add(out=Dp1[:], in0=Dp1[:], scalar1=1.0), s, drain=True)
    negbdt = sc[0:1, 1:2]
    E16 = sc[0:1, 32:48]

    x2bf = [ctx.enter_context(nc.sbuf_tensor(f"{name}_x2bf{i}", [128, 8 * CH], BF16)) for i in range(2)]
    dtp_sb = ctx.enter_context(nc.sbuf_tensor(name + "_dtp", [64, CH], F32))
    Bp = ctx.enter_context(nc.sbuf_tensor(name + "_Bp", [16, CH], F32))
    xsp = ctx.enter_context(nc.sbuf_tensor(name + "_xsp", [16, CH], F32))
    Cp = ctx.enter_context(nc.sbuf_tensor(name + "_Cp", [16, CH], F32))
    rowsS = ctx.enter_context(nc.sbuf_tensor(name + "_rows", [1, 2 * CH], F32))
    Ab = ctx.enter_context(nc.sbuf_tensor(name + "_Ab", [16, CH], F32))
    BxP = ctx.enter_context(nc.sbuf_tensor(name + "_Bx", [16, CH], F32))
    hs = ctx.enter_context(nc.sbuf_tensor(name + "_hs", [16, T], F32))
    hcbf = ctx.enter_context(nc.sbuf_tensor(name + "_hc", [16, T], BF16))

    bufs = make_ln_bufs(ctx, nc, name + "_ln", with_default_fill="none")
    bufs["ps"] = ps
    rin = bufs["rin"]; zb = bufs["zb"]
    s_ymm = kb.sem(name + "_ymm")
    s_yst = kb.sem(name + "_yst")

    RR = []
    proj_done = {}
    for b in range(NB):
        for j in range(NCH):
            pj = j % 2
            g = b * NCH + j
            wx = [(s, proj_done[g - 2])] if g >= 2 else []
            _, vx = kb.dma("sync", x2bf[pj][:].rearrange("p (k t) -> p k t", k=8),
                           hbm_chunk_view(x2bf_hbm, b, j), sem=s_xin, wait=wx)
            fw = [(s_xin, vx)]
            for (proj_sb, wbf, CC, pw) in ((dtp_sb, Wdtp_bf, 64, psW[0]), (Bp, WB_bf, 16, psW[1]),
                                           (xsp, Wxs_bf, 16, psW[2]), (Cp, WC_bf, 16, psW[3])):
                for k in range(8):
                    kb.chain("tensor", lambda e, pw=pw, k=k, wbf=wbf, CC=CC, pj=pj: e.matmul(
                        pw[0:CC, :], wbf[:, k * CC:(k + 1) * CC],
                        x2bf[pj][:, k * CH:(k + 1) * CH],
                        start=(k == 0), stop=(k == 7)), s, extra_wait=fw)
                    fw = []
                kb.chain("scalar", lambda e, pw=pw, proj_sb=proj_sb, CC=CC: e.copy(
                    out=proj_sb[:], in_=pw[0:CC, :]), s)
            proj_done[g] = s.n
            t1 = rowsS[:, 0:CH]; dtn = rowsS[:, CH:2 * CH]
            kb.chain("tensor", lambda e: e.matmul(psW[0][0:1, :], WdtT_sb[:], dtp_sb[:],
                                                  start=True, stop=True), s)
            kb.chain("scalar", lambda e: e.activation(out=t1, in_=psW[0][0:1, :], func=AF.Sigmoid,
                                                      scale=-1.0, bias=negbdt), s)
            kb.chain("scalar", lambda e: e.activation(out=dtn, in_=t1, func=AF.Ln), s)
            kb.chain("tensor", lambda e: e.matmul(psW[0][0:16, :], E16, dtn, start=True, stop=True), s)
            kb.chain("scalar", lambda e: e.activation(out=Ab[:], in_=psW[0][0:16, :], func=AF.Exp), s)
            kb.chain("vector", lambda e: e.tensor_scalar_mul(out=t1, in0=dtn, scalar1=-1.0), s)
            kb.chain("tensor", lambda e: e.matmul(psW[1][0:16, :], consts_sb["ones_row"][:, 0:16],
                                                  t1, start=True, stop=True), s)
            kb.chain("vector", lambda e: e.tensor_mul(out=BxP[:], in0=Bp[:], in1=xsp[:]), s)
            kb.chain("vector", lambda e: e.tensor_mul(out=BxP[:], in0=BxP[:], in1=psW[1][0:16, :]), s)
            if j == 0:
                kb.chain("vector", lambda e, j=j: e.tensor_tensor_scan(
                    out=hs[:, j * CH:(j + 1) * CH], data0=Ab[:], data1=BxP[:],
                    initial=0.0, op0=ALU.mult, op1=ALU.add), s)
            else:
                kb.chain("vector", lambda e, j=j: e.tensor_tensor_scan(
                    out=hs[:, j * CH:(j + 1) * CH], data0=Ab[:], data1=BxP[:],
                    initial=hs[:, j * CH - 1:j * CH], op0=ALU.mult, op1=ALU.add), s)
            kb.chain("vector", lambda e, j=j: e.tensor_mul(
                out=Ab[:], in0=hs[:, j * CH:(j + 1) * CH], in1=Cp[:]), s)
            kb.chain("scalar", lambda e, j=j: e.copy(out=hcbf[:, j * CH:(j + 1) * CH],
                                                     in_=Ab[:]), s)
        kb.barrier()

        def fill_z(j, pj, wait_free, b=b):
            w = [(s_yst, fill_z.consumed[j - 2])] if j >= 2 else []
            _, v1 = kb.dma("sync", sb_chunk3(rin[pj]), hbm_chunk_view(x2f32_hbm, b, j),
                           sem=s_rin, wait=w)
            for k in range(8):
                par = 2 + (k % 2)
                wmm = [(s_yst, fill_z.last_stt[par])] if fill_z.last_stt[par] else []
                kb.op("tensor", lambda e, k=k, j=j, par=par: e.matmul(
                    psW[par][:], WoutT_bf[:, k * 128:(k + 1) * 128],
                    hcbf[:, j * CH:(j + 1) * CH], start=True, stop=True),
                    wait=wmm, inc=[(s_ymm, 1)])
                vmm = s_ymm.n
                sl = slice(k * CH, (k + 1) * CH)
                wst = [(s_ymm, vmm), (s_rin, v1)] + (list(wait_free) if k == 0 else [])
                kb.op("vector", lambda e, pj=pj, sl=sl, k=k, par=par: e.scalar_tensor_tensor(
                    out=zb[pj][:, sl], in0=rin[pj][:, sl], scalar=Dp1[:, k:k + 1],
                    in1=psW[par][:], op0=ALU.mult, op1=ALU.add),
                    wait=wst, inc=[(s_yst, 1)])
                fill_z.last_stt[par] = s_yst.n
            fill_z.consumed[j] = s_yst.n
            return (s_yst, s_yst.n)
        fill_z.consumed = [0] * NCH
        fill_z.last_stt = {2: 0, 3: 0}

        def out_view(j, b=b):
            return (hbm_chunk_view(out_f32_hbm, b, j), hbm_chunk_view(out_bf_hbm, b, j))

        RR.append(emit_ln_stream(kb, bufs, consts_sb, NCH, fill_z, out_view, f"{name}_ln{b}"))
        kb.barrier()
    return RR


def sim_run(nc, in_map, out_names):
    from concourse.bass_interp import CoreSim
    nc.finalize()
    sim = CoreSim(nc, require_finite=False, require_nnan=False)
    for k, v in in_map.items():
        view = sim.tensor(k)
        view[:] = v
    sim.simulate()
    return {k: np.array(sim.tensor(k)) for k in out_names}



def make_matmul_fill(kb, ctx, nc, ps, name, WT_bf, act_hbm, res_hbm, bufs, b):
    CH = LN_CH
    psW = ps["psW"]
    rin = bufs["rin"]
    ain = bufs["ain"]
    zb = bufs["zb"]
    s_in = [kb.sem("FI_in0"), kb.sem("FI_in1")]
    s_mm = kb.sem("FI_mm")
    s_v = kb.sem("FI_v")

    def fill_z(j, pj, wait_free):
        w = [(s_v, fill_z.consumed[j - 2])] if j >= 2 else []
        sS = s_in[pj]
        _, v1 = kb.dma("sync", sb_chunk3(rin[pj]), hbm_chunk_view(res_hbm, b, j), sem=sS, wait=w)
        _, v2 = kb.dma("sync", sb_chunk3(ain[pj]), hbm_chunk_view(act_hbm, b, j), sem=sS, new_group=False)
        for m in range(8):
            par = 2 + (m % 2)
            wmm = [(sS, v2)] if m == 0 else []
            if fill_z.last_tt[par]:
                wmm.append((s_v, fill_z.last_tt[par]))
            for k in range(8):
                kb.op("tensor", lambda e, m=m, k=k, par=par, pj=pj: e.matmul(
                    psW[par][:], WT_bf[:, k * 1024 + m * 128:k * 1024 + (m + 1) * 128],
                    ain[pj][:, k * CH:(k + 1) * CH],
                    start=(k == 0), stop=(k == 7)),
                    wait=(wmm if k == 0 else []),
                    inc=([(s_mm, 1)] if k == 7 else []))
            vmm = s_mm.n
            sl = slice(m * CH, (m + 1) * CH)
            wst = [(s_mm, vmm)] + (list(wait_free) if m == 0 else [])
            kb.op("vector", lambda e, pj=pj, sl=sl, par=par: e.tensor_add(
                out=zb[pj][:, sl], in0=rin[pj][:, sl], in1=psW[par][:]),
                wait=wst, inc=[(s_v, 1)])
            fill_z.last_tt[par] = s_v.n
        fill_z.consumed[j] = s_v.n
        return (s_v, s_v.n)
    fill_z.consumed = [0] * (T // CH)
    fill_z.last_tt = {2: 0, 3: 0}
    return fill_z



def emit_attn_phase(kb, ctx, nc, consts_sb, ps, whbm, x_res_hbm, attnpre_hbm,
                    out_f32_hbm, out_bf_hbm, name="att"):
    CH = 512
    NCH = T // CH
    NTT = T // 128
    psW = ps["psW"]
    s_mm = kb.sem(name + "_mm")
    s_e = kb.sem(name + "_e")
    s_g = kb.sem(name + "_g")
    s_v = kb.sem(name + "_v")
    s_wq = kb.sem(name + "_wq")
    s_x = kb.sem(name + "_x")
    s_st = kb.sem(name + "_stdma")

    with ExitStack() as actx:
        x_bf = actx.enter_context(nc.sbuf_tensor(name + "_xbf", [128, 8 * T], BF16))
        V_aug = actx.enter_context(nc.sbuf_tensor(name + "_vaug", [128, NTT * NH * 65], BF16))
        WvT_bf = actx.enter_context(nc.sbuf_tensor(name + "_wv", [128, 8 * CH], BF16))
        wqk = actx.enter_context(nc.sbuf_tensor(name + "_wqk", [128, 2 * 8 * 64], BF16))
        Qa = actx.enter_context(nc.sbuf_tensor(name + "_qa", [66, T], BF16))
        Ka = actx.enter_context(nc.sbuf_tensor(name + "_ka", [66, T], BF16))
        qsq = actx.enter_context(nc.sbuf_tensor(name + "_qsq", [64, CH], BF16))
        Pt = actx.enter_context(nc.sbuf_tensor(name + "_pt", [128, NTT * CH], BF16))
        stage = [actx.enter_context(nc.sbuf_tensor(f"{name}_stage{i}", [64, CH], BF16)) for i in range(2)]
        arow = actx.enter_context(nc.sbuf_tensor(name + "_arow", [66, 16 + 2 * CH], F32))
        ones66 = actx.enter_context(nc.sbuf_tensor(name + "_o66", [66, 128], F32))
        recbc = actx.enter_context(nc.sbuf_tensor(name + "_recbc", [64, CH], F32))

        kn1 = arow[64:65, 0:8]
        mkcell = arow[64:65, 8:9]
        qnrow = arow[64:65, 16:16 + CH]
        rec = arow[64:65, 16 + CH:16 + 2 * CH]

        kb.op("vector", lambda e: e.memset(ones66[:], 1.0))
        kb.op("vector", lambda e: e.memset(Ka[64:65, :], 1.0))
        kb.op("vector", lambda e: e.memset(
            V_aug[:].rearrange("p (s c) -> p s c", c=65)[:, :, 64:65], 1.0), inc=[(s_v, 1)])

        last_STmm_h = 0
        last_PVmm = {}
        stage_dma = {0: 0, 1: 0}
        stage_tt_bank = {0: 0, 1: 0}
        rec_val = {0: 0, 1: 0}
        stage_tt = 0

        for b in range(NB):
            _, vx = kb.dma("gpsimd", x_bf[:].rearrange("p (k t) -> p k t", k=8),
                           hbm_chunk_bview(x_res_hbm, b), sem=s_x)
            vau_evict = [0] * (2 * NTT)
            for half in range(2):
                wload = [] if (b == 0 and half == 0) else [(s_mm, s_mm.n)]
                _, vwv = kb.dma("gpsimd", WvT_bf[:].rearrange("p (k c) -> p k c", c=CH),
                                whbm["WvT"].rearrange("(k p) c -> p k c", p=128)[:, :, half * CH:(half + 1) * CH],
                                sem=s_wq, wait=wload)
                for i in range(NTT):
                    u = half * NTT + i
                    par = 2 + (u % 2)
                    wv = [(s_x, vx), (s_wq, vwv)] if i == 0 else []
                    if i == 0 and half == 0:
                        wv = wv + [(s_v, s_v.n)]
                    if u >= 2:
                        wv = wv + [(s_e, vau_evict[u - 2])]
                    if b > 0 and u < 2:
                        wv = wv + [(s_v, stage_tt)]
                    for k in range(8):
                        kb.op("tensor", lambda e, i=i, k=k, par=par: e.matmul(
                            psW[par][:],
                            x_bf[:, k * T + i * 128:k * T + (i + 1) * 128],
                            WvT_bf[:, k * CH:(k + 1) * CH],
                            start=(k == 0), stop=(k == 7)),
                            wait=(wv if k == 0 else []),
                            inc=([(s_mm, 1)] if k == 7 else []))
                    vmi = s_mm.n
                    kb.op("scalar", lambda e, i=i, half=half, par=par: e.copy(
                        out=V_aug[:, i * NH * 65 + half * 8 * 65:i * NH * 65 + (half * 8 + 8) * 65].rearrange(
                            "p (h c) -> p h c", c=65)[:, :, 0:64],
                        in_=psW[par][:].rearrange("p (h c) -> p h c", c=64)),
                        wait=[(s_mm, vmi)], inc=[(s_e, 1)])
                    vau_evict[u] = s_e.n
            kb.barrier()
            for h in range(NH):
                _, vw = kb.dma("gpsimd", wqk[:, 0:512].rearrange("p (k c) -> p k c", c=64),
                               whbm["WqT"].rearrange("(k p) c -> p k c", p=128)[:, :, h * 64:(h + 1) * 64],
                               sem=s_wq)
                _, vw2 = kb.dma("gpsimd", wqk[:, 512:1024].rearrange("p (k c) -> p k c", c=64),
                                whbm["WkT"].rearrange("(k p) c -> p k c", p=128)[:, :, h * 64:(h + 1) * 64],
                                sem=s_wq, new_group=False)
                for (dst, off, scale) in ((Qa, 0, 0.125), (Ka, 512, 1.0)):
                    for j in range(NCH):
                        par = j % 2
                        wpr = []
                        if j == 0 and off == 0:
                            wpr = [(s_wq, vw2), (s_mm, last_STmm_h), (s_e, s_e.n)]
                        for k in range(8):
                            kb.op("tensor", lambda e, j=j, k=k, par=par, off=off: e.matmul(
                                psW[par][0:64, :], wqk[:, off + k * 64:off + (k + 1) * 64],
                                x_bf[:, k * T + j * CH:k * T + (j + 1) * CH],
                                start=(k == 0), stop=(k == 7)),
                                wait=(wpr if k == 0 else []),
                                inc=([(s_mm, 1)] if k == 7 else []))
                        vpj = s_mm.n
                        kb.op("scalar", lambda e, j=j, par=par, dst=dst, scale=scale: e.activation(
                            out=dst[0:64, j * CH:(j + 1) * CH], in_=psW[par][0:64, :],
                            func=AF.Identity, scale=scale),
                            wait=[(s_mm, vpj)], inc=[(s_e, 1)])
                vqk = s_e.n
                last_cons = ("v", s_v.n)
                prev_vns = 0
                for j in range(NCH):
                    wsq = [(s_e, vqk)] if j == 0 else [(s_mm, prev_vns)]
                    kb.op("scalar", lambda e, j=j: e.activation(
                        out=qsq[:], in_=Ka[0:64, j * CH:(j + 1) * CH], func=AF.Square),
                        wait=wsq, inc=[(s_e, 1)])
                    vsq = s_e.n
                    wnm = [(s_e, vsq), (s_v if last_cons[0] == "v" else s_e, last_cons[1])]
                    kb.op("tensor", lambda e: e.matmul(
                        psW[2][64:65, :], consts_sb["ones_col_bf"][0:64, :],
                        qsq[:], start=True, stop=True),
                        wait=wnm, inc=[(s_mm, 1)])
                    vns = s_mm.n
                    prev_vns = vns
                    kb.op("vector", lambda e, j=j: e.tensor_reduce(
                        out=kn1[:, j:j + 1], in_=psW[2][64:65, :], axis=AXX, op=ALU.max),
                        wait=[(s_mm, vns)], inc=[(s_v, 1)])
                    last_cons = ("v", s_v.n)
                kb.op("vector", lambda e: e.tensor_reduce(
                    out=mkcell, in_=kn1, axis=AXX, op=ALU.max), inc=[(s_v, 1)])
                vmk = s_v.n
                kb.op("scalar", lambda e: e.activation(out=mkcell, in_=mkcell, func=AF.Sqrt),
                      wait=[(s_v, vmk)], inc=[(s_e, 1)], drain=True)
                vmk2 = s_e.n
                for j in range(NCH):
                    kb.op("scalar", lambda e, j=j: e.activation(
                        out=qsq[:], in_=Qa[0:64, j * CH:(j + 1) * CH], func=AF.Square),
                        wait=([(s_v, s_v.n)] if j == 0 else []),
                        inc=[(s_e, 1)])
                    vsq = s_e.n
                    wnm = [(s_e, vsq), (s_v if last_cons[0] == "v" else s_e, last_cons[1])]
                    kb.op("tensor", lambda e: e.matmul(
                        psW[2][64:65, :], consts_sb["ones_col_bf"][0:64, :],
                        qsq[:], start=True, stop=True),
                        wait=wnm, inc=[(s_mm, 1)])
                    vns = s_mm.n
                    kb.op("scalar", lambda e: e.activation(
                        out=qnrow, in_=psW[2][64:65, :], func=AF.Sqrt),
                        wait=[(s_mm, vns)], inc=[(s_e, 1)])
                    vqn = s_e.n
                    kb.op("vector", lambda e, j=j: e.tensor_scalar(
                        out=Qa[64:65, j * CH:(j + 1) * CH], in0=qnrow, scalar1=mkcell, scalar2=-1.0,
                        op0=ALU.mult, op1=ALU.mult),
                        wait=[(s_e, vqn)], inc=[(s_v, 1)], drain=True)
                    last_cons = ("v", s_v.n)
                vaug = s_v.n

                psP2 = ps["psP2"]
                psA = ps["psA"]
                exp_pair_prev = 0
                pv_prev = []
                norm_pend = None

                def emit_pv(item):
                    tkt, sem_, val, blk_, ntk_, pvbank_, first, last = item
                    wpv = [(sem_, val)]
                    if first and stage_tt_bank[blk_ % 2]:
                        wpv.append((s_v, stage_tt_bank[blk_ % 2]))
                    kb.op("tensor", lambda e, tkt=tkt, ntk_=ntk_, h=h, pvbank_=pvbank_: e.matmul(
                        pvbank_[0:65, :],
                        V_aug[:, tkt * NH * 65 + h * 65:tkt * NH * 65 + (h + 1) * 65],
                        Pt[:, tkt * CH:(tkt + 1) * CH],
                        start=(tkt == 0), stop=(tkt == ntk_ - 1)),
                        wait=wpv, inc=[(s_mm, 1)])
                    last_PVmm[tkt] = s_mm.n
                    if last:
                        kb.op("vector", lambda e, pvbank_=pvbank_: e.reciprocal(
                            out=rec, in_=pvbank_[64:65, :]),
                            wait=[(s_mm, s_mm.n)], inc=[(s_v, 1)], drain=True)
                        rec_val[blk_ % 2] = s_v.n

                def emit_norm(nblk, nbank):
                    kb.op("tensor", lambda e: e.matmul(
                        psA[0:64, :], ones66[64:65, 0:64], rec, start=True, stop=True),
                        wait=[(s_v, rec_val[nblk % 2])], inc=[(s_mm, 1)])
                    vbc = s_mm.n
                    kb.op("vector", lambda e: e.tensor_copy(out=recbc[:], in_=psA[0:64, :]),
                          wait=[(s_mm, vbc)], inc=[(s_v, 1)])
                    vrecbc = s_v.n
                    wstg = [(s_v, vrecbc)]
                    if stage_dma[nblk % 2]:
                        wstg.append((s_st, stage_dma[nblk % 2]))
                    kb.op("vector", lambda e, nbank=nbank: e.tensor_mul(
                        out=stage[nblk % 2][:], in0=nbank[0:64, :], in1=recbc[:]),
                        wait=wstg, inc=[(s_v, 1)])
                    stage_tt_bank[nblk % 2] = s_v.n
                    _, sdv = kb.dma("sync", attnpre_hbm[b][h * 64:(h + 1) * 64, nblk * CH:(nblk + 1) * CH],
                                    stage[nblk % 2][:], sem=s_st, wait=[(s_v, stage_tt_bank[nblk % 2])])
                    stage_dma[nblk % 2] = sdv

                for blk in range(8):
                    ntk = 4 * (blk + 1)
                    pvbank = psW[2] if blk % 2 == 0 else psW[3]
                    pvw = []
                    exp_done = [0] * NTT
                    for p in range(ntk // 2):
                        for _ in range(2):
                            if pv_prev:
                                emit_pv(pv_prev.pop(0))
                        tA, tB = 2 * p, 2 * p + 1
                        wst = [(s_v, vaug)] if (blk == 0 and p == 0) else []
                        if p >= 1:
                            wst.append((s_e, exp_done[2 * p - 2]))
                        elif blk > 0:
                            wst.append((s_e, exp_pair_prev))
                        kb.op("tensor", lambda e, tA=tA, blk=blk: e.matmul(
                            psP2[:, 0:CH], Ka[0:65, tA * 128:(tA + 1) * 128],
                            Qa[0:65, blk * CH:(blk + 1) * CH], start=True, stop=True),
                            wait=wst)
                        kb.op("tensor", lambda e, tB=tB, blk=blk: e.matmul(
                            psP2[:, CH:2 * CH], Ka[0:65, tB * 128:(tB + 1) * 128],
                            Qa[0:65, blk * CH:(blk + 1) * CH], start=True, stop=True),
                            inc=[(s_mm, 1)])
                        vstm = s_mm.n
                        wexp = [(s_mm, vstm)]
                        for tx in (tA, tB):
                            if last_PVmm.get(tx):
                                wexp.append((s_mm, last_PVmm[tx]))
                        kb.op("scalar", lambda e, tA=tA: e.activation(
                            out=Pt[:, tA * CH:(tA + 2) * CH], in_=psP2[:, 0:2 * CH], func=AF.Exp),
                            wait=wexp, inc=[(s_e, 1)])
                        exp_done[tA] = exp_done[tB] = s_e.n
                        for tx in (tA, tB):
                            if tx * 128 >= blk * CH:
                                def mask(e, tx=tx, q0=blk * CH):
                                    i0 = tx * CH
                                    lo = tx * 128 - q0
                                    if lo > 0:
                                        e.memset(Pt[:, i0:i0 + lo], 0.0)
                                    return e.affine_select(
                                        out=Pt[:, i0 + lo:i0 + lo + 128], in_=Pt[:, i0 + lo:i0 + lo + 128],
                                        pattern=[[1, 128]], compare_op=ALU.is_ge, fill=0.0,
                                        base=0, channel_multiplier=-1)
                                kb.op("gpsimd", mask, wait=[(s_e, exp_done[tx])], inc=[(s_g, 1)])
                                pvw.append((tx, s_g, s_g.n, blk, ntk, pvbank, tx == 0, tx == ntk - 1))
                            else:
                                pvw.append((tx, s_e, exp_done[tx], blk, ntk, pvbank, tx == 0, tx == ntk - 1))
                    exp_pair_prev = exp_done[ntk - 1]
                    if blk == 7:
                        last_STmm_h = s_mm.n
                    while pv_prev:
                        emit_pv(pv_prev.pop(0))
                    if norm_pend is not None:
                        emit_norm(*norm_pend)
                    pv_prev = pvw
                    norm_pend = (blk, pvbank)
                while pv_prev:
                    emit_pv(pv_prev.pop(0))
                emit_norm(*norm_pend)
                stage_tt = stage_tt_bank[1]
            kb.barrier()
        kb.barrier()

    WoT_bf = ctx.enter_context(nc.sbuf_tensor(name + "_wo", [128, 8 * 1024], BF16))
    _, vwo = kb.dma("gpsimd", WoT_bf[:].rearrange("p (k c) -> p k c", c=1024),
                    whbm["WoT"].rearrange("(k p) c -> p k c", p=128))
    bufs = make_ln_bufs(ctx, nc, name + "_ln", with_default_fill=BF16)
    bufs["ps"] = ps
    kb.wait("tensor", kb.sem("dma_gpsimd"), vwo)
    kb.barrier()
    RR = []
    for b in range(NB):
        fill = make_matmul_fill(kb, ctx, nc, ps, f"{name}_f{b}", WoT_bf, attnpre_hbm,
                                x_res_hbm, bufs, b)

        def out_view(j, b=b):
            return (hbm_chunk_view(out_f32_hbm, b, j), hbm_chunk_view(out_bf_hbm, b, j))

        RR.append(emit_ln_stream(kb, bufs, consts_sb, NCH, fill, out_view, f"{name}_ln{b}"))
        kb.barrier()
    return RR


def hbm_chunk_bview(hbm, b):
    return hbm[b].rearrange("(k p) t -> p k t", p=128)



def emit_ffn0_phase(kb, ctx, nc, consts_sb, ps, whbm, x1bf_hbm, x1f_hbm,
                    hsum_hbm, ffnout_hbm, out_f32_hbm, out_bf_hbm, name="ffn0", cut=None):
    CH = 512
    NCH = T // CH
    NT = T // 128
    psW = ps["psW"]
    s_mm = kb.sem(name + "_mm")
    s_e = kb.sem(name + "_e")
    s_v = kb.sem(name + "_v")
    s_c = kb.sem(name + "_c")
    s_wi = kb.sem(name + "_wi")
    s_hs = kb.sem(name + "_hs")
    s_ho = kb.sem(name + "_ho")
    s_fo = kb.sem(name + "_fo")
    s_xb = kb.sem(name + "_xb")

    val_rows = ctx.enter_context(nc.sbuf_tensor(name + "_valr", [1, 2 * T], F32))

    def _subA(b, nb_):
        with ExitStack() as actx:
            x1_bf = actx.enter_context(nc.sbuf_tensor(nb_ + "_x1bf", [128, 8 * T], BF16))
            gwt = actx.enter_context(nc.sbuf_tensor(nb_ + "_gwt", [128, 8 * 8], BF16))
            gbt = actx.enter_context(nc.sbuf_tensor(nb_ + "_gbt", [128, 8], F32))
            tie = actx.enter_context(nc.sbuf_tensor(nb_ + "_tie", [128, 8], F32))
            laT = actx.enter_context(nc.sbuf_tensor(nb_ + "_laT", [128, 8 * 128], BF16))
            lbT = actx.enter_context(nc.sbuf_tensor(nb_ + "_lbT", [128, DFF], BF16))
            lbneg = actx.enter_context(nc.sbuf_tensor(nb_ + "_lbn", [128, 128], BF16))
            Rl = actx.enter_context(nc.sbuf_tensor(nb_ + "_Rl", [8, 128], BF16))
            bi_sb = actx.enter_context(nc.sbuf_tensor(nb_ + "_bi", [128, 32], F32))
            garr = actx.enter_context(nc.sbuf_tensor(nb_ + "_gr", [128, 1544], F32))
            indc = actx.enter_context(nc.sbuf_tensor(nb_ + "_indc", [128, 2 * NT * 8], BF16))
            valc = actx.enter_context(nc.sbuf_tensor(nb_ + "_valc", [128, NT], F32))
            ind1T = actx.enter_context(nc.sbuf_tensor(nb_ + "_i1T", [8, T], BF16))
            ind2T = actx.enter_context(nc.sbuf_tensor(nb_ + "_i2T", [8, T], BF16))
            vct = actx.enter_context(nc.sbuf_tensor(nb_ + "_vct", [32, 128], F32))
            z1 = actx.enter_context(nc.sbuf_tensor(nb_ + "_z1", [128, T], BF16))
            z2 = actx.enter_context(nc.sbuf_tensor(nb_ + "_z2", [128, T], BF16))
            iexp = actx.enter_context(nc.sbuf_tensor(nb_ + "_iexp", [128, CH], BF16))
            valbc = actx.enter_context(nc.sbuf_tensor(nb_ + "_vbc", [128, 8 * CH], F32))
            wic = [actx.enter_context(nc.sbuf_tensor(f"{nb_}_wic{i}", [128, 8 * 128], BF16)) for i in range(2)]
            hst = [actx.enter_context(nc.sbuf_tensor(f"{nb_}_hst{i}", [128, 2 * CH], BF16)) for i in range(4)]
            h2p = [actx.enter_context(nc.sbuf_tensor(f"{nb_}_h2p{i}", [128, CH], F32)) for i in range(2)]
            ident = actx.enter_context(nc.sbuf_tensor(nb_ + "_id", [128, 128], BF16))
            identf = actx.enter_context(nc.sbuf_tensor(nb_ + "_idf", [128, 128], F32))

            kb.dma("sync", gbt[:], whbm["gate_b_t"])
            kb.dma("sync", tie[:], consts_sb["tie_hbm"], new_group=False)
            kb.dma("sync", Rl[:], consts_sb["Rl_hbm"], new_group=False)
            kb.dma("sync", bi_sb[:], whbm["bi"].rearrange("m p -> p m"),
                   new_group=False, allow_slow_non_contiguous=True)
            kb.dma("sync", ident[:], consts_sb["ident_hbm"], new_group=False)
            _, vld = kb.dma("sync", identf[:], consts_sb["identf_hbm"], new_group=False)
            kb.dma("gpsimd", laT[:].rearrange("p (k c) -> p k c", c=128),
                   whbm["loraAT"].rearrange("(k p) c -> p k c", p=128))
            kb.dma("gpsimd", lbT[:], whbm["loraBT"], new_group=False)
            _, vld2 = kb.dma("gpsimd", gwt[:].rearrange("p (k c) -> p k c", c=8),
                             whbm["gate_wT"].rearrange("(k p) c -> p k c", p=128), new_group=False)
            _, vxb = kb.dma("gpsimd", x1_bf[:].rearrange("p (k t) -> p k t", k=8),
                            hbm_chunk_bview(x1bf_hbm, b), sem=s_xb)
            kb.wait("tensor", kb.sem("dma_sync"), vld)
            kb.wait("tensor", kb.sem("dma_gpsimd"), vld2)
            kb.wait("tensor", s_xb, vxb)
            kb.barrier()

            kb.chain("vector", lambda e: e.memset(garr[:], 0.0), s_c)
            lgA = garr[:, 0:256]; lselA = garr[:, 256:512]; maskedA = garr[:, 512:768]
            emA = garr[:, 768:1024]
            m1nA = garr[:, 1024:1152]
            mm1A = garr[:, 1152:1280]
            m2A = garr[:, 1280:1408]
            rrA = garr[:, 1408:1536]
            sc8 = garr[:, 1536:1544]
            for i in range(NT):
                for k in range(8):
                    kb.chain("tensor", lambda e, i=i, k=k: e.matmul(
                        psW[0][:, 0:8], x1_bf[:, k * T + i * 128:k * T + (i + 1) * 128],
                        gwt[:, k * 8:(k + 1) * 8], start=(k == 0), stop=(k == 7)), s_c)
                def p1(e, i=i):
                    e.tensor_add(out=lgA[:, i * 8:(i + 1) * 8], in0=psW[0][:, 0:8], in1=gbt[:])
                    e.drain()
                    e.tensor_add(out=lselA[:, i * 8:(i + 1) * 8], in0=lgA[:, i * 8:(i + 1) * 8], in1=tie[:])
                    e.drain()
                    e.tensor_reduce(out=m1nA[:, 4 * i:4 * i + 1], in_=lselA[:, i * 8:(i + 1) * 8],
                                    axis=AXX, op=ALU.max, negate=True)
                    e.drain()
                    return e.tensor_scalar_mul(out=mm1A[:, 4 * i:4 * i + 1],
                                               in0=m1nA[:, 4 * i:4 * i + 1], scalar1=-1.0)
                kb.chain("vector", p1, s_c, drain=True)
            for i in range(NT):
                def p2(e, i=i):
                    e.tensor_scalar(out=indc[:, i * 8:(i + 1) * 8], in0=lselA[:, i * 8:(i + 1) * 8],
                                    scalar1=mm1A[:, 4 * i:4 * i + 1], scalar2=None, op0=ALU.is_ge)
                    e.tensor_scalar(out=sc8, in0=lselA[:, i * 8:(i + 1) * 8],
                                    scalar1=mm1A[:, 4 * i:4 * i + 1], scalar2=None, op0=ALU.is_lt)
                    e.drain()
                    e.tensor_scalar(out=sc8, in0=sc8, scalar1=1e30, scalar2=-1e30,
                                    op0=ALU.mult, op1=ALU.add)
                    e.drain()
                    e.tensor_add(out=maskedA[:, i * 8:(i + 1) * 8], in0=lselA[:, i * 8:(i + 1) * 8], in1=sc8)
                    e.drain()
                    return e.tensor_reduce(out=m2A[:, 4 * i:4 * i + 1],
                                           in_=maskedA[:, i * 8:(i + 1) * 8], axis=AXX, op=ALU.max)
                kb.chain("vector", p2, s_c, drain=True)
            for i in range(NT):
                kb.chain("vector", lambda e, i=i: e.tensor_scalar(
                    out=indc[:, NT * 8 + i * 8:NT * 8 + (i + 1) * 8],
                    in0=maskedA[:, i * 8:(i + 1) * 8],
                    scalar1=m2A[:, 4 * i:4 * i + 1], scalar2=None, op0=ALU.is_ge), s_c, drain=True)
                kb.chain("scalar", lambda e, i=i: e.activation(
                    out=emA[:, i * 8:(i + 1) * 8], in_=lgA[:, i * 8:(i + 1) * 8],
                    func=AF.Exp, bias=m1nA[:, 4 * i:4 * i + 1]), s_c, drain=True)
            for i in range(NT):
                def p4(e, i=i):
                    e.tensor_reduce(out=rrA[:, 4 * i:4 * i + 1], in_=emA[:, i * 8:(i + 1) * 8],
                                    axis=AXX, op=ALU.add)
                    e.drain()
                    e.reciprocal(out=rrA[:, 4 * i + 1:4 * i + 2], in_=rrA[:, 4 * i:4 * i + 1])
                    e.tensor_add(out=sc8, in0=indc[:, i * 8:(i + 1) * 8],
                                 in1=indc[:, NT * 8 + i * 8:NT * 8 + (i + 1) * 8])
                    e.drain()
                    e.tensor_mul(out=sc8, in0=emA[:, i * 8:(i + 1) * 8], in1=sc8)
                    e.drain()
                    e.tensor_reduce(out=valc[:, i:i + 1], in_=sc8, axis=AXX, op=ALU.add)
                    e.drain()
                    return e.tensor_mul(out=valc[:, i:i + 1], in0=valc[:, i:i + 1],
                                        in1=rrA[:, 4 * i + 1:4 * i + 2])
                kb.chain("vector", p4, s_c, drain=True)
            for i in range(NT):
                kb.chain("tensor", lambda e, i=i: e.transpose(
                    psW[0][0:8, 0:128].bitcast(BF16)[:, 0:128],
                    indc[:, i * 8:(i + 1) * 8], ident[:]), s_c)
                kb.chain("vector", lambda e, i=i: e.tensor_copy(
                    out=ind1T[:, i * 128:(i + 1) * 128], in_=psW[0][0:8, 0:128].bitcast(BF16)[:, 0:128]), s_c)
                kb.chain("tensor", lambda e, i=i: e.transpose(
                    psW[0][0:8, 0:128].bitcast(BF16)[:, 0:128],
                    indc[:, NT * 8 + i * 8:NT * 8 + (i + 1) * 8], ident[:]), s_c)
                kb.chain("vector", lambda e, i=i: e.tensor_copy(
                    out=ind2T[:, i * 128:(i + 1) * 128], in_=psW[0][0:8, 0:128].bitcast(BF16)[:, 0:128]), s_c)
            kb.chain("tensor", lambda e: e.transpose(psW[0][0:32, 0:128], valc[:], identf[:]), s_c)
            kb.chain("vector", lambda e: e.tensor_copy(out=vct[:], in_=psW[0][0:32, 0:128]), s_c)
            vchain = s_c.n
            s_vr = kb.sem(name + "_vr")
            _, vvr = kb.dma("sync", val_rows[0:1, b * T:(b + 1) * T], vct[:], sem=s_vr,
                            wait=[(s_c, vchain)])
            for j in range(NCH):
                kb.chain("tensor", lambda e, j=j, b=b: e.matmul(
                    psW[0][:, 0:CH], consts_sb["ones_row"][:],
                    val_rows[0:1, b * T + j * CH:b * T + (j + 1) * CH], start=True, stop=True), s_c,
                    extra_wait=([(s_vr, vvr)] if j == 0 else []))
                kb.chain("vector", lambda e, j=j: e.tensor_copy(out=valbc[:, j * CH:(j + 1) * CH],
                                                                in_=psW[0][:, 0:CH]), s_c)
            if cut == "gate":
                if "dbg_garr" in whbm:
                    kb.dma("sync", whbm["dbg_garr"], garr[:, :], wait=[(s_c, s_c.n)])
                    kb.dma("gpsimd", whbm["dbg_indc"], indc[:, :], wait=[(s_c, s_c.n)])
                kb.barrier()
                return
            for j in range(NCH):
                for k in range(8):
                    kb.chain("tensor", lambda e, j=j, k=k: e.matmul(
                        psW[0][:, 0:CH], laT[:, k * 128:(k + 1) * 128],
                        x1_bf[:, k * T + j * CH:k * T + (j + 1) * CH],
                        start=(k == 0), stop=(k == 7)), s_c)
                kb.chain("scalar", lambda e, j=j: e.copy(out=z1[:, j * CH:(j + 1) * CH],
                                                         in_=psW[0][:, 0:CH]), s_c)
                kb.chain("tensor", lambda e, j=j: e.matmul(
                    psW[0][:, 0:CH], Rl[:], ind2T[:, j * CH:(j + 1) * CH], start=True, stop=True), s_c)
                kb.chain("vector", lambda e, j=j: e.tensor_copy(out=iexp[:], in_=psW[0][:, 0:CH]), s_c)
                kb.chain("vector", lambda e, j=j: e.tensor_mul(
                    out=z2[:, j * CH:(j + 1) * CH], in0=z1[:, j * CH:(j + 1) * CH], in1=iexp[:]), s_c)
                kb.chain("tensor", lambda e, j=j: e.matmul(
                    psW[0][:, 0:CH], Rl[:], ind1T[:, j * CH:(j + 1) * CH], start=True, stop=True), s_c)
                kb.chain("vector", lambda e, j=j: e.tensor_copy(out=iexp[:], in_=psW[0][:, 0:CH]), s_c)
                kb.chain("vector", lambda e, j=j: e.tensor_mul(
                    out=z1[:, j * CH:(j + 1) * CH], in0=z1[:, j * CH:(j + 1) * CH], in1=iexp[:]), s_c)
            kb.barrier()
            if cut == "z":
                return

            relu_done = {2: 0, 3: 0}
            uB_done = {0: 0, 1: 0}
            h2p_free = [0, 0]
            hst_free = [0, 0, 0, 0]
            wiv = [0, 0]
            for m in range(32):
                pw = m % 2
                wwi = [(s_mm, wiv[pw])] if wiv[pw] else []
                _, vwi = kb.dma("gpsimd", wic[pw][:].rearrange("p (k c) -> p k c", c=128),
                                whbm["wiT"].rearrange("(k p) c -> p k c", p=128)[:, :, m * 128:(m + 1) * 128],
                                sem=s_wi, wait=wwi)
                kb.op("vector", lambda e, pw=pw, m=m: e.tensor_scalar_mul(
                    out=lbneg[:], in0=lbT[:, m * 128:(m + 1) * 128], scalar1=-1.0),
                    wait=[(s_v, s_v.n), (s_mm, s_mm.n)], inc=[(s_v, 1)], drain=True)
                vneg = s_v.n
                for j in range(NCH):
                    par = 2 + (j % 2)
                    parB = j % 2
                    hb = (2 * m + j) % 4
                    wsh = [(s_wi, vwi), (s_v, vneg)] if j == 0 else []
                    if relu_done[par]:
                        wsh.append((s_e, relu_done[par]))
                    for k in range(8):
                        kb.op("tensor", lambda e, j=j, k=k, par=par, pw=pw: e.matmul(
                            psW[par][:], wic[pw][:, k * 128:(k + 1) * 128],
                            x1_bf[:, k * T + j * CH:k * T + (j + 1) * CH],
                            start=(k == 0), stop=False),
                            wait=(wsh if k == 0 else []))
                    kb.op("tensor", lambda e, j=j, par=par, m=m: e.matmul(
                        psW[par][:], lbT[:, m * 128:(m + 1) * 128],
                        z1[:, j * CH:(j + 1) * CH], start=False, stop=True),
                        inc=[(s_mm, 1)])
                    v1 = s_mm.n
                    wB = [(s_v, uB_done[parB])] if uB_done[parB] else []
                    kb.op("tensor", lambda e, j=j, parB=parB, m=m: e.matmul(
                        psW[parB][:], lbT[:, m * 128:(m + 1) * 128],
                        z2[:, j * CH:(j + 1) * CH], start=True, stop=False),
                        wait=wB)
                    kb.op("tensor", lambda e, j=j, parB=parB, m=m: e.matmul(
                        psW[parB][:], lbneg[:], z1[:, j * CH:(j + 1) * CH],
                        start=False, stop=True), inc=[(s_mm, 1)])
                    v2 = s_mm.n
                    whst = [(s_mm, v1)]
                    if hst_free[hb]:
                        whst.append((s_hs, hst_free[hb]))
                    kb.op("scalar", lambda e, par=par, m=m, hb=hb: e.activation(
                        out=hst[hb][:, 0:CH], in_=psW[par][:], func=AF.Relu,
                        bias=bi_sb[:, m:m + 1]),
                        wait=whst, inc=[(s_e, 1)])
                    ve1 = s_e.n
                    kb.op("scalar", lambda e, parB=parB, hb=hb: e.copy(
                        out=hst[hb][:, CH:2 * CH], in_=psW[parB][:]),
                        wait=[(s_mm, v2)], inc=[(s_e, 1)])
                    veB = s_e.n
                    kb.op("vector", lambda e, par=par, hb=hb: e.tensor_add(
                        out=h2p[hb % 2][:], in0=psW[par][:], in1=hst[hb][:, CH:2 * CH]),
                        wait=[(s_e, veB), (s_v, h2p_free[hb % 2])], inc=[(s_v, 1)])
                    vh2p = s_v.n
                    uB_done[parB] = vh2p
                    kb.op("scalar", lambda e, m=m, hb=hb: e.activation(
                        out=hst[hb][:, CH:2 * CH], in_=h2p[hb % 2][:], func=AF.Relu,
                        bias=bi_sb[:, m:m + 1]),
                        wait=[(s_v, vh2p)], inc=[(s_e, 1)])
                    ve2 = s_e.n
                    relu_done[par] = ve2
                    def hsum_tt(e, hb=hb, j=j):
                        e.tensor_add(out=hst[hb][:, 0:CH], in0=hst[hb][:, 0:CH], in1=hst[hb][:, CH:2 * CH])
                        return e.tensor_mul(out=hst[hb][:, 0:CH], in0=hst[hb][:, 0:CH],
                                            in1=valbc[:, j * CH:(j + 1) * CH])
                    kb.op("vector", hsum_tt, wait=[(s_e, ve2)], inc=[(s_v, 1)])
                    vtt = s_v.n
                    h2p_free[hb % 2] = vtt
                    _, vhs = kb.dma("sync", hsum_hbm[b][m * 128:(m + 1) * 128, j * CH:(j + 1) * CH],
                                    hst[hb][:, 0:CH], sem=s_hs, wait=[(s_v, vtt)])
                    hst_free[hb] = vhs
                wiv[pw] = s_mm.n
            kb.barrier()


    def _subB(b, nb_):
        with ExitStack() as bctx:
            woT_bf = bctx.enter_context(nc.sbuf_tensor(nb_ + "_wo", [128, 32 * 1024], BF16))
            hin = [bctx.enter_context(nc.sbuf_tensor(f"{nb_}_hin{i}", [128, 32 * CH], BF16)) for i in range(2)]
            bo_sb = bctx.enter_context(nc.sbuf_tensor(nb_ + "_bo", [1, 1024], F32))
            fst = [bctx.enter_context(nc.sbuf_tensor(f"{nb_}_fst{i}", [128, CH], BF16)) for i in range(4)]
            _, vwo = kb.dma("gpsimd", woT_bf[:].rearrange("p (k c) -> p k c", c=1024),
                            whbm["woT"].rearrange("(k p) c -> p k c", p=128))
            _, vbo = kb.dma("sync", bo_sb[:], whbm["bo_row"])
            kb.op("vector", lambda e, b=b: e.tensor_scalar_mul(
                out=val_rows[0:1, b * T:(b + 1) * T], in0=val_rows[0:1, b * T:(b + 1) * T],
                scalar1=2.0), wait=[(kb.sem("dma_sync"), kb.sem("dma_sync").n)], inc=[(s_v, 1)])
            vv2 = s_v.n
            kb.wait("tensor", kb.sem("dma_gpsimd"), vwo)
            kb.wait("tensor", kb.sem("dma_sync"), vbo)
            kb.wait("tensor", s_v, vv2)
            kb.barrier()
            evict_done = {2: 0, 3: 0}
            fst_free = [0, 0, 0, 0]
            hin_free = [0, 0]
            for j in range(NCH):
                pj = j % 2
                whi = [(s_mm, hin_free[pj])] if hin_free[pj] else []
                _, vhi = kb.dma("sync", hin[pj][:].rearrange("p (k t) -> p k t", k=32),
                                hsum_hbm[b].rearrange("(k p) t -> p k t", p=128)[:, :, j * CH:(j + 1) * CH],
                                sem=s_ho, wait=whi)
                for mo in range(8):
                    par = 2 + (mo % 2)
                    fb = (8 * j + mo) % 4
                    wmm = [(s_ho, vhi)] if mo == 0 else []
                    if evict_done[par]:
                        wmm.append((s_e, evict_done[par]))
                    for k in range(32):
                        kb.op("tensor", lambda e, mo=mo, k=k, par=par, pj=pj: e.matmul(
                            psW[par][:], woT_bf[:, k * 1024 + mo * 128:k * 1024 + (mo + 1) * 128],
                            hin[pj][:, k * CH:(k + 1) * CH],
                            start=(k == 0), stop=False),
                            wait=(wmm if k == 0 else []))
                    kb.op("tensor", lambda e, mo=mo, j=j, b=b, par=par: e.matmul(
                        psW[par][:], bo_sb[0:1, mo * 128:(mo + 1) * 128],
                        val_rows[0:1, b * T + j * CH:b * T + (j + 1) * CH],
                        start=False, stop=True), inc=[(s_mm, 1)])
                    vmm = s_mm.n
                    wev = [(s_mm, vmm)]
                    if fst_free[fb]:
                        wev.append((s_fo, fst_free[fb]))
                    kb.op("scalar", lambda e, par=par, fb=fb: e.copy(
                        out=fst[fb][:], in_=psW[par][:]),
                        wait=wev, inc=[(s_e, 1)])
                    evict_done[par] = s_e.n
                    _, vfo = kb.dma("sync", ffnout_hbm[b][mo * 128:(mo + 1) * 128, j * CH:(j + 1) * CH],
                                    fst[fb][:], sem=s_fo, wait=[(s_e, evict_done[par])])
                    fst_free[fb] = vfo
                hin_free[pj] = s_mm.n
            kb.barrier()


    for b in range(NB):
        _subA(b, f"{name}{b}")
        if cut is None:
            _subB(b, f"{name}{b}")

    if "dbg_val" in whbm:
        kb.dma("sync", whbm["dbg_val"], val_rows[0:1, :])
        kb.barrier()

    if cut is not None:
        return None
    return emit_ln_phase(kb, ctx, nc, consts_sb, ps, x1f_hbm, ffnout_hbm, BF16,
                         out_f32_hbm, out_bf_hbm, list(range(NB)), name + "_ln")



def emit_ffn1_phase(kb, ctx, nc, consts_sb, ps, whbm, x3bf_hbm, x3f_hbm,
                    qk_hbm, g_hbm, out_f32_hbm, out_bf_hbm, name="ffn1"):
    CH = 512
    NCH = T // CH
    NT = T // 128
    psW = ps["psW"]
    s_mm = kb.sem(name + "_mm")
    s_e = kb.sem(name + "_e")
    s_v = kb.sem(name + "_v")
    s_c = kb.sem(name + "_c")
    s_ld = kb.sem(name + "_ld")
    s_qs = kb.sem(name + "_qs")
    s_ks = kb.sem(name + "_ks")
    s_go = kb.sem(name + "_go")
    s_gi = kb.sem(name + "_gi")

    with ExitStack() as s1:
        x3bf = s1.enter_context(nc.sbuf_tensor(name + "_x3", [128, 8 * T], BF16))
        WQb = s1.enter_context(nc.sbuf_tensor(name + "_wq", [128, 8 * 1024], BF16))
        WKb = s1.enter_context(nc.sbuf_tensor(name + "_wk", [128, 8 * 1024], BF16))
        WIb = s1.enter_context(nc.sbuf_tensor(name + "_wi", [128, 8 * 1024], BF16))
        qc = [s1.enter_context(nc.sbuf_tensor(f"{name}_qc{i}", [128, 8 * CH], BF16)) for i in range(2)]
        for wn, wsb in (("WQT", WQb), ("WKT", WKb), ("WinvT", WIb)):
            kb.dma("gpsimd", wsb[:].rearrange("p (k c) -> p k c", c=1024),
                   whbm[wn].rearrange("(k p) c -> p k c", p=128))
        vw = kb.sem("dma_gpsimd").n
        kb.wait("tensor", kb.sem("dma_gpsimd"), vw)
        kb.barrier()

        def s1_emit(b):
            _, vx = kb.dma("gpsimd", x3bf[:].rearrange("p (k t) -> p k t", k=8),
                           hbm_chunk_bview(x3bf_hbm, b), sem=s_ld)
            evd = {2: 0, 3: 0}
            qcfree = [0, 0]
            for src_i, (wsb, dst_i) in enumerate(((WQb, 0), (WKb, 1))):
                for j in range(NCH):
                    pj = j % 2
                    wq0 = [(s_ld, vx)] if (j == 0 and src_i == 0) else []
                    if qcfree[pj]:
                        wq0 = wq0 + [(s_mm, qcfree[pj])]
                    for m in range(8):
                        par = 2 + (m % 2)
                        wmm = list(wq0) if m == 0 else []
                        if evd[par]:
                            wmm.append((s_e, evd[par]))
                        for k in range(8):
                            kb.op("tensor", lambda e, m=m, k=k, j=j, par=par, wsb=wsb: e.matmul(
                                psW[par][:], wsb[:, k * 1024 + m * 128:k * 1024 + (m + 1) * 128],
                                x3bf[:, k * T + j * CH:k * T + (j + 1) * CH],
                                start=(k == 0), stop=(k == 7)),
                                wait=(wmm if k == 0 else []),
                                inc=([(s_mm, 1)] if k == 7 else []))
                        vm = s_mm.n
                        kb.op("scalar", lambda e, m=m, par=par, pj=pj: e.copy(
                            out=qc[pj][:, m * CH:(m + 1) * CH], in_=psW[par][:]),
                            wait=([(s_mm, vm), (s_qs, s_qs.n)] if m == 0 else [(s_mm, vm)]),
                            inc=[(s_e, 1)])
                        evd[par] = s_e.n
                    vch = s_e.n
                    _, vq = kb.dma("sync", hbm_chunk_view(qk_hbm[dst_i], b, j),
                                   sb_chunk3(qc[pj]), sem=s_qs, wait=[(s_e, vch)])
                    for m in range(8):
                        par = 2 + (m % 2)
                        wmm = [(s_e, evd[par])]
                        for k in range(8):
                            kb.op("tensor", lambda e, m=m, k=k, par=par, pj=pj: e.matmul(
                                psW[par][:], WIb[:, k * 1024 + m * 128:k * 1024 + (m + 1) * 128],
                                qc[pj][:, k * CH:(k + 1) * CH],
                                start=(k == 0), stop=(k == 7)),
                                wait=(wmm if k == 0 else []),
                                inc=([(s_mm, 1)] if k == 7 else []))
                        vm = s_mm.n
                        kb.op("scalar", lambda e, m=m, par=par, pj=pj: e.copy(
                            out=qc[pj][:, m * CH:(m + 1) * CH], in_=psW[par][:]),
                            wait=[(s_mm, vm), (s_qs, vq)] if m == 0 else [(s_mm, vm)],
                            inc=[(s_e, 1)])
                        evd[par] = s_e.n
                    vch = s_e.n
                    _, vq2 = kb.dma("sync", hbm_chunk_view(qk_hbm[2 + dst_i], b, j),
                                    sb_chunk3(qc[pj]), sem=s_qs, wait=[(s_e, vch)])
                    qcfree[pj] = s_mm.n
            kb.barrier()
        for b in range(NB):
            s1_emit(b)
    kb.barrier()

    with ExitStack() as s2:
        Vb = s2.enter_context(nc.sbuf_tensor(name + "_Vb", [128, 8 * 512], BF16))
        Wb = s2.enter_context(nc.sbuf_tensor(name + "_Wb", [128, 8 * 512], BF16))
        rwU = s2.enter_context(nc.sbuf_tensor(name + "_rwU", [128, 8 * 1024], BF16))
        rw1 = rwU
        Ub = rwU
        rw2 = s2.enter_context(nc.sbuf_tensor(name + "_rw2", [128, 8 * 8], BF16))
        rb1 = s2.enter_context(nc.sbuf_tensor(name + "_rb1", [128, 8], F32))
        comb = s2.enter_context(nc.sbuf_tensor(name + "_cmb", [128, 8], F32))
        bexp = s2.enter_context(nc.sbuf_tensor(name + "_bex", [128, 8], F32))
        bk = s2.enter_context(nc.sbuf_tensor(name + "_bk", [8, 1024], BF16))
        Rk = s2.enter_context(nc.sbuf_tensor(name + "_Rk", [8, 512], BF16))
        alr = s2.enter_context(nc.sbuf_tensor(name + "_al", [1, 4], F32))
        acol = s2.enter_context(nc.sbuf_tensor(name + "_ac", [8, 4], F32))
        identf = s2.enter_context(nc.sbuf_tensor(name + "_idf", [128, 128], F32))
        xV = s2.enter_context(nc.sbuf_tensor(name + "_xV", [128, 4 * T], BF16))
        yW = s2.enter_context(nc.sbuf_tensor(name + "_yW", [128, 4 * T], BF16))
        prodw = s2.enter_context(nc.sbuf_tensor(name + "_pw", [128, 4 * T], BF16))
        hc = s2.enter_context(nc.sbuf_tensor(name + "_hc", [128, 8 * CH], BF16))
        qsb = [s2.enter_context(nc.sbuf_tensor(f"{name}_qsb{i}", [128, 8 * CH], BF16)) for i in range(2)]
        warr = s2.enter_context(nc.sbuf_tensor(name + "_wa", [128, 512], F32))
        wT = s2.enter_context(nc.sbuf_tensor(name + "_wT", [8, T], BF16))
        invn = s2.enter_context(nc.sbuf_tensor(name + "_in", [128, CH], F32))
        if "invn_sb" not in consts_sb:
            c_invn = s2.enter_context(nc.sbuf_tensor(name + "_cinvn", [1, T], F32))
            kb.dma("sync", c_invn[:], consts_sb["invn_hbm"])
            consts_sb["invn_sb"] = c_invn
        gst = [s2.enter_context(nc.sbuf_tensor(f"{name}_gst{i}", [128, CH], BF16)) for i in range(4)]
        gfin = [s2.enter_context(nc.sbuf_tensor(f"{name}_gf{i}", [128, 8 * CH], BF16)) for i in range(2)]

        kb.dma("sync", rb1[:], whbm["rb1"].rearrange("k p -> p k"), allow_slow_non_contiguous=True)
        kb.dma("sync", comb[:], whbm["rb2_t"], new_group=False)
        kb.dma("sync", bexp[:], whbm["expb_t"], new_group=False)
        kb.dma("sync", Rk[:], consts_sb["Rk_hbm"], new_group=False)
        kb.dma("sync", alr[0:1, 0:1], whbm["alpha"], new_group=False)
        _, vld0 = kb.dma("sync", identf[:], consts_sb["identf_hbm"], new_group=False)
        _, vld1 = kb.dma("gpsimd", rw2[:].rearrange("p (k c) -> p k c", c=8),
                         whbm["rw2T"].rearrange("(k p) c -> p k c", p=128))
        kb.wait("vector", kb.sem("dma_sync"), vld0)
        kb.wait("vector", kb.sem("dma_gpsimd"), vld1)
        kb.chain("vector", lambda e: e.tensor_add(out=comb[:], in0=comb[:], in1=bexp[:]), s_c, drain=True)
        kb.chain("tensor", lambda e: e.matmul(psW[0][0:8, 0:1], consts_sb["ones_row"][:, 0:8],
                                              alr[0:1, 0:1], start=True, stop=True), s_c)
        kb.chain("vector", lambda e: e.tensor_copy(out=acol[:, 0:1], in_=psW[0][0:8, 0:1]), s_c, drain=True)
        kb.barrier()

        def branch_emit(b, br):
            qsrc = qk_hbm[0 if br == 0 else 2]
            ksrc = qk_hbm[1 if br == 0 else 3]
            VT = "Vf_T" if br == 0 else "Vi_T"
            WT = "Wf_T" if br == 0 else "Wi_T"
            UF = "Uf_flat" if br == 0 else "Ui_flat"
            BK = "bf_k" if br == 0 else "bi_k"
            kb.dma("gpsimd", Vb[:].rearrange("p (k c) -> p k c", c=512),
                   whbm[VT].rearrange("(k p) c -> p k c", p=128))
            kb.dma("gpsimd", Wb[:].rearrange("p (k c) -> p k c", c=512),
                   whbm[WT].rearrange("(k p) c -> p k c", p=128), new_group=False)
            kb.dma("gpsimd", rw1[:].rearrange("p (k c) -> p k c", c=1024),
                   whbm["rw1T"].rearrange("(k p) c -> p k c", p=128), new_group=False)
            _, vwl = kb.dma("gpsimd", bk[:], whbm[BK], new_group=False)
            kb.wait("tensor", kb.sem("dma_gpsimd"), vwl)
            kb.barrier()
            evd = {2: 0, 3: 0}
            for j in range(NCH):
                pj = j % 2
                _, vq = kb.dma("sync", sb_chunk3(qsb[pj]), hbm_chunk_view(qsrc, b, j), sem=s_qs,
                               wait=[(s_mm, s_mm.n)])
                for m in range(4):
                    par = 2 + (m % 2)
                    wmm = [(s_qs, vq)] if m == 0 else []
                    if evd[par]:
                        wmm.append((s_e, evd[par]))
                    for k in range(8):
                        kb.op("tensor", lambda e, m=m, k=k, par=par, pj=pj: e.matmul(
                            psW[par][:], Vb[:, k * 512 + m * 128:k * 512 + (m + 1) * 128],
                            qsb[pj][:, k * CH:(k + 1) * CH],
                            start=(k == 0), stop=(k == 7)),
                            wait=(wmm if k == 0 else []),
                            inc=([(s_mm, 1)] if k == 7 else []))
                    vm = s_mm.n
                    kb.op("scalar", lambda e, m=m, j=j, par=par: e.copy(
                        out=xV[:, m * T + j * CH:m * T + (j + 1) * CH], in_=psW[par][:]),
                        wait=[(s_mm, vm)], inc=[(s_e, 1)])
                    evd[par] = s_e.n
                for m in range(8):
                    par = 2 + (m % 2)
                    wmm = [(s_e, evd[par])] + ([(s_c, s_c.n)] if m == 0 else [])
                    for k in range(8):
                        kb.op("tensor", lambda e, m=m, k=k, par=par, pj=pj: e.matmul(
                            psW[par][:], rw1[:, k * 1024 + m * 128:k * 1024 + (m + 1) * 128],
                            qsb[pj][:, k * CH:(k + 1) * CH],
                            start=(k == 0), stop=(k == 7)),
                            wait=(wmm if k == 0 else []),
                            inc=([(s_mm, 1)] if k == 7 else []))
                    vm = s_mm.n
                    kb.op("scalar", lambda e, m=m, par=par: e.activation(
                        out=hc[:, m * CH:(m + 1) * CH], in_=psW[par][:], func=AF.Gelu,
                        bias=rb1[:, m:m + 1]),
                        wait=[(s_mm, vm)], inc=[(s_e, 1)])
                    evd[par] = s_e.n
                vh = s_e.n
                for tt in range(4):
                    ti = j * 4 + tt
                    for k in range(8):
                        kb.chain("tensor", lambda e, tt=tt, k=k: e.matmul(
                            psW[0][:, 0:8], hc[:, k * CH + tt * 128:k * CH + (tt + 1) * 128],
                            rw2[:, k * 8:(k + 1) * 8], start=(k == 0), stop=(k == 7)), s_c,
                            extra_wait=([(s_e, vh)] if (tt == 0 and k == 0) else []))
                    def smax(e, ti=ti):
                        e.tensor_add(out=warr[:, 0:8], in0=psW[0][:, 0:8], in1=comb[:])
                        e.drain()
                        e.tensor_reduce(out=warr[:, 12:13], in_=warr[:, 0:8], axis=AXX,
                                        op=ALU.max, negate=True)
                        return e.drain()
                    kb.chain("vector", smax, s_c, drain=True)
                    kb.chain("scalar", lambda e: e.activation(
                        out=warr[:, 16:24], in_=warr[:, 0:8], func=AF.Exp,
                        bias=warr[:, 12:13]), s_c, drain=True)
                    def smax2(e, ti=ti):
                        e.tensor_reduce(out=warr[:, 24:25], in_=warr[:, 16:24], axis=AXX, op=ALU.add)
                        e.drain()
                        e.reciprocal(out=warr[:, 28:29], in_=warr[:, 24:25])
                        e.drain()
                        return e.tensor_scalar(out=warr[:, 32 + (ti % 32) * 8:32 + (ti % 32) * 8 + 8]
                                               if False else warr[:, 32:40],
                                               in0=warr[:, 16:24], scalar1=warr[:, 28:29],
                                               scalar2=None, op0=ALU.mult)
                    kb.chain("vector", smax2, s_c, drain=True)
                    kb.chain("tensor", lambda e, ti=ti: e.transpose(
                        psW[0][0:8, 0:128], warr[:, 32:40], identf[:]), s_c)
                    kb.chain("vector", lambda e, ti=ti: e.tensor_copy(
                        out=wT[:, ti * 128:(ti + 1) * 128], in_=psW[0][0:8, 0:128]), s_c, drain=True)
            evd = {2: 0, 3: 0}
            for j in range(NCH):
                pj = j % 2
                _, vk = kb.dma("sync", sb_chunk3(qsb[pj]), hbm_chunk_view(ksrc, b, j), sem=s_ks,
                               wait=[(s_mm, s_mm.n)])
                for m in range(4):
                    par = 2 + (m % 2)
                    wmm = [(s_ks, vk)] if m == 0 else []
                    if evd[par]:
                        wmm.append((s_e, evd[par]))
                    for k in range(8):
                        kb.op("tensor", lambda e, m=m, k=k, par=par, pj=pj: e.matmul(
                            psW[par][:], Wb[:, k * 512 + m * 128:k * 512 + (m + 1) * 128],
                            qsb[pj][:, k * CH:(k + 1) * CH],
                            start=(k == 0), stop=(k == 7)),
                            wait=(wmm if k == 0 else []),
                            inc=([(s_mm, 1)] if k == 7 else []))
                    vm = s_mm.n
                    kb.op("scalar", lambda e, m=m, j=j, par=par: e.copy(
                        out=yW[:, m * T + j * CH:m * T + (j + 1) * CH], in_=psW[par][:]),
                        wait=[(s_mm, vm)], inc=[(s_e, 1)])
                    evd[par] = s_e.n
            vyw = s_e.n
            for m in range(4):
                kb.chain("vector", lambda e, m=m: e.tensor_tensor_scan(
                    out=yW[:, m * T:(m + 1) * T], data0=yW[:, m * T:(m + 1) * T],
                    data1=yW[:, m * T:(m + 1) * T], initial=0.0,
                    op0=ALU.add, op1=ALU.bypass), s_c,
                    extra_wait=([(s_e, vyw)] if m == 0 else []))
            if br == 1:
                kb.chain("vector", lambda e: e.tensor_scalar(
                    out=wT[:], in0=wT[:], scalar1=acol[:, 0:1], scalar2=None,
                    op0=ALU.mult), s_c, drain=True)
            for m in range(4):
                for j in range(NCH):
                    par = 2 + (j % 2)
                    kb.chain("tensor", lambda e, m=m, j=j, par=par: e.matmul(
                        psW[par][:], Rk[:, m * 128:(m + 1) * 128],
                        wT[:, j * CH:(j + 1) * CH], start=True, stop=True), s_c)
                    def prod_tt(e, m=m, j=j, par=par):
                        e.tensor_mul(out=prodw[:, m * T + j * CH:m * T + (j + 1) * CH],
                                     in0=xV[:, m * T + j * CH:m * T + (j + 1) * CH],
                                     in1=yW[:, m * T + j * CH:m * T + (j + 1) * CH])
                        return e.tensor_mul(out=prodw[:, m * T + j * CH:m * T + (j + 1) * CH],
                                            in0=prodw[:, m * T + j * CH:m * T + (j + 1) * CH],
                                            in1=psW[par][:])
                    kb.chain("vector", prod_tt, s_c)
            vprod = s_c.n
            _, vub = kb.dma("gpsimd", Ub[:, 0:4 * 1024].rearrange("p (k c) -> p k c", c=1024),
                            whbm[UF].rearrange("(k p) c -> p k c", p=128),
                            wait=[(s_c, vprod), (s_mm, s_mm.n)])
            kb.wait("tensor", kb.sem("dma_gpsimd"), vub)
            evd = {2: 0, 3: 0}
            gfree = [0, 0, 0, 0]
            for j in range(NCH):
                pj = j % 2
                kb.chain("tensor", lambda e, j=j: e.matmul(
                    psW[0][:, 0:CH], consts_sb["ones_row"][:],
                    consts_sb["invn_sb"][0:1, j * CH:(j + 1) * CH], start=True, stop=True), s_c)
                kb.chain("vector", lambda e: e.tensor_copy(out=invn[:], in_=psW[0][:, 0:CH]), s_c,
                         extra_wait=[(s_v, s_v.n)])
                vin = s_c.n
                if br == 1:
                    _, vgf = kb.dma("sync", sb_chunk3(gfin[pj]), hbm_chunk_view(g_hbm, b, j),
                                    sem=s_go, wait=[(s_v, s_v.n)])
                    branch_emit.gf_dma[j % 2] = vgf
                for m in range(8):
                    par = 2 + (m % 2)
                    gb = (8 * j + m) % 4
                    wmm = [(s_c, vin)] if m == 0 else []
                    if evd[par]:
                        wmm.append((s_v, evd[par]))
                    for k in range(4):
                        kb.op("tensor", lambda e, m=m, k=k, j=j, par=par: e.matmul(
                            psW[par][:], Ub[:, k * 1024 + m * 128:k * 1024 + (m + 1) * 128],
                            prodw[:, k * T + j * CH:k * T + (j + 1) * CH],
                            start=(k == 0), stop=False),
                            wait=(wmm if k == 0 else []))
                    kb.op("tensor", lambda e, m=m, j=j, par=par: e.matmul(
                        psW[par][:], bk[:, m * 128:(m + 1) * 128],
                        wT[:, j * CH:(j + 1) * CH], start=False, stop=True),
                        inc=[(s_mm, 1)])
                    vm = s_mm.n
                    wtv = [(s_mm, vm)]
                    if gfree[gb]:
                        wtv.append((s_gi, gfree[gb]))
                    if br == 0:
                        kb.op("vector", lambda e, par=par, gb=gb: e.tensor_mul(
                            out=gst[gb][:], in0=psW[par][:], in1=invn[:]),
                            wait=wtv, inc=[(s_v, 1)])
                    else:
                        def gcomb(e, par=par, gb=gb, m=m, pj=pj):
                            e.tensor_mul(out=gst[gb][:], in0=psW[par][:], in1=invn[:])
                            return e.tensor_add(out=gst[gb][:], in0=gst[gb][:],
                                                in1=gfin[pj][:, m * CH:(m + 1) * CH])
                        wtv2 = wtv + ([(s_go, branch_emit.gf_dma[j % 2])] if m == 0 else [])
                        kb.op("vector", gcomb, wait=wtv2, inc=[(s_v, 1)])
                    evd[par] = s_v.n
                    _, vg = kb.dma("sync", g_hbm[b][m * 128:(m + 1) * 128, j * CH:(j + 1) * CH],
                                   gst[gb][:], sem=s_gi, wait=[(s_v, evd[par])])
                    gfree[gb] = vg
            kb.barrier()
        branch_emit.gf_dma = [0, 0]

        for b in range(NB):
            branch_emit.gf_dma = [0, 0]
            branch_emit(b, 0)
            branch_emit(b, 1)
    kb.barrier()

    WOb = ctx.enter_context(nc.sbuf_tensor(name + "_wo", [128, 8 * 1024], BF16))
    _, vwo = kb.dma("gpsimd", WOb[:].rearrange("p (k c) -> p k c", c=1024),
                    whbm["WOT"].rearrange("(k p) c -> p k c", p=128))
    bufs = make_ln_bufs(ctx, nc, name + "_ln", with_default_fill=BF16)
    bufs["ps"] = ps
    kb.wait("tensor", kb.sem("dma_gpsimd"), vwo)
    kb.barrier()
    RR = []
    for b in range(NB):
        fill = make_matmul_fill(kb, ctx, nc, ps, f"{name}_f{b}", WOb, g_hbm,
                                x3f_hbm, bufs, b)

        def out_view(j, b=b):
            return (hbm_chunk_view(out_f32_hbm, b, j), hbm_chunk_view(out_bf_hbm, b, j))

        RR.append(emit_ln_stream(kb, bufs, consts_sb, NCH, fill, out_view, f"{name}_ln{b}"))
        kb.barrier()
    return RR



W1 = ["WqT", "WkT", "WvT", "WoT"]
W3 = ["gate_wT", "gate_b_t", "bi", "loraAT", "loraBT", "wiT", "woT", "bo_row"]
W4 = ["WdtpT", "WBT", "WCT", "WxsT", "WdtT", "bdt", "A_log", "Dp_col", "WoutT"]
W5 = ["WQT", "WKT", "WinvT", "WOT", "Vf_T", "Wf_T", "Uf_flat", "bf_k",
      "Vi_T", "Wi_T", "Ui_flat", "bi_k", "rw1T", "rb1", "rw2T", "rb2_t",
      "expb_t", "alpha"]
WALL = W1 + W3 + W4 + W5


def _setup_consts(nc, ctx, kb, hc):
    cs = {}
    c_onesb = ctx.enter_context(nc.sbuf_tensor("c_onesb", [128, 1], BF16))
    c_onesr = ctx.enter_context(nc.sbuf_tensor("c_onesr", [1, 128], F32))
    c_eps = ctx.enter_context(nc.sbuf_tensor("c_eps", [1, 1], F32))
    kb.dma("sync", c_onesb[:], nc.inline_tensor(hc["ones_col_bf"], "cc1").ap())
    kb.dma("sync", c_onesr[:], nc.inline_tensor(hc["ones_row"], "cc2").ap())
    kb.dma("sync", c_eps[:], nc.inline_tensor(np.full((1, 1), EPS, np.float32), "cc3").ap())
    cs["ones_col_bf"] = c_onesb
    cs["ones_row"] = c_onesr
    cs["eps11"] = c_eps
    cs["invn_hbm"] = nc.inline_tensor(hc["inv_n"], "cc4").ap()
    cs["tie_hbm"] = nc.inline_tensor(hc["tie_eps"], "ctie").ap()
    cs["Rl_hbm"] = nc.inline_tensor(hc["R_lora"].astype(BF), "cRl").ap()
    cs["Rk_hbm"] = nc.inline_tensor(hc["R_kr"].astype(BF), "cRk").ap()
    cs["ident_hbm"] = nc.inline_tensor(hc["identity_bf"], "cid").ap()
    cs["identf_hbm"] = nc.inline_tensor(np.eye(128, dtype=np.float32), "cidf").ap()
    return cs


def _build_fused(wshapes, hc):
    nc = bass.Bass("TRN2", target_bir_lowering=False, debug=False, detect_race_conditions=False)
    xres = nc.dram_tensor("xres", [NB, D, T], F32, kind="ExternalInput").ap()
    x4 = nc.dram_tensor("x4", [NB, D, T], F32, kind="ExternalOutput").ap()
    whbm = {n: nc.dram_tensor(n, list(wshapes[n]), F32, kind="ExternalInput").ap() for n in WALL}
    dr = lambda nm, sh, dt: nc.dram_tensor(nm, sh, dt, kind="Internal").ap()
    attnpre = dr("attnpre", [NB, D, T], BF16)
    x1f = dr("x1f", [NB, D, T], F32)
    x1b = dr("x1b", [NB, D, T], BF16)
    hsum = dr("hsum", [NB, DFF, T], BF16)
    ffnout = dr("ffnout", [NB, D, T], BF16)
    x2f = dr("x2f", [NB, D, T], F32)
    x2b = dr("x2b", [NB, D, T], BF16)
    x3f = dr("x3f", [NB, D, T], F32)
    x3b = dr("x3b", [NB, D, T], BF16)
    qk = [dr(f"qk{i}", [NB, D, T], BF16) for i in range(4)]
    g = dr("gcomb", [NB, D, T], BF16)
    x4b = dr("x4b", [NB, D, T], BF16)
    with ExitStack() as ctx:
        kb = KB(nc, ctx)
        cs = _setup_consts(nc, ctx, kb, hc)
        ps = make_psum_pool(ctx, nc)
        kb.barrier()
        with ExitStack() as pctx:
            emit_attn_phase(kb, pctx, nc, cs, ps, whbm, xres, attnpre, x1f, x1b)
            kb.barrier()
        with ExitStack() as pctx:
            emit_ffn0_phase(kb, pctx, nc, cs, ps, whbm, x1b, x1f, hsum, ffnout, x2f, x2b)
            kb.barrier()
        with ExitStack() as pctx:
            emit_ssm_phase(kb, pctx, nc, cs, ps, whbm, x2b, x2f, x3f, x3b)
            kb.barrier()
        with ExitStack() as pctx:
            emit_ffn1_phase(kb, pctx, nc, cs, ps, whbm, x3b, x3f, qk, g, x4, x4b)
            kb.barrier()
        kb.finalize()
    return nc


def kernel(**inputs):
    inputs = {k: np.asarray(v) for k, v in inputs.items()}
    w = prep_weights(inputs)
    hc = host_constants()
    NCORES = 8
    x = inputs["L0_attn_x"]
    shards = [np.ascontiguousarray(x[2 * c:2 * c + 2].transpose(0, 2, 1)) for c in range(NCORES)]
    wsh = {k: v.shape for k, v in w.items()}
    nc = _build_fused(wsh, hc)
    maps = [{"xres": shards[c], **{n: w[n] for n in WALL}} for c in range(NCORES)]
    rr = run_bass_kernel_spmd(nc, maps, core_ids=list(range(NCORES))).results
    out = np.concatenate([np.asarray(rr[c]["x4"]).transpose(0, 2, 1) for c in range(NCORES)], axis=0)
    return np.ascontiguousarray(out.astype(np.float32))


# version 12
# speedup vs baseline: 1.0548x; 1.0164x over previous
import numpy as np
import concourse.bass as bass
import concourse.mybir as mybir
from concourse.bass_utils import run_bass_kernel_spmd
from contextlib import ExitStack
import ml_dtypes

F32 = mybir.dt.float32
BF16 = mybir.dt.bfloat16
AF = mybir.ActivationFunctionType
ALU = mybir.AluOpType
AXX = mybir.AxisListType.X
BF = ml_dtypes.bfloat16

ENGINES = ("tensor", "vector", "scalar", "gpsimd", "sync")

D = 1024
T = 4096
NB = 2
NH = 16
HD = 64
DFF = 4096
EPS = 1e-5


class Sem:
    __slots__ = ("h", "n", "name")

    def __init__(self, h, name):
        self.h = h
        self.n = 0
        self.name = name

    def bump(self, k):
        self.n += k
        return self.n


class KB:

    def __init__(self, nc, ctx):
        self.nc = nc
        self.ctx = ctx
        self.q = {e: [] for e in ENGINES}
        self._sems = {}
        self.dma_sems = set()
        self.barrier_sem = self.sem("kb_barrier")

    def sem(self, name):
        if name not in self._sems:
            h = self.ctx.enter_context(self.nc.semaphore(name))
            self._sems[name] = Sem(h, name)
        return self._sems[name]

    def op(self, eng, fn, wait=(), inc=(), drain=False):
        wait = tuple((s, v) for s, v in wait)
        planned = []
        inc2 = []
        for s, k in inc:
            planned.append(s.bump(k))
            inc2.append((s, k))

        def thunk(e, fn=fn, wait=wait, inc2=tuple(inc2), drain=drain):
            for s, v in wait:
                e.wait_ge(s.h, v)
            inst = fn(e)
            if drain:
                e.drain()
                for s, k in inc2:
                    e.sem_inc(s.h, k)
            else:
                for s, k in inc2:
                    assert inst is not None, "inc attached to op returning None"
                    inst = inst.then_inc(s.h, k)
            return inst

        self.q[eng].append(thunk)
        return planned

    def wait(self, eng, sem, val):
        self.op(eng, lambda e: None, wait=[(sem, val)])

    def chain(self, eng, fn, s, extra_wait=(), drain=False):
        v = s.n
        self.op(eng, fn, wait=[(s, v)] + list(extra_wait), inc=[(s, 1)], drain=drain)
        return s.n

    def dma(self, queue, out, in_, sem=None, wait=(), new_group=True, **kw):
        s = sem if sem is not None else self.sem("dma_" + queue)
        self.dma_sems.add(s)
        w = list(wait)
        if new_group:
            w.append((s, s.n))
        v = self.op(queue, lambda e: e.dma_start(out=out, in_=in_, **kw),
                    wait=w, inc=[(s, 16)])
        return s, v[0]

    def barrier(self, engines=ENGINES, name=None):
        arr = self.barrier_sem
        active = [e for e in engines]
        dma_waits = [(s, s.n) for s in self.dma_sems if s.n > 0]
        for e in active:
            def arrive(eng, e=e):
                if e != "sync":
                    eng.drain()
                return eng.sem_inc(arr.h, 1)
            self.op(e, arrive)
            arr.bump(1)
        tgt = arr.n
        for e in active:
            def waitall(eng, tgt=tgt, dma_waits=tuple(dma_waits)):
                eng.wait_ge(arr.h, tgt)
                for s, v in dma_waits:
                    eng.wait_ge(s.h, v)
            self.q[e].append(waitall)

    def finalize(self):
        block = self.ctx.enter_context(self.nc.Block())
        for e in ENGINES:
            ops = self.q[e]
            if not ops:
                continue
            def runner(eng, ops=ops):
                for t in ops:
                    t(eng)
            getattr(block, e)(runner)



def prep_weights(inp):
    w = {}
    w["WqT"] = np.ascontiguousarray(inp["L0_attn_Wq"].T)
    w["WkT"] = np.ascontiguousarray(inp["L0_attn_Wk"].T)
    w["WvT"] = np.ascontiguousarray(inp["L0_attn_Wv"].T)
    w["WoT"] = np.ascontiguousarray(inp["L0_attn_Wo"].T)
    w["wiT"] = np.ascontiguousarray(inp["L0_ffn_wi"].T)
    w["woT"] = np.ascontiguousarray(inp["L0_ffn_wo"].T)
    w["bi"] = np.ascontiguousarray(inp["L0_ffn_bi"].reshape(32, 128))
    w["bo_row"] = np.ascontiguousarray(inp["L0_ffn_bo"].reshape(1, 1024))
    w["loraAT"] = np.ascontiguousarray(inp["L0_ffn_lora_A"].reshape(128, 1024).T)
    w["loraBT"] = np.ascontiguousarray(
        inp["L0_ffn_lora_B"].transpose(0, 2, 1).reshape(128, DFF))
    w["gate_wT"] = np.ascontiguousarray(inp["L0_ffn_gate_w"].T)
    w["gate_b_t"] = np.ascontiguousarray(np.tile(inp["L0_ffn_gate_b"].reshape(1, 8), (128, 1)))
    w["gate_b_t32"] = np.ascontiguousarray(np.tile(inp["L0_ffn_gate_b"].reshape(1, 8), (128, 32)))
    Wxp = inp["L1_attn_Wxproj"]
    w["WdtpT"] = np.ascontiguousarray(Wxp[:64].T)
    w["WBT"] = np.ascontiguousarray(Wxp[64:80].T)
    w["WCT"] = np.ascontiguousarray(Wxp[80:96].T)
    w["WxsT"] = np.ascontiguousarray(inp["L1_attn_Wxs"].T)
    w["WdtT"] = np.ascontiguousarray(inp["L1_attn_Wdt"].T)
    w["bdt"] = np.ascontiguousarray(inp["L1_attn_bdt"].reshape(1, 1))
    w["A_log"] = np.ascontiguousarray(inp["L1_attn_A_log"].reshape(1, 16))
    w["Dp_col"] = np.ascontiguousarray(inp["L1_attn_Dp"].reshape(8, 128))
    w["WoutT"] = np.ascontiguousarray(inp["L1_attn_Wout"].T)
    w["WQT"] = np.ascontiguousarray(inp["L1_ffn_W_Q"].T)
    w["WKT"] = np.ascontiguousarray(inp["L1_ffn_W_K"].T)
    w["WOT"] = np.ascontiguousarray(inp["L1_ffn_W_O"].T)
    w["WinvT"] = np.ascontiguousarray(inp["L1_ffn_W_inv"].T)
    w["Vf_T"] = np.ascontiguousarray(inp["L1_ffn_V_fwd"].transpose(1, 0, 2).reshape(D, 512))
    w["Wf_T"] = np.ascontiguousarray(inp["L1_ffn_W_fwd"].transpose(1, 0, 2).reshape(D, 512))
    w["Uf_flat"] = np.ascontiguousarray(inp["L1_ffn_U_fwd"].transpose(0, 2, 1).reshape(512, D))
    w["bf_k"] = np.ascontiguousarray(inp["L1_ffn_b_fwd"])
    w["Vi_T"] = np.ascontiguousarray(inp["L1_ffn_W_inv_exp"].transpose(1, 0, 2).reshape(D, 512))
    w["Wi_T"] = np.ascontiguousarray(inp["L1_ffn_V_inv"].transpose(1, 0, 2).reshape(D, 512))
    w["Ui_flat"] = np.ascontiguousarray(inp["L1_ffn_U_inv"].transpose(0, 2, 1).reshape(512, D))
    w["bi_k"] = np.ascontiguousarray(inp["L1_ffn_b_inv"])
    w["rw1T"] = np.ascontiguousarray(inp["L1_ffn_router_w1"].T)
    w["rb1"] = np.ascontiguousarray(inp["L1_ffn_router_b1"].reshape(8, 128))
    w["rw2T"] = np.ascontiguousarray(inp["L1_ffn_router_w2"].T)
    w["rb2"] = np.ascontiguousarray(inp["L1_ffn_router_b2"].reshape(1, 8))
    w["expb"] = np.ascontiguousarray(inp["L1_ffn_expert_bias"].reshape(1, 8))
    w["rb2_t"] = np.ascontiguousarray(np.tile(inp["L1_ffn_router_b2"].reshape(1, 8), (128, 1)))
    w["expb_t"] = np.ascontiguousarray(np.tile(inp["L1_ffn_expert_bias"].reshape(1, 8), (128, 1)))
    w["alpha"] = np.ascontiguousarray(np.asarray(inp["L1_ffn_alpha_bi"]).reshape(1, 1))
    return w


def host_constants():
    c = {}
    c["identity_bf"] = np.eye(128, dtype=BF)
    c["ones_col"] = np.ones((128, 1), np.float32)
    c["ones_col_bf"] = np.ones((128, 1), BF)
    c["ones_row"] = np.ones((1, 128), np.float32)
    c["inv_n"] = (1.0 / np.arange(1, T + 1, dtype=np.float64)).astype(np.float32).reshape(1, T)
    c["tie_eps"] = np.tile((np.arange(8, dtype=np.float32) * -1e-5).reshape(1, 8), (128, 1))
    c["tie32"] = np.tile((np.arange(8, dtype=np.float32) * -1e-5).reshape(1, 8), (128, 32))
    R = np.zeros((8, 128), np.float32)
    for e in range(8):
        R[e, e * 16:(e + 1) * 16] = 1.0
    c["R_lora"] = R
    Rk = np.zeros((8, 512), np.float32)
    for k in range(8):
        Rk[k, k * 64:(k + 1) * 64] = 1.0
    c["R_kr"] = Rk
    return c



LN_CH = 512


def make_ln_bufs(ctx, nc, name, with_default_fill=None):
    CH = LN_CH
    bufs = {"name": name}
    if with_default_fill is not None:
        bufs["rin"] = [ctx.enter_context(nc.sbuf_tensor(f"{name}_rin{i}", [128, 8 * CH], F32)) for i in range(2)]
        if with_default_fill != "none":
            bufs["ain"] = [ctx.enter_context(nc.sbuf_tensor(f"{name}_ain{i}", [128, 8 * CH], with_default_fill)) for i in range(2)]
    bufs["zb"] = [ctx.enter_context(nc.sbuf_tensor(f"{name}_z{i}", [128, 8 * CH], F32)) for i in range(2)]
    bufs["zbf"] = [ctx.enter_context(nc.sbuf_tensor(f"{name}_zbf{i}", [128, 8 * CH], BF16)) for i in range(2)]
    bufs["zsq"] = [ctx.enter_context(nc.sbuf_tensor(f"{name}_zsq{i}", [128, 8 * CH], BF16)) for i in range(2)]
    bufs["rows"] = [ctx.enter_context(nc.sbuf_tensor(f"{name}_rows{i}", [1, 5 * CH], F32)) for i in range(2)]
    bufs["xo"] = [ctx.enter_context(nc.sbuf_tensor(f"{name}_xo{i}", [128, 8 * CH], F32)) for i in range(2)]
    bufs["xob"] = [ctx.enter_context(nc.sbuf_tensor(f"{name}_xob{i}", [128, 8 * CH], BF16)) for i in range(1)]
    return bufs


def make_psum_pool(ctx, nc):
    ps = {}
    ps["ps_sz"] = ctx.enter_context(nc.psum_tensor("g_psz", [1, LN_CH], F32))
    ps["ps_sq"] = ctx.enter_context(nc.psum_tensor("g_psq", [1, LN_CH], F32))
    ps["psA"] = ctx.enter_context(nc.psum_tensor("g_psA", [128, LN_CH], F32))
    ps["psB"] = ctx.enter_context(nc.psum_tensor("g_psB", [128, LN_CH], F32))
    psP2 = ctx.enter_context(nc.psum_tensor("g_psP2", [128, 2 * LN_CH], F32))
    ps["psP2"] = psP2
    ps["psW"] = [psP2[:, 0:LN_CH], psP2[:, LN_CH:2 * LN_CH],
                 ctx.enter_context(nc.psum_tensor("g_psW2", [128, LN_CH], F32)),
                 ctx.enter_context(nc.psum_tensor("g_psW3", [128, LN_CH], F32))]
    return ps


def hbm_chunk_view(hbm, b, j, CH=LN_CH):
    return hbm[b].rearrange("(k p) t -> p k t", p=128)[:, :, j * CH:(j + 1) * CH]


def sb_chunk3(t):
    return t[:].rearrange("p (k t) -> p k t", k=8)


def emit_ln_stream(kb, bufs, consts_sb, nchunks, fill_z, out_view, name):
    CH = LN_CH
    zb = bufs["zb"]; zbf = bufs["zbf"]; zsq = bufs["zsq"]; rows = bufs["rows"]
    xo = bufs["xo"]; xob = bufs["xob"]
    ps = bufs["ps"]
    ps_sz = ps["ps_sz"]; ps_sq = ps["ps_sq"]; psA = ps["psA"]; psB = ps["psB"]
    s_out = kb.sem("LN_out")
    s_cg = kb.sem("LN_cg")
    s_cs = kb.sem("LN_cs")
    s_st = kb.sem("LN_st")
    s_rv = kb.sem("LN_rv")
    s_rs = kb.sem("LN_rs")
    s_bc = kb.sem("LN_bc")
    s_av = kb.sem("LN_av")
    s_ag = kb.sem("LN_ag")
    ones_bf = consts_sb["ones_col_bf"]
    ones_row = consts_sb["ones_row"]
    R = [dict() for _ in range(nchunks)]

    for j in range(nchunks):
        pj = j % 2
        r = R[j]
        wait_free = [(s_av, R[j - 2]["tt_done"])] if j >= 2 else []
        zsem, zval = fill_z(j, pj, wait_free)
        r["zready"] = zval

        wc = [(zsem, zval)]
        if j >= 2:
            wc = wc + [(s_st, R[j - 2]["stats"])]
        kb.op("gpsimd", lambda e, pj=pj: e.tensor_copy(out=zbf[pj][:], in_=zb[pj][:]),
              wait=wc, inc=[(s_cg, 1)])
        r["cg"] = s_cg.n
        kb.op("scalar", lambda e, pj=pj: e.activation(out=zsq[pj][:], in_=zb[pj][:], func=AF.Square),
              wait=wc, inc=[(s_cs, 1)])
        r["cs"] = s_cs.n

        wmm = [(s_cg, r["cg"]), (s_cs, r["cs"])]
        if j >= 1:
            wmm.append((s_rv, R[j - 1]["row1"]))
        for k in range(8):
            kb.op("tensor", lambda e, k=k, pj=pj: e.matmul(ps_sz[:], ones_bf[:],
                  zbf[pj][:, k * CH:(k + 1) * CH], start=(k == 0), stop=(k == 7)),
                  wait=(wmm if k == 0 else []))
        for k in range(8):
            kb.op("tensor", lambda e, k=k, pj=pj: e.matmul(ps_sq[:], ones_bf[:],
                  zsq[pj][:, k * CH:(k + 1) * CH], start=(k == 0), stop=(k == 7)),
                  inc=([(s_st, 1)] if k == 7 else []))
        r["stats"] = s_st.n

        mu = rows[pj][:, 0:CH]; var = rows[pj][:, CH:2 * CH]; a_r = rows[pj][:, 2 * CH:3 * CH]
        b_r = rows[pj][:, 3 * CH:4 * CH]; t0 = rows[pj][:, 4 * CH:5 * CH]
        wrow = [(s_st, r["stats"])]
        if j >= 2:
            wrow.append((s_bc, R[j - 2]["bc"]))

        def rowmath(e, mu=mu, var=var, t0=t0):
            e.tensor_scalar_mul(out=mu, in0=ps_sz[:], scalar1=1.0 / D)
            e.tensor_scalar_mul(out=var, in0=ps_sq[:], scalar1=1.0 / D)
            e.tensor_mul(out=t0, in0=mu, in1=mu)
            return e.tensor_sub(out=var, in0=var, in1=t0)
        kb.op("vector", rowmath, wait=wrow, inc=[(s_rv, 1)])
        r["row1"] = s_rv.n

        def rowexp(e, var=var, a_r=a_r, t0=t0):
            e.activation(out=t0, in_=var, func=AF.Ln, bias=consts_sb["eps11"][:, 0:1])
            return e.activation(out=a_r, in_=t0, func=AF.Exp, scale=-0.5)
        kb.op("scalar", rowexp, wait=[(s_rv, r["row1"])], inc=[(s_rs, 1)])
        r["row2"] = s_rs.n

        def rowb(e, b_r=b_r, mu=mu, a_r=a_r):
            e.tensor_mul(out=b_r, in0=mu, in1=a_r)
            return e.tensor_scalar_mul(out=b_r, in0=b_r, scalar1=-1.0)
        kb.op("vector", rowb, wait=[(s_rs, r["row2"])], inc=[(s_rv, 1)])
        r["row3"] = s_rv.n

        wbc = [(s_rv, r["row3"])]
        if j >= 1:
            wbc.append((s_av, R[j - 1]["tt_done"]))
        kb.op("tensor", lambda e, a_r=a_r: e.matmul(psA[:], ones_row[:], a_r, start=True, stop=True),
              wait=wbc)
        kb.op("tensor", lambda e, b_r=b_r: e.matmul(psB[:], ones_row[:], b_r, start=True, stop=True),
              inc=[(s_bc, 1)])
        r["bc"] = s_bc.n

        wap = [(s_bc, r["bc"])]
        if j >= 2:
            wap.append((s_out, R[j - 2]["outdma"]))
        for k in range(8):
            sl = slice(k * CH, (k + 1) * CH)
            kb.op("vector", lambda e, pj=pj, sl=sl: e.tensor_mul(out=xo[pj][:, sl], in0=zb[pj][:, sl], in1=psA[:]),
                  wait=(wap if k == 0 else []))
            kb.op("vector", lambda e, pj=pj, sl=sl: e.tensor_add(out=xo[pj][:, sl], in0=xo[pj][:, sl], in1=psB[:]),
                  inc=([(s_av, 1)] if k == 7 else []))
        r["tt_done"] = s_av.n
        wxob = [(s_av, r["tt_done"])]
        if j >= 1:
            wxob.append((s_out, R[j - 1]["outdma"]))
        kb.op("gpsimd", lambda e, pj=pj: e.tensor_copy(out=xob[0][:], in_=xo[pj][:]),
              wait=wxob, inc=[(s_ag, 1)])
        r["ag"] = s_ag.n

        of, ob = out_view(j)
        kb.dma("sync", of, sb_chunk3(xo[pj]), sem=s_out, wait=[(s_av, r["tt_done"])])
        _, vo = kb.dma("sync", ob, sb_chunk3(xob[0]), sem=s_out, new_group=False,
                       wait=[(s_ag, r["ag"])])
        r["outdma"] = vo
    return R


def emit_ln_phase(kb, ctx, nc, consts_sb, ps, res_hbm, addin_hbm, addin_dtype,
                  out_f32_hbm, out_bf_hbm, b_list, name, bufs=None):
    CH = LN_CH
    NCH = T // CH
    if bufs is None:
        bufs = make_ln_bufs(ctx, nc, name, with_default_fill=(addin_dtype if addin_hbm is not None else "none"))
        bufs["ps"] = ps
    rin = bufs["rin"]; ain = bufs.get("ain"); zb = bufs["zb"]
    s_in = [kb.sem("LN_in0"), kb.sem("LN_in1")]
    s_zc = kb.sem("LN_zc")
    nch_total = NCH * len(b_list)

    def fill_z(g, pj, wait_free):
        b = b_list[g // NCH]; j = g % NCH
        w = [(s_zc, fill_z.consumed[g - 2])] if g >= 2 else []
        sS = s_in[pj]
        _, v1 = kb.dma("sync", sb_chunk3(rin[pj]), hbm_chunk_view(res_hbm, b, j), sem=sS, wait=w)
        v2 = v1
        if ain is not None:
            _, v2 = kb.dma("sync", sb_chunk3(ain[pj]), hbm_chunk_view(addin_hbm, b, j), sem=sS, new_group=False)
        wz = [(sS, max(v1, v2))] + list(wait_free)
        if ain is not None:
            kb.op("vector", lambda e: e.tensor_add(out=zb[pj][:], in0=rin[pj][:], in1=ain[pj][:]),
                  wait=wz, inc=[(s_zc, 1)])
        else:
            kb.op("vector", lambda e: e.tensor_copy(out=zb[pj][:], in_=rin[pj][:]),
                  wait=wz, inc=[(s_zc, 1)])
        fill_z.consumed[g] = s_zc.n
        return (s_zc, s_zc.n)
    fill_z.consumed = [0] * nch_total

    def out_view(g):
        b = b_list[g // NCH]; j = g % NCH
        return (hbm_chunk_view(out_f32_hbm, b, j), hbm_chunk_view(out_bf_hbm, b, j))

    return emit_ln_stream(kb, bufs, consts_sb, nch_total, fill_z, out_view, name)



def emit_ssm_phase(kb, ctx, nc, consts_sb, ps, whbm, x2bf_hbm, x2f32_hbm,
                   out_f32_hbm, out_bf_hbm, name="ssm"):
    CH = LN_CH
    NCH = T // CH
    s = kb.sem(name + "_chain")
    s_xin = kb.sem(name + "_xin")
    s_rin = kb.sem(name + "_resin")
    s_gp = kb.sem("dma_gpsimd")
    psW = ps["psW"]

    Wdtp_bf = ctx.enter_context(nc.sbuf_tensor(name + "_wdtp", [128, 8 * 64], BF16))
    WB_bf = ctx.enter_context(nc.sbuf_tensor(name + "_wB", [128, 8 * 16], BF16))
    WC_bf = ctx.enter_context(nc.sbuf_tensor(name + "_wC", [128, 8 * 16], BF16))
    Wxs_bf = ctx.enter_context(nc.sbuf_tensor(name + "_wxs", [128, 8 * 16], BF16))
    WdtT_sb = ctx.enter_context(nc.sbuf_tensor(name + "_wdt", [64, 1], F32))
    sc = ctx.enter_context(nc.sbuf_tensor(name + "_sc", [1, 64], F32))
    Dp1 = ctx.enter_context(nc.sbuf_tensor(name + "_dp1", [128, 8], F32))
    WoutT_bf = ctx.enter_context(nc.sbuf_tensor(name + "_wout", [16, 1024], BF16))

    for wn, wsb, cc in (("WdtpT", Wdtp_bf, 64), ("WBT", WB_bf, 16), ("WCT", WC_bf, 16), ("WxsT", Wxs_bf, 16)):
        kb.dma("gpsimd", wsb[:].rearrange("p (k c) -> p k c", c=cc),
               whbm[wn].rearrange("(k p) c -> p k c", p=128))
    kb.dma("sync", WdtT_sb[:], whbm["WdtT"])
    kb.dma("sync", sc[0:1, 0:1], whbm["bdt"])
    kb.dma("sync", sc[0:1, 16:32], whbm["A_log"])
    kb.dma("sync", Dp1[:], whbm["Dp_col"].rearrange("k p -> p k"), allow_slow_non_contiguous=True)
    _, gv = kb.dma("gpsimd", WoutT_bf[:], whbm["WoutT"])
    ldv = kb.sem("dma_sync").n
    s_dma0 = kb.sem("dma_sync")
    kb.chain("vector", lambda e: e.tensor_scalar_mul(out=sc[0:1, 1:2], in0=sc[0:1, 0:1], scalar1=-1.0),
             s, extra_wait=[(s_dma0, ldv), (s_gp, gv)], drain=True)
    kb.chain("scalar", lambda e: e.activation(out=sc[0:1, 32:48], in_=sc[0:1, 16:32], func=AF.Exp), s, drain=True)
    kb.chain("vector", lambda e: e.tensor_scalar_add(out=Dp1[:], in0=Dp1[:], scalar1=1.0), s, drain=True)
    negbdt = sc[0:1, 1:2]
    E16 = sc[0:1, 32:48]

    x2bf = [ctx.enter_context(nc.sbuf_tensor(f"{name}_x2bf{i}", [128, 8 * CH], BF16)) for i in range(1)]
    dtp_sb = ctx.enter_context(nc.sbuf_tensor(name + "_dtp", [64, CH], F32))
    Bp = ctx.enter_context(nc.sbuf_tensor(name + "_Bp", [16, CH], F32))
    xsp = ctx.enter_context(nc.sbuf_tensor(name + "_xsp", [16, CH], F32))
    Cp = ctx.enter_context(nc.sbuf_tensor(name + "_Cp", [16, CH], F32))
    rowsS = ctx.enter_context(nc.sbuf_tensor(name + "_rows", [1, 2 * CH], F32))
    Ab = ctx.enter_context(nc.sbuf_tensor(name + "_Ab", [16, CH], F32))
    BxP = ctx.enter_context(nc.sbuf_tensor(name + "_Bx", [16, CH], F32))
    hs = ctx.enter_context(nc.sbuf_tensor(name + "_hs", [16, CH], F32))
    carry = ctx.enter_context(nc.sbuf_tensor(name + "_carry", [16, 4], F32))
    hcbf = ctx.enter_context(nc.sbuf_tensor(name + "_hc", [16, T], BF16))

    bufs = make_ln_bufs(ctx, nc, name + "_ln", with_default_fill="none")
    bufs["ps"] = ps
    rin = bufs["rin"]; zb = bufs["zb"]
    s_ymm = kb.sem(name + "_ymm")
    s_yst = kb.sem(name + "_yst")

    RR = []
    proj_done = {}
    for b in range(NB):
        for j in range(NCH):
            pj = 0
            g = b * NCH + j
            wx = [(s, proj_done[g - 1])] if g >= 1 else []
            _, vx = kb.dma("sync", x2bf[pj][:].rearrange("p (k t) -> p k t", k=8),
                           hbm_chunk_view(x2bf_hbm, b, j), sem=s_xin, wait=wx)
            fw = [(s_xin, vx)]
            s_pe = kb.sem(name + "_pe")
            s_pv = kb.sem(name + "_pev")
            for pi, (proj_sb, wbf, CC, pw) in enumerate(
                    ((dtp_sb, Wdtp_bf, 64, psW[0]), (Bp, WB_bf, 16, psW[1]),
                     (xsp, Wxs_bf, 16, psW[2]), (Cp, WC_bf, 16, psW[3]))):
                wmm = list(fw)
                fw = []
                if g >= 1:
                    wmm.append((s, s.n))
                for k in range(8):
                    kb.op("tensor", lambda e, pw=pw, k=k, wbf=wbf, CC=CC, pj=pj: e.matmul(
                        pw[0:CC, :], wbf[:, k * CC:(k + 1) * CC],
                        x2bf[pj][:, k * CH:(k + 1) * CH],
                        start=(k == 0), stop=(k == 7)),
                        wait=(wmm if k == 0 else []),
                        inc=([(s_pe, 1)] if k == 7 else []))
                vpe = s_pe.n
                kb.op("scalar", lambda e, pw=pw, proj_sb=proj_sb, CC=CC: e.copy(
                    out=proj_sb[:], in_=pw[0:CC, :]),
                    wait=[(s_pe, vpe)], inc=[(s_pv, 1)])
            kb.chain("vector", lambda e: e.engine_nop(), s, extra_wait=[(s_pv, s_pv.n)])
            proj_done[g] = s.n
            t1 = rowsS[:, 0:CH]; dtn = rowsS[:, CH:2 * CH]
            kb.chain("tensor", lambda e: e.matmul(psW[0][0:1, :], WdtT_sb[:], dtp_sb[:],
                                                  start=True, stop=True), s)
            kb.chain("scalar", lambda e: e.activation(out=t1, in_=psW[0][0:1, :], func=AF.Sigmoid,
                                                      scale=-1.0, bias=negbdt), s)
            kb.chain("scalar", lambda e: e.activation(out=dtn, in_=t1, func=AF.Ln), s)
            kb.chain("tensor", lambda e: e.matmul(psW[0][0:16, :], E16, dtn, start=True, stop=True), s)
            kb.chain("scalar", lambda e: e.activation(out=Ab[:], in_=psW[0][0:16, :], func=AF.Exp), s)
            kb.chain("vector", lambda e: e.tensor_scalar_mul(out=t1, in0=dtn, scalar1=-1.0), s)
            kb.chain("tensor", lambda e: e.matmul(psW[1][0:16, :], consts_sb["ones_row"][:, 0:16],
                                                  t1, start=True, stop=True), s)
            kb.chain("vector", lambda e: e.tensor_mul(out=BxP[:], in0=Bp[:], in1=xsp[:]), s)
            kb.chain("vector", lambda e: e.tensor_mul(out=BxP[:], in0=BxP[:], in1=psW[1][0:16, :]), s)
            if j == 0:
                kb.chain("vector", lambda e: e.tensor_tensor_scan(
                    out=hs[:], data0=Ab[:], data1=BxP[:],
                    initial=0.0, op0=ALU.mult, op1=ALU.add), s)
            else:
                kb.chain("vector", lambda e: e.tensor_tensor_scan(
                    out=hs[:], data0=Ab[:], data1=BxP[:],
                    initial=carry[:, 0:1], op0=ALU.mult, op1=ALU.add), s)
            kb.chain("vector", lambda e: e.tensor_copy(out=carry[:, 0:1], in_=hs[:, CH - 1:CH]),
                     s, drain=True)
            kb.chain("vector", lambda e: e.tensor_mul(
                out=Ab[:], in0=hs[:], in1=Cp[:]), s)
            kb.chain("scalar", lambda e, j=j: e.copy(out=hcbf[:, j * CH:(j + 1) * CH],
                                                     in_=Ab[:]), s)
        kb.barrier()

        def fill_z(j, pj, wait_free, b=b):
            w = [(s_yst, fill_z.consumed[j - 2])] if j >= 2 else []
            _, v1 = kb.dma("sync", sb_chunk3(rin[pj]), hbm_chunk_view(x2f32_hbm, b, j),
                           sem=s_rin, wait=w)
            for k in range(8):
                par = k % 4
                wmm = [(s_yst, fill_z.last_stt[par])] if fill_z.last_stt[par] else []
                kb.op("tensor", lambda e, k=k, j=j, par=par: e.matmul(
                    psW[par][:], WoutT_bf[:, k * 128:(k + 1) * 128],
                    hcbf[:, j * CH:(j + 1) * CH], start=True, stop=True),
                    wait=wmm, inc=[(s_ymm, 1)])
                vmm = s_ymm.n
                sl = slice(k * CH, (k + 1) * CH)
                wst = [(s_ymm, vmm), (s_rin, v1)] + (list(wait_free) if k == 0 else [])
                kb.op("vector", lambda e, pj=pj, sl=sl, k=k, par=par: e.scalar_tensor_tensor(
                    out=zb[pj][:, sl], in0=rin[pj][:, sl], scalar=Dp1[:, k:k + 1],
                    in1=psW[par][:], op0=ALU.mult, op1=ALU.add),
                    wait=wst, inc=[(s_yst, 1)])
                fill_z.last_stt[par] = s_yst.n
            fill_z.consumed[j] = s_yst.n
            return (s_yst, s_yst.n)
        fill_z.consumed = [0] * NCH
        fill_z.last_stt = {0: 0, 1: 0, 2: 0, 3: 0}

        def out_view(j, b=b):
            return (hbm_chunk_view(out_f32_hbm, b, j), hbm_chunk_view(out_bf_hbm, b, j))

        RR.append(emit_ln_stream(kb, bufs, consts_sb, NCH, fill_z, out_view, f"{name}_ln{b}"))
        kb.barrier()
    return RR


def sim_run(nc, in_map, out_names):
    from concourse.bass_interp import CoreSim
    nc.finalize()
    sim = CoreSim(nc, require_finite=False, require_nnan=False)
    for k, v in in_map.items():
        view = sim.tensor(k)
        view[:] = v
    sim.simulate()
    return {k: np.array(sim.tensor(k)) for k in out_names}



def make_matmul_fill(kb, ctx, nc, ps, name, WT_bf, act_hbm, res_hbm, bufs, b):
    CH = LN_CH
    psW = ps["psW"]
    rin = bufs["rin"]
    ain = bufs["ain"]
    zb = bufs["zb"]
    s_in = [kb.sem("FI_in0"), kb.sem("FI_in1")]
    s_mm = kb.sem("FI_mm")
    s_v = kb.sem("FI_v")

    def fill_z(j, pj, wait_free):
        w = [(s_v, fill_z.consumed[j - 2])] if j >= 2 else []
        sS = s_in[pj]
        _, v1 = kb.dma("sync", sb_chunk3(rin[pj]), hbm_chunk_view(res_hbm, b, j), sem=sS, wait=w)
        _, v2 = kb.dma("sync", sb_chunk3(ain[pj]), hbm_chunk_view(act_hbm, b, j), sem=sS, new_group=False)
        for m in range(8):
            par = m % 4
            wmm = [(sS, v2)] if m == 0 else []
            if fill_z.last_tt[par]:
                wmm.append((s_v, fill_z.last_tt[par]))
            for k in range(8):
                kb.op("tensor", lambda e, m=m, k=k, par=par, pj=pj: e.matmul(
                    psW[par][:], WT_bf[:, k * 1024 + m * 128:k * 1024 + (m + 1) * 128],
                    ain[pj][:, k * CH:(k + 1) * CH],
                    start=(k == 0), stop=(k == 7)),
                    wait=(wmm if k == 0 else []),
                    inc=([(s_mm, 1)] if k == 7 else []))
            vmm = s_mm.n
            sl = slice(m * CH, (m + 1) * CH)
            wst = [(s_mm, vmm)] + (list(wait_free) if m == 0 else [])
            kb.op("vector", lambda e, pj=pj, sl=sl, par=par: e.tensor_add(
                out=zb[pj][:, sl], in0=rin[pj][:, sl], in1=psW[par][:]),
                wait=wst, inc=[(s_v, 1)])
            fill_z.last_tt[par] = s_v.n
        fill_z.consumed[j] = s_v.n
        return (s_v, s_v.n)
    fill_z.consumed = [0] * (T // CH)
    fill_z.last_tt = {0: 0, 1: 0, 2: 0, 3: 0}
    return fill_z



def emit_attn_phase(kb, ctx, nc, consts_sb, ps, whbm, x_res_hbm, attnpre_hbm,
                    out_f32_hbm, out_bf_hbm, name="att"):
    CH = 512
    NCH = T // CH
    NTT = T // 128
    psW = ps["psW"]
    s_mm = kb.sem(name + "_mm")
    s_e = kb.sem(name + "_e")
    s_g = kb.sem(name + "_g")
    s_v = kb.sem(name + "_v")
    s_wq = kb.sem(name + "_wq")
    s_x = kb.sem(name + "_x")
    s_st = kb.sem(name + "_stdma")

    with ExitStack() as actx:
        x_bf = actx.enter_context(nc.sbuf_tensor(name + "_xbf", [128, 8 * T], BF16))
        V_aug = actx.enter_context(nc.sbuf_tensor(name + "_vaug", [128, NTT * NH * 65], BF16))
        WvT_bf = actx.enter_context(nc.sbuf_tensor(name + "_wv", [128, 8 * CH], BF16))
        wqk = actx.enter_context(nc.sbuf_tensor(name + "_wqk", [128, 2 * 8 * 64], BF16))
        Qa = actx.enter_context(nc.sbuf_tensor(name + "_qa", [66, T], BF16))
        Ka = actx.enter_context(nc.sbuf_tensor(name + "_ka", [66, T], BF16))
        qsq = actx.enter_context(nc.sbuf_tensor(name + "_qsq", [64, CH], BF16))
        Pt = actx.enter_context(nc.sbuf_tensor(name + "_pt", [128, NTT * CH], BF16))
        stage = [actx.enter_context(nc.sbuf_tensor(f"{name}_stage{i}", [64, CH], BF16)) for i in range(2)]
        arow = actx.enter_context(nc.sbuf_tensor(name + "_arow", [66, 16 + 2 * CH], F32))
        ones66 = actx.enter_context(nc.sbuf_tensor(name + "_o66", [66, 128], F32))
        recbc = actx.enter_context(nc.sbuf_tensor(name + "_recbc", [64, CH], F32))

        kn1 = arow[64:65, 0:8]
        mkcell = arow[64:65, 8:9]
        qnrow = arow[64:65, 16:16 + CH]
        rec = arow[64:65, 16 + CH:16 + 2 * CH]

        kb.op("vector", lambda e: e.memset(ones66[:], 1.0))
        kb.op("vector", lambda e: e.memset(Ka[64:65, :], 1.0))
        kb.op("vector", lambda e: e.memset(
            V_aug[:].rearrange("p (s c) -> p s c", c=65)[:, :, 64:65], 1.0), inc=[(s_v, 1)])

        last_STmm_h = 0
        last_PVmm = {}
        stage_dma = {0: 0, 1: 0}
        stage_tt_bank = {0: 0, 1: 0}
        rec_val = {0: 0, 1: 0}
        stage_tt = 0

        for b in range(NB):
            _, vx = kb.dma("gpsimd", x_bf[:].rearrange("p (k t) -> p k t", k=8),
                           hbm_chunk_bview(x_res_hbm, b), sem=s_x)
            vau_evict = [0] * (2 * NTT)
            for half in range(2):
                wload = [] if (b == 0 and half == 0) else [(s_mm, s_mm.n)]
                _, vwv = kb.dma("gpsimd", WvT_bf[:].rearrange("p (k c) -> p k c", c=CH),
                                whbm["WvT"].rearrange("(k p) c -> p k c", p=128)[:, :, half * CH:(half + 1) * CH],
                                sem=s_wq, wait=wload)
                for i in range(NTT):
                    u = half * NTT + i
                    par = 2 + (u % 2)
                    wv = [(s_x, vx), (s_wq, vwv)] if i == 0 else []
                    if i == 0 and half == 0:
                        wv = wv + [(s_v, s_v.n)]
                    if u >= 2:
                        wv = wv + [(s_e, vau_evict[u - 2])]
                    if b > 0 and u < 2:
                        wv = wv + [(s_v, stage_tt)]
                    for k in range(8):
                        kb.op("tensor", lambda e, i=i, k=k, par=par: e.matmul(
                            psW[par][:],
                            x_bf[:, k * T + i * 128:k * T + (i + 1) * 128],
                            WvT_bf[:, k * CH:(k + 1) * CH],
                            start=(k == 0), stop=(k == 7)),
                            wait=(wv if k == 0 else []),
                            inc=([(s_mm, 1)] if k == 7 else []))
                    vmi = s_mm.n
                    kb.op("scalar", lambda e, i=i, half=half, par=par: e.copy(
                        out=V_aug[:, i * NH * 65 + half * 8 * 65:i * NH * 65 + (half * 8 + 8) * 65].rearrange(
                            "p (h c) -> p h c", c=65)[:, :, 0:64],
                        in_=psW[par][:].rearrange("p (h c) -> p h c", c=64)),
                        wait=[(s_mm, vmi)], inc=[(s_e, 1)])
                    vau_evict[u] = s_e.n
            kb.barrier()
            for h in range(NH):
                _, vw = kb.dma("gpsimd", wqk[:, 0:512].rearrange("p (k c) -> p k c", c=64),
                               whbm["WqT"].rearrange("(k p) c -> p k c", p=128)[:, :, h * 64:(h + 1) * 64],
                               sem=s_wq)
                _, vw2 = kb.dma("gpsimd", wqk[:, 512:1024].rearrange("p (k c) -> p k c", c=64),
                                whbm["WkT"].rearrange("(k p) c -> p k c", p=128)[:, :, h * 64:(h + 1) * 64],
                                sem=s_wq, new_group=False)
                for (dst, off, scale) in ((Qa, 0, 0.125), (Ka, 512, 1.0)):
                    for j in range(NCH):
                        par = j % 2
                        wpr = []
                        if j == 0 and off == 0:
                            wpr = [(s_wq, vw2), (s_mm, last_STmm_h), (s_e, s_e.n)]
                        for k in range(8):
                            kb.op("tensor", lambda e, j=j, k=k, par=par, off=off: e.matmul(
                                psW[par][0:64, :], wqk[:, off + k * 64:off + (k + 1) * 64],
                                x_bf[:, k * T + j * CH:k * T + (j + 1) * CH],
                                start=(k == 0), stop=(k == 7)),
                                wait=(wpr if k == 0 else []),
                                inc=([(s_mm, 1)] if k == 7 else []))
                        vpj = s_mm.n
                        kb.op("scalar", lambda e, j=j, par=par, dst=dst, scale=scale: e.activation(
                            out=dst[0:64, j * CH:(j + 1) * CH], in_=psW[par][0:64, :],
                            func=AF.Identity, scale=scale),
                            wait=[(s_mm, vpj)], inc=[(s_e, 1)])
                vqk = s_e.n
                last_cons = ("v", s_v.n)
                prev_vns = 0
                for j in range(NCH):
                    wsq = [(s_e, vqk)] if j == 0 else [(s_mm, prev_vns)]
                    kb.op("scalar", lambda e, j=j: e.activation(
                        out=qsq[:], in_=Ka[0:64, j * CH:(j + 1) * CH], func=AF.Square),
                        wait=wsq, inc=[(s_e, 1)])
                    vsq = s_e.n
                    wnm = [(s_e, vsq), (s_v if last_cons[0] == "v" else s_e, last_cons[1])]
                    kb.op("tensor", lambda e: e.matmul(
                        psW[2][64:65, :], consts_sb["ones_col_bf"][0:64, :],
                        qsq[:], start=True, stop=True),
                        wait=wnm, inc=[(s_mm, 1)])
                    vns = s_mm.n
                    prev_vns = vns
                    kb.op("vector", lambda e, j=j: e.tensor_reduce(
                        out=kn1[:, j:j + 1], in_=psW[2][64:65, :], axis=AXX, op=ALU.max),
                        wait=[(s_mm, vns)], inc=[(s_v, 1)])
                    last_cons = ("v", s_v.n)
                kb.op("vector", lambda e: e.tensor_reduce(
                    out=mkcell, in_=kn1, axis=AXX, op=ALU.max), inc=[(s_v, 1)])
                vmk = s_v.n
                kb.op("scalar", lambda e: e.activation(out=mkcell, in_=mkcell, func=AF.Sqrt),
                      wait=[(s_v, vmk)], inc=[(s_e, 1)], drain=True)
                vmk2 = s_e.n
                for j in range(NCH):
                    kb.op("scalar", lambda e, j=j: e.activation(
                        out=qsq[:], in_=Qa[0:64, j * CH:(j + 1) * CH], func=AF.Square),
                        wait=([(s_v, s_v.n)] if j == 0 else []),
                        inc=[(s_e, 1)])
                    vsq = s_e.n
                    wnm = [(s_e, vsq), (s_v if last_cons[0] == "v" else s_e, last_cons[1])]
                    kb.op("tensor", lambda e: e.matmul(
                        psW[2][64:65, :], consts_sb["ones_col_bf"][0:64, :],
                        qsq[:], start=True, stop=True),
                        wait=wnm, inc=[(s_mm, 1)])
                    vns = s_mm.n
                    kb.op("scalar", lambda e: e.activation(
                        out=qnrow, in_=psW[2][64:65, :], func=AF.Sqrt),
                        wait=[(s_mm, vns)], inc=[(s_e, 1)])
                    vqn = s_e.n
                    kb.op("vector", lambda e, j=j: e.tensor_scalar(
                        out=Qa[64:65, j * CH:(j + 1) * CH], in0=qnrow, scalar1=mkcell, scalar2=-1.0,
                        op0=ALU.mult, op1=ALU.mult),
                        wait=[(s_e, vqn)], inc=[(s_v, 1)], drain=True)
                    last_cons = ("v", s_v.n)
                vaug = s_v.n

                psP2 = ps["psP2"]
                psA = ps["psA"]
                exp_pair_prev = 0
                pv_prev = []
                norm_pend = None

                def emit_pv(item):
                    tkt, wpv0, blk_, ntk_, pvbank_, first, last = item
                    wpv = list(wpv0) if first else []
                    if first and stage_tt_bank[blk_ % 2]:
                        wpv.append((s_v, stage_tt_bank[blk_ % 2]))
                    kb.op("tensor", lambda e, tkt=tkt, ntk_=ntk_, h=h, pvbank_=pvbank_: e.matmul(
                        pvbank_[0:65, :],
                        V_aug[:, tkt * NH * 65 + h * 65:tkt * NH * 65 + (h + 1) * 65],
                        Pt[:, tkt * CH:(tkt + 1) * CH],
                        start=(tkt == 0), stop=(tkt == ntk_ - 1)),
                        wait=wpv, inc=[(s_mm, 1)])
                    last_PVmm[tkt] = s_mm.n
                    if last:
                        kb.op("vector", lambda e, pvbank_=pvbank_: e.reciprocal(
                            out=rec, in_=pvbank_[64:65, :]),
                            wait=[(s_mm, s_mm.n)], inc=[(s_v, 1)], drain=True)
                        rec_val[blk_ % 2] = s_v.n

                def emit_norm(nblk, nbank):
                    kb.op("tensor", lambda e: e.matmul(
                        psA[0:64, :], ones66[64:65, 0:64], rec, start=True, stop=True),
                        wait=[(s_v, rec_val[nblk % 2])], inc=[(s_mm, 1)])
                    vbc = s_mm.n
                    kb.op("vector", lambda e: e.tensor_copy(out=recbc[:], in_=psA[0:64, :]),
                          wait=[(s_mm, vbc)], inc=[(s_v, 1)])
                    vrecbc = s_v.n
                    wstg = [(s_v, vrecbc)]
                    if stage_dma[nblk % 2]:
                        wstg.append((s_st, stage_dma[nblk % 2]))
                    kb.op("vector", lambda e, nbank=nbank: e.tensor_mul(
                        out=stage[nblk % 2][:], in0=nbank[0:64, :], in1=recbc[:]),
                        wait=wstg, inc=[(s_v, 1)])
                    stage_tt_bank[nblk % 2] = s_v.n
                    _, sdv = kb.dma("sync", attnpre_hbm[b][h * 64:(h + 1) * 64, nblk * CH:(nblk + 1) * CH],
                                    stage[nblk % 2][:], sem=s_st, wait=[(s_v, stage_tt_bank[nblk % 2])])
                    stage_dma[nblk % 2] = sdv

                for blk in range(8):
                    ntk = 4 * (blk + 1)
                    pvbank = psW[2] if blk % 2 == 0 else psW[3]
                    pvw = []
                    exp_done = [0] * NTT
                    for p in range(ntk // 2):
                        for _ in range(2):
                            if pv_prev:
                                emit_pv(pv_prev.pop(0))
                        tA, tB = 2 * p, 2 * p + 1
                        wst = [(s_v, vaug)] if (blk == 0 and p == 0) else []
                        if p >= 1:
                            wst.append((s_e, exp_done[2 * p - 2]))
                        elif blk > 0:
                            wst.append((s_e, exp_pair_prev))
                        kb.op("tensor", lambda e, tA=tA, blk=blk: e.matmul(
                            psP2[:, 0:CH], Ka[0:65, tA * 128:(tA + 1) * 128],
                            Qa[0:65, blk * CH:(blk + 1) * CH], start=True, stop=True),
                            wait=wst)
                        kb.op("tensor", lambda e, tB=tB, blk=blk: e.matmul(
                            psP2[:, CH:2 * CH], Ka[0:65, tB * 128:(tB + 1) * 128],
                            Qa[0:65, blk * CH:(blk + 1) * CH], start=True, stop=True),
                            inc=[(s_mm, 1)])
                        vstm = s_mm.n
                        wexp = [(s_mm, vstm)]
                        for tx in (tA, tB):
                            if last_PVmm.get(tx):
                                wexp.append((s_mm, last_PVmm[tx]))
                        kb.op("scalar", lambda e, tA=tA: e.activation(
                            out=Pt[:, tA * CH:(tA + 2) * CH], in_=psP2[:, 0:2 * CH], func=AF.Exp),
                            wait=wexp, inc=[(s_e, 1)])
                        exp_done[tA] = exp_done[tB] = s_e.n
                        for tx in (tA, tB):
                            if tx * 128 >= blk * CH:
                                def mask(e, tx=tx, q0=blk * CH):
                                    i0 = tx * CH
                                    lo = tx * 128 - q0
                                    if lo > 0:
                                        e.memset(Pt[:, i0:i0 + lo], 0.0)
                                    return e.affine_select(
                                        out=Pt[:, i0 + lo:i0 + lo + 128], in_=Pt[:, i0 + lo:i0 + lo + 128],
                                        pattern=[[1, 128]], compare_op=ALU.is_ge, fill=0.0,
                                        base=0, channel_multiplier=-1)
                                kb.op("gpsimd", mask, wait=[(s_e, exp_done[tx])], inc=[(s_g, 1)])
                                pvw.append((tx, None, blk, ntk, pvbank, False, tx == ntk - 1))
                            else:
                                pvw.append((tx, None, blk, ntk, pvbank, False, tx == ntk - 1))
                    exp_pair_prev = exp_done[ntk - 1]
                    blkwait = [(s_e, exp_done[ntk - 1]), (s_g, s_g.n)]
                    pvw[0] = (pvw[0][0], blkwait, blk, ntk, pvbank, True, pvw[0][6])
                    if blk == 7:
                        last_STmm_h = s_mm.n
                    while pv_prev:
                        emit_pv(pv_prev.pop(0))
                    if norm_pend is not None:
                        emit_norm(*norm_pend)
                    pv_prev = pvw
                    norm_pend = (blk, pvbank)
                while pv_prev:
                    emit_pv(pv_prev.pop(0))
                emit_norm(*norm_pend)
                stage_tt = stage_tt_bank[1]
            kb.barrier()
        kb.barrier()

    WoT_bf = ctx.enter_context(nc.sbuf_tensor(name + "_wo", [128, 8 * 1024], BF16))
    _, vwo = kb.dma("gpsimd", WoT_bf[:].rearrange("p (k c) -> p k c", c=1024),
                    whbm["WoT"].rearrange("(k p) c -> p k c", p=128))
    bufs = make_ln_bufs(ctx, nc, name + "_ln", with_default_fill=BF16)
    bufs["ps"] = ps
    kb.wait("tensor", kb.sem("dma_gpsimd"), vwo)
    kb.barrier()
    RR = []
    for b in range(NB):
        fill = make_matmul_fill(kb, ctx, nc, ps, f"{name}_f{b}", WoT_bf, attnpre_hbm,
                                x_res_hbm, bufs, b)

        def out_view(j, b=b):
            return (hbm_chunk_view(out_f32_hbm, b, j), hbm_chunk_view(out_bf_hbm, b, j))

        RR.append(emit_ln_stream(kb, bufs, consts_sb, NCH, fill, out_view, f"{name}_ln{b}"))
        kb.barrier()
    return RR


def hbm_chunk_bview(hbm, b):
    return hbm[b].rearrange("(k p) t -> p k t", p=128)



def emit_ffn0_phase(kb, ctx, nc, consts_sb, ps, whbm, x1bf_hbm, x1f_hbm,
                    hsum_hbm, ffnout_hbm, out_f32_hbm, out_bf_hbm, name="ffn0", cut=None):
    CH = 512
    NCH = T // CH
    NT = T // 128
    psW = ps["psW"]
    s_mm = kb.sem(name + "_mm")
    s_e = kb.sem(name + "_e")
    s_v = kb.sem(name + "_v")
    s_c = kb.sem(name + "_c")
    s_wi = kb.sem(name + "_wi")
    s_hs = kb.sem(name + "_hs")
    s_ho = kb.sem(name + "_ho")
    s_fo = kb.sem(name + "_fo")
    s_xb = kb.sem(name + "_xb")

    val_rows = ctx.enter_context(nc.sbuf_tensor(name + "_valr", [1, 2 * T], F32))

    def _subA(b, nb_):
        with ExitStack() as actx:
            x1_bf = actx.enter_context(nc.sbuf_tensor(nb_ + "_x1bf", [128, 8 * T], BF16))
            gwt = actx.enter_context(nc.sbuf_tensor(nb_ + "_gwt", [128, 8 * 8], BF16))
            gbt = actx.enter_context(nc.sbuf_tensor(nb_ + "_gbt", [128, 256], F32))
            tie = actx.enter_context(nc.sbuf_tensor(nb_ + "_tie", [128, 256], F32))
            laT = actx.enter_context(nc.sbuf_tensor(nb_ + "_laT", [128, 8 * 128], BF16))
            lbT = actx.enter_context(nc.sbuf_tensor(nb_ + "_lbT", [128, DFF], BF16))
            lbneg = actx.enter_context(nc.sbuf_tensor(nb_ + "_lbn", [128, 128], BF16))
            Rl = actx.enter_context(nc.sbuf_tensor(nb_ + "_Rl", [8, 128], BF16))
            bi_sb = actx.enter_context(nc.sbuf_tensor(nb_ + "_bi", [128, 32], F32))
            garr = actx.enter_context(nc.sbuf_tensor(nb_ + "_gr", [128, 1440], F32))
            indc = actx.enter_context(nc.sbuf_tensor(nb_ + "_indc", [128, 2 * NT * 8], BF16))
            valc = actx.enter_context(nc.sbuf_tensor(nb_ + "_valc", [128, NT], F32))
            ind1T = actx.enter_context(nc.sbuf_tensor(nb_ + "_i1T", [8, T], BF16))
            ind2T = actx.enter_context(nc.sbuf_tensor(nb_ + "_i2T", [8, T], BF16))
            vct = actx.enter_context(nc.sbuf_tensor(nb_ + "_vct", [32, 128], F32))
            z1 = actx.enter_context(nc.sbuf_tensor(nb_ + "_z1", [128, T], BF16))
            z2 = actx.enter_context(nc.sbuf_tensor(nb_ + "_z2", [128, T], BF16))
            iexp = actx.enter_context(nc.sbuf_tensor(nb_ + "_iexp", [128, CH], BF16))
            valbc = actx.enter_context(nc.sbuf_tensor(nb_ + "_vbc", [128, 8 * CH], F32))
            wic = [actx.enter_context(nc.sbuf_tensor(f"{nb_}_wic{i}", [128, 8 * 128], BF16)) for i in range(2)]
            hst = [actx.enter_context(nc.sbuf_tensor(f"{nb_}_hst{i}", [128, 2 * CH], BF16)) for i in range(4)]
            h2p = [actx.enter_context(nc.sbuf_tensor(f"{nb_}_h2p{i}", [128, CH], F32)) for i in range(2)]
            ident = actx.enter_context(nc.sbuf_tensor(nb_ + "_id", [128, 128], BF16))
            identf = actx.enter_context(nc.sbuf_tensor(nb_ + "_idf", [128, 128], F32))

            kb.dma("sync", gbt[:], whbm["gate_b_t32"])
            kb.dma("sync", tie[:], consts_sb["tie32_hbm"], new_group=False)
            kb.dma("sync", Rl[:], consts_sb["Rl_hbm"], new_group=False)
            kb.dma("sync", bi_sb[:], whbm["bi"].rearrange("m p -> p m"),
                   new_group=False, allow_slow_non_contiguous=True)
            kb.dma("sync", ident[:], consts_sb["ident_hbm"], new_group=False)
            _, vld = kb.dma("sync", identf[:], consts_sb["identf_hbm"], new_group=False)
            kb.dma("gpsimd", laT[:].rearrange("p (k c) -> p k c", c=128),
                   whbm["loraAT"].rearrange("(k p) c -> p k c", p=128))
            kb.dma("gpsimd", lbT[:], whbm["loraBT"], new_group=False)
            _, vld2 = kb.dma("gpsimd", gwt[:].rearrange("p (k c) -> p k c", c=8),
                             whbm["gate_wT"].rearrange("(k p) c -> p k c", p=128), new_group=False)
            _, vxb = kb.dma("gpsimd", x1_bf[:].rearrange("p (k t) -> p k t", k=8),
                            hbm_chunk_bview(x1bf_hbm, b), sem=s_xb)
            kb.wait("tensor", kb.sem("dma_sync"), vld)
            kb.wait("tensor", kb.sem("dma_gpsimd"), vld2)
            kb.wait("tensor", s_xb, vxb)
            kb.barrier()

            kb.chain("vector", lambda e: e.memset(garr[:], 0.0), s_c)
            lgA = garr[:, 0:256]; lselA = garr[:, 256:512]; maskedA = garr[:, 512:768]
            emA = garr[:, 768:1024]; subT = garr[:, 1024:1280]
            m1nA = garr[:, 1280:1312]; mm1A = garr[:, 1312:1344]; m2A = garr[:, 1344:1376]
            esA = garr[:, 1376:1408]; rrA = garr[:, 1408:1440]
            vstart = s_c.n
            for i in range(NT):
                for k in range(8):
                    first = (i == 0 and k == 0)
                    last = (i == NT - 1 and k == 7)
                    kb.op("tensor", lambda e, i=i, k=k, first=first, last=last: e.matmul(
                        psW[0][:, i * 8:(i + 1) * 8], x1_bf[:, k * T + i * 128:k * T + (i + 1) * 128],
                        gwt[:, k * 8:(k + 1) * 8], start=first, stop=last),
                        wait=([(s_c, vstart)] if first else []),
                        inc=([(s_c, 1)] if last else []))
            def gate_v1(e):
                e.tensor_add(out=lgA, in0=psW[0][:, 0:256], in1=gbt[:])
                e.tensor_add(out=lselA, in0=lgA, in1=tie[:])
                e.drain()
                e.tensor_reduce(out=m1nA, in_=lselA.rearrange("p (i e) -> p i e", e=8),
                                axis=AXX, op=ALU.max, negate=True)
                e.drain()
                e.tensor_scalar_mul(out=mm1A, in0=m1nA, scalar1=-1.0)
                e.drain()
                mmb = mm1A.rearrange("p (i o) -> p i o", o=1).broadcast_to([128, NT, 8])
                e.tensor_tensor(out=indc[:, 0:256], in0=lselA.rearrange("p (i e) -> p i e", e=8),
                                in1=mmb, op=ALU.is_ge)
                e.tensor_tensor(out=subT.rearrange("p (i e) -> p i e", e=8),
                                in0=lselA.rearrange("p (i e) -> p i e", e=8), in1=mmb, op=ALU.is_lt)
                e.drain()
                e.tensor_scalar(out=subT, in0=subT, scalar1=1e30, scalar2=-1e30,
                                op0=ALU.mult, op1=ALU.add)
                e.drain()
                e.tensor_add(out=maskedA, in0=lselA, in1=subT)
                e.drain()
                e.tensor_reduce(out=m2A, in_=maskedA.rearrange("p (i e) -> p i e", e=8),
                                axis=AXX, op=ALU.max)
                e.drain()
                e.tensor_tensor(out=indc[:, 256:512], in0=maskedA.rearrange("p (i e) -> p i e", e=8),
                                in1=m2A.rearrange("p (i o) -> p i o", o=1).broadcast_to([128, NT, 8]),
                                op=ALU.is_ge)
                return e.tensor_sub(out=subT.rearrange("p (i e) -> p i e", e=8),
                                    in0=lgA.rearrange("p (i e) -> p i e", e=8),
                                    in1=mm1A.rearrange("p (i o) -> p i o", o=1).broadcast_to([128, NT, 8]))
            kb.chain("vector", gate_v1, s_c, drain=True)
            kb.chain("scalar", lambda e: e.activation(out=emA, in_=subT, func=AF.Exp), s_c, drain=True)
            def gate_v2(e):
                e.tensor_reduce(out=esA, in_=emA.rearrange("p (i e) -> p i e", e=8),
                                axis=AXX, op=ALU.add)
                e.drain()
                e.reciprocal(out=rrA, in_=esA)
                e.tensor_add(out=subT, in0=indc[:, 0:256], in1=indc[:, 256:512])
                e.drain()
                e.tensor_mul(out=subT, in0=emA, in1=subT)
                e.drain()
                e.tensor_reduce(out=valc[:], in_=subT.rearrange("p (i e) -> p i e", e=8),
                                axis=AXX, op=ALU.add)
                e.drain()
                return e.tensor_mul(out=valc[:], in0=valc[:], in1=rrA)
            kb.chain("vector", gate_v2, s_c, drain=True)
            for i in range(NT):
                kb.chain("tensor", lambda e, i=i: e.transpose(
                    psW[0][0:8, 0:128].bitcast(BF16)[:, 0:128],
                    indc[:, i * 8:(i + 1) * 8], ident[:]), s_c)
                kb.chain("vector", lambda e, i=i: e.tensor_copy(
                    out=ind1T[:, i * 128:(i + 1) * 128], in_=psW[0][0:8, 0:128].bitcast(BF16)[:, 0:128]), s_c)
                kb.chain("tensor", lambda e, i=i: e.transpose(
                    psW[0][0:8, 0:128].bitcast(BF16)[:, 0:128],
                    indc[:, NT * 8 + i * 8:NT * 8 + (i + 1) * 8], ident[:]), s_c)
                kb.chain("vector", lambda e, i=i: e.tensor_copy(
                    out=ind2T[:, i * 128:(i + 1) * 128], in_=psW[0][0:8, 0:128].bitcast(BF16)[:, 0:128]), s_c)
            kb.chain("tensor", lambda e: e.transpose(psW[0][0:32, 0:128], valc[:], identf[:]), s_c)
            kb.chain("vector", lambda e: e.tensor_copy(out=vct[:], in_=psW[0][0:32, 0:128]), s_c)
            vchain = s_c.n
            s_vr = kb.sem(name + "_vr")
            _, vvr = kb.dma("sync", val_rows[0:1, b * T:(b + 1) * T], vct[:], sem=s_vr,
                            wait=[(s_c, vchain)])
            for j in range(NCH):
                kb.chain("tensor", lambda e, j=j, b=b: e.matmul(
                    psW[0][:, 0:CH], consts_sb["ones_row"][:],
                    val_rows[0:1, b * T + j * CH:b * T + (j + 1) * CH], start=True, stop=True), s_c,
                    extra_wait=([(s_vr, vvr)] if j == 0 else []))
                kb.chain("vector", lambda e, j=j: e.tensor_copy(out=valbc[:, j * CH:(j + 1) * CH],
                                                                in_=psW[0][:, 0:CH]), s_c)
            if cut == "gate":
                if "dbg_garr" in whbm:
                    kb.dma("sync", whbm["dbg_garr"][:, 0:1440], garr[:, :], wait=[(s_c, s_c.n)])
                    kb.dma("gpsimd", whbm["dbg_indc"], indc[:, :], wait=[(s_c, s_c.n)])
                kb.barrier()
                return
            for j in range(NCH):
                for k in range(8):
                    kb.chain("tensor", lambda e, j=j, k=k: e.matmul(
                        psW[0][:, 0:CH], laT[:, k * 128:(k + 1) * 128],
                        x1_bf[:, k * T + j * CH:k * T + (j + 1) * CH],
                        start=(k == 0), stop=(k == 7)), s_c)
                kb.chain("scalar", lambda e, j=j: e.copy(out=z1[:, j * CH:(j + 1) * CH],
                                                         in_=psW[0][:, 0:CH]), s_c)
                kb.chain("tensor", lambda e, j=j: e.matmul(
                    psW[0][:, 0:CH], Rl[:], ind2T[:, j * CH:(j + 1) * CH], start=True, stop=True), s_c)
                kb.chain("vector", lambda e, j=j: e.tensor_copy(out=iexp[:], in_=psW[0][:, 0:CH]), s_c)
                kb.chain("vector", lambda e, j=j: e.tensor_mul(
                    out=z2[:, j * CH:(j + 1) * CH], in0=z1[:, j * CH:(j + 1) * CH], in1=iexp[:]), s_c)
                kb.chain("tensor", lambda e, j=j: e.matmul(
                    psW[0][:, 0:CH], Rl[:], ind1T[:, j * CH:(j + 1) * CH], start=True, stop=True), s_c)
                kb.chain("vector", lambda e, j=j: e.tensor_copy(out=iexp[:], in_=psW[0][:, 0:CH]), s_c)
                kb.chain("vector", lambda e, j=j: e.tensor_mul(
                    out=z1[:, j * CH:(j + 1) * CH], in0=z1[:, j * CH:(j + 1) * CH], in1=iexp[:]), s_c)
            kb.barrier()
            if cut == "z":
                return

            relu_done = {2: 0, 3: 0}
            uB_done = {0: 0, 1: 0}
            h2p_free = [0, 0]
            hst_free = [0, 0, 0, 0]
            wiv = [0, 0]
            for m in range(32):
                pw = m % 2
                wwi = [(s_mm, wiv[pw])] if wiv[pw] else []
                _, vwi = kb.dma("gpsimd", wic[pw][:].rearrange("p (k c) -> p k c", c=128),
                                whbm["wiT"].rearrange("(k p) c -> p k c", p=128)[:, :, m * 128:(m + 1) * 128],
                                sem=s_wi, wait=wwi)
                kb.op("vector", lambda e, pw=pw, m=m: e.tensor_scalar_mul(
                    out=lbneg[:], in0=lbT[:, m * 128:(m + 1) * 128], scalar1=-1.0),
                    wait=[(s_v, s_v.n), (s_mm, s_mm.n)], inc=[(s_v, 1)], drain=True)
                vneg = s_v.n
                for j in range(NCH):
                    par = 2 + (j % 2)
                    parB = j % 2
                    hb = (2 * m + j) % 4
                    wsh = [(s_wi, vwi), (s_v, vneg)] if j == 0 else []
                    if relu_done[par]:
                        wsh.append((s_e, relu_done[par]))
                    for k in range(8):
                        kb.op("tensor", lambda e, j=j, k=k, par=par, pw=pw: e.matmul(
                            psW[par][:], wic[pw][:, k * 128:(k + 1) * 128],
                            x1_bf[:, k * T + j * CH:k * T + (j + 1) * CH],
                            start=(k == 0), stop=False),
                            wait=(wsh if k == 0 else []))
                    kb.op("tensor", lambda e, j=j, par=par, m=m: e.matmul(
                        psW[par][:], lbT[:, m * 128:(m + 1) * 128],
                        z1[:, j * CH:(j + 1) * CH], start=False, stop=True),
                        inc=[(s_mm, 1)])
                    v1 = s_mm.n
                    wB = [(s_v, uB_done[parB])] if uB_done[parB] else []
                    kb.op("tensor", lambda e, j=j, parB=parB, m=m: e.matmul(
                        psW[parB][:], lbT[:, m * 128:(m + 1) * 128],
                        z2[:, j * CH:(j + 1) * CH], start=True, stop=False),
                        wait=wB)
                    kb.op("tensor", lambda e, j=j, parB=parB, m=m: e.matmul(
                        psW[parB][:], lbneg[:], z1[:, j * CH:(j + 1) * CH],
                        start=False, stop=True), inc=[(s_mm, 1)])
                    v2 = s_mm.n
                    whst = [(s_mm, v1)]
                    if hst_free[hb]:
                        whst.append((s_hs, hst_free[hb]))
                    kb.op("scalar", lambda e, par=par, m=m, hb=hb: e.activation(
                        out=hst[hb][:, 0:CH], in_=psW[par][:], func=AF.Relu,
                        bias=bi_sb[:, m:m + 1]),
                        wait=whst, inc=[(s_e, 1)])
                    ve1 = s_e.n
                    kb.op("scalar", lambda e, parB=parB, hb=hb: e.copy(
                        out=hst[hb][:, CH:2 * CH], in_=psW[parB][:]),
                        wait=[(s_mm, v2)], inc=[(s_e, 1)])
                    veB = s_e.n
                    kb.op("vector", lambda e, par=par, hb=hb: e.tensor_add(
                        out=h2p[hb % 2][:], in0=psW[par][:], in1=hst[hb][:, CH:2 * CH]),
                        wait=[(s_e, veB), (s_v, h2p_free[hb % 2])], inc=[(s_v, 1)])
                    vh2p = s_v.n
                    uB_done[parB] = vh2p
                    kb.op("scalar", lambda e, m=m, hb=hb: e.activation(
                        out=hst[hb][:, CH:2 * CH], in_=h2p[hb % 2][:], func=AF.Relu,
                        bias=bi_sb[:, m:m + 1]),
                        wait=[(s_v, vh2p)], inc=[(s_e, 1)])
                    ve2 = s_e.n
                    relu_done[par] = ve2
                    def hsum_tt(e, hb=hb, j=j):
                        e.tensor_add(out=hst[hb][:, 0:CH], in0=hst[hb][:, 0:CH], in1=hst[hb][:, CH:2 * CH])
                        return e.tensor_mul(out=hst[hb][:, 0:CH], in0=hst[hb][:, 0:CH],
                                            in1=valbc[:, j * CH:(j + 1) * CH])
                    kb.op("vector", hsum_tt, wait=[(s_e, ve2)], inc=[(s_v, 1)])
                    vtt = s_v.n
                    h2p_free[hb % 2] = vtt
                    _, vhs = kb.dma("sync", hsum_hbm[b][m * 128:(m + 1) * 128, j * CH:(j + 1) * CH],
                                    hst[hb][:, 0:CH], sem=s_hs, wait=[(s_v, vtt)])
                    hst_free[hb] = vhs
                wiv[pw] = s_mm.n
            kb.barrier()


    def _subB(b, nb_):
        with ExitStack() as bctx:
            woT_bf = bctx.enter_context(nc.sbuf_tensor(nb_ + "_wo", [128, 32 * 1024], BF16))
            hin = [bctx.enter_context(nc.sbuf_tensor(f"{nb_}_hin{i}", [128, 32 * CH], BF16)) for i in range(2)]
            bo_sb = bctx.enter_context(nc.sbuf_tensor(nb_ + "_bo", [1, 1024], F32))
            fst = [bctx.enter_context(nc.sbuf_tensor(f"{nb_}_fst{i}", [128, CH], BF16)) for i in range(4)]
            _, vwo = kb.dma("gpsimd", woT_bf[:].rearrange("p (k c) -> p k c", c=1024),
                            whbm["woT"].rearrange("(k p) c -> p k c", p=128))
            _, vbo = kb.dma("sync", bo_sb[:], whbm["bo_row"])
            kb.op("vector", lambda e, b=b: e.tensor_scalar_mul(
                out=val_rows[0:1, b * T:(b + 1) * T], in0=val_rows[0:1, b * T:(b + 1) * T],
                scalar1=2.0), wait=[(kb.sem("dma_sync"), kb.sem("dma_sync").n)], inc=[(s_v, 1)])
            vv2 = s_v.n
            kb.wait("tensor", kb.sem("dma_gpsimd"), vwo)
            kb.wait("tensor", kb.sem("dma_sync"), vbo)
            kb.wait("tensor", s_v, vv2)
            kb.barrier()
            evict_done = {0: 0, 1: 0, 2: 0, 3: 0}
            fst_free = [0, 0, 0, 0]
            hin_free = [0, 0]
            for j in range(NCH):
                pj = j % 2
                whi = [(s_mm, hin_free[pj])] if hin_free[pj] else []
                _, vhi = kb.dma("sync", hin[pj][:].rearrange("p (k t) -> p k t", k=32),
                                hsum_hbm[b].rearrange("(k p) t -> p k t", p=128)[:, :, j * CH:(j + 1) * CH],
                                sem=s_ho, wait=whi)
                for mo in range(8):
                    par = mo % 4
                    fb = (8 * j + mo) % 4
                    wmm = [(s_ho, vhi)] if mo == 0 else []
                    if evict_done[par]:
                        wmm.append((s_e, evict_done[par]))
                    for k in range(32):
                        kb.op("tensor", lambda e, mo=mo, k=k, par=par, pj=pj: e.matmul(
                            psW[par][:], woT_bf[:, k * 1024 + mo * 128:k * 1024 + (mo + 1) * 128],
                            hin[pj][:, k * CH:(k + 1) * CH],
                            start=(k == 0), stop=False),
                            wait=(wmm if k == 0 else []))
                    kb.op("tensor", lambda e, mo=mo, j=j, b=b, par=par: e.matmul(
                        psW[par][:], bo_sb[0:1, mo * 128:(mo + 1) * 128],
                        val_rows[0:1, b * T + j * CH:b * T + (j + 1) * CH],
                        start=False, stop=True), inc=[(s_mm, 1)])
                    vmm = s_mm.n
                    wev = [(s_mm, vmm)]
                    if fst_free[fb]:
                        wev.append((s_fo, fst_free[fb]))
                    kb.op("scalar", lambda e, par=par, fb=fb: e.copy(
                        out=fst[fb][:], in_=psW[par][:]),
                        wait=wev, inc=[(s_e, 1)])
                    evict_done[par] = s_e.n
                    _, vfo = kb.dma("sync", ffnout_hbm[b][mo * 128:(mo + 1) * 128, j * CH:(j + 1) * CH],
                                    fst[fb][:], sem=s_fo, wait=[(s_e, evict_done[par])])
                    fst_free[fb] = vfo
                hin_free[pj] = s_mm.n
            kb.barrier()


    for b in range(NB):
        _subA(b, f"{name}{b}")
        if cut is None:
            _subB(b, f"{name}{b}")

    if "dbg_val" in whbm:
        kb.dma("sync", whbm["dbg_val"], val_rows[0:1, :])
        kb.barrier()

    if cut is not None:
        return None
    return emit_ln_phase(kb, ctx, nc, consts_sb, ps, x1f_hbm, ffnout_hbm, BF16,
                         out_f32_hbm, out_bf_hbm, list(range(NB)), name + "_ln")



def emit_ffn1_phase(kb, ctx, nc, consts_sb, ps, whbm, x3bf_hbm, x3f_hbm,
                    qk_hbm, g_hbm, out_f32_hbm, out_bf_hbm, name="ffn1"):
    CH = 512
    NCH = T // CH
    NT = T // 128
    psW = ps["psW"]
    s_mm = kb.sem(name + "_mm")
    s_e = kb.sem(name + "_e")
    s_v = kb.sem(name + "_v")
    s_c = kb.sem(name + "_c")
    s_ld = kb.sem(name + "_ld")
    s_qs = kb.sem(name + "_qs")
    s_ks = kb.sem(name + "_ks")
    s_go = kb.sem(name + "_go")
    s_gi = kb.sem(name + "_gi")

    with ExitStack() as s1:
        x3bf = s1.enter_context(nc.sbuf_tensor(name + "_x3", [128, 8 * T], BF16))
        WQb = s1.enter_context(nc.sbuf_tensor(name + "_wq", [128, 8 * 1024], BF16))
        WKb = s1.enter_context(nc.sbuf_tensor(name + "_wk", [128, 8 * 1024], BF16))
        WIb = s1.enter_context(nc.sbuf_tensor(name + "_wi", [128, 8 * 1024], BF16))
        qc = [s1.enter_context(nc.sbuf_tensor(f"{name}_qc{i}", [128, 8 * CH], BF16)) for i in range(2)]
        for wn, wsb in (("WQT", WQb), ("WKT", WKb), ("WinvT", WIb)):
            kb.dma("gpsimd", wsb[:].rearrange("p (k c) -> p k c", c=1024),
                   whbm[wn].rearrange("(k p) c -> p k c", p=128))
        vw = kb.sem("dma_gpsimd").n
        kb.wait("tensor", kb.sem("dma_gpsimd"), vw)
        kb.barrier()

        def s1_emit(b):
            _, vx = kb.dma("gpsimd", x3bf[:].rearrange("p (k t) -> p k t", k=8),
                           hbm_chunk_bview(x3bf_hbm, b), sem=s_ld)
            evd = {2: 0, 3: 0}
            qcfree = [0, 0]
            for src_i, (wsb, dst_i) in enumerate(((WQb, 0), (WKb, 1))):
                for j in range(NCH):
                    pj = j % 2
                    wq0 = [(s_ld, vx)] if (j == 0 and src_i == 0) else []
                    if qcfree[pj]:
                        wq0 = wq0 + [(s_mm, qcfree[pj])]
                    for m in range(8):
                        par = 2 + (m % 2)
                        wmm = list(wq0) if m == 0 else []
                        if evd[par]:
                            wmm.append((s_e, evd[par]))
                        for k in range(8):
                            kb.op("tensor", lambda e, m=m, k=k, j=j, par=par, wsb=wsb: e.matmul(
                                psW[par][:], wsb[:, k * 1024 + m * 128:k * 1024 + (m + 1) * 128],
                                x3bf[:, k * T + j * CH:k * T + (j + 1) * CH],
                                start=(k == 0), stop=(k == 7)),
                                wait=(wmm if k == 0 else []),
                                inc=([(s_mm, 1)] if k == 7 else []))
                        vm = s_mm.n
                        kb.op("scalar", lambda e, m=m, par=par, pj=pj: e.copy(
                            out=qc[pj][:, m * CH:(m + 1) * CH], in_=psW[par][:]),
                            wait=([(s_mm, vm), (s_qs, s_qs.n)] if m == 0 else [(s_mm, vm)]),
                            inc=[(s_e, 1)])
                        evd[par] = s_e.n
                    vch = s_e.n
                    _, vq = kb.dma("sync", hbm_chunk_view(qk_hbm[dst_i], b, j),
                                   sb_chunk3(qc[pj]), sem=s_qs, wait=[(s_e, vch)])
                    for m in range(8):
                        par = 2 + (m % 2)
                        wmm = [(s_e, evd[par])]
                        for k in range(8):
                            kb.op("tensor", lambda e, m=m, k=k, par=par, pj=pj: e.matmul(
                                psW[par][:], WIb[:, k * 1024 + m * 128:k * 1024 + (m + 1) * 128],
                                qc[pj][:, k * CH:(k + 1) * CH],
                                start=(k == 0), stop=(k == 7)),
                                wait=(wmm if k == 0 else []),
                                inc=([(s_mm, 1)] if k == 7 else []))
                        vm = s_mm.n
                        kb.op("scalar", lambda e, m=m, par=par, pj=pj: e.copy(
                            out=qc[pj][:, m * CH:(m + 1) * CH], in_=psW[par][:]),
                            wait=[(s_mm, vm), (s_qs, vq)] if m == 0 else [(s_mm, vm)],
                            inc=[(s_e, 1)])
                        evd[par] = s_e.n
                    vch = s_e.n
                    _, vq2 = kb.dma("sync", hbm_chunk_view(qk_hbm[2 + dst_i], b, j),
                                    sb_chunk3(qc[pj]), sem=s_qs, wait=[(s_e, vch)])
                    qcfree[pj] = s_mm.n
            kb.barrier()
        for b in range(NB):
            s1_emit(b)
    kb.barrier()

    with ExitStack() as s2:
        Vb = s2.enter_context(nc.sbuf_tensor(name + "_Vb", [128, 8 * 512], BF16))
        Wb = s2.enter_context(nc.sbuf_tensor(name + "_Wb", [128, 8 * 512], BF16))
        rwU = s2.enter_context(nc.sbuf_tensor(name + "_rwU", [128, 8 * 1024], BF16))
        rw1 = rwU
        Ub = rwU
        rw2 = s2.enter_context(nc.sbuf_tensor(name + "_rw2", [128, 8 * 8], BF16))
        rb1 = s2.enter_context(nc.sbuf_tensor(name + "_rb1", [128, 8], F32))
        comb = s2.enter_context(nc.sbuf_tensor(name + "_cmb", [128, 8], F32))
        bexp = s2.enter_context(nc.sbuf_tensor(name + "_bex", [128, 8], F32))
        bk = s2.enter_context(nc.sbuf_tensor(name + "_bk", [8, 1024], BF16))
        Rk = s2.enter_context(nc.sbuf_tensor(name + "_Rk", [8, 512], BF16))
        alr = s2.enter_context(nc.sbuf_tensor(name + "_al", [1, 4], F32))
        acol = s2.enter_context(nc.sbuf_tensor(name + "_ac", [8, 4], F32))
        identf = s2.enter_context(nc.sbuf_tensor(name + "_idf", [128, 128], F32))
        xV = s2.enter_context(nc.sbuf_tensor(name + "_xV", [128, 4 * T], BF16))
        yW = s2.enter_context(nc.sbuf_tensor(name + "_yW", [128, 4 * T], BF16))
        prodw = s2.enter_context(nc.sbuf_tensor(name + "_pw", [128, 4 * T], BF16))
        hc = s2.enter_context(nc.sbuf_tensor(name + "_hc", [128, 8 * CH], BF16))
        qsb = [s2.enter_context(nc.sbuf_tensor(f"{name}_qsb{i}", [128, 8 * CH], BF16)) for i in range(2)]
        warr = s2.enter_context(nc.sbuf_tensor(name + "_wa", [128, 512], F32))
        wT = s2.enter_context(nc.sbuf_tensor(name + "_wT", [8, T], BF16))
        invn = s2.enter_context(nc.sbuf_tensor(name + "_in", [128, CH], F32))
        if "invn_sb" not in consts_sb:
            c_invn = s2.enter_context(nc.sbuf_tensor(name + "_cinvn", [1, T], F32))
            kb.dma("sync", c_invn[:], consts_sb["invn_hbm"])
            consts_sb["invn_sb"] = c_invn
        gst = [s2.enter_context(nc.sbuf_tensor(f"{name}_gst{i}", [128, CH], BF16)) for i in range(4)]
        gfin = [s2.enter_context(nc.sbuf_tensor(f"{name}_gf{i}", [128, 8 * CH], BF16)) for i in range(2)]

        kb.dma("sync", rb1[:], whbm["rb1"].rearrange("k p -> p k"), allow_slow_non_contiguous=True)
        kb.dma("sync", comb[:], whbm["rb2_t"], new_group=False)
        kb.dma("sync", bexp[:], whbm["expb_t"], new_group=False)
        kb.dma("sync", Rk[:], consts_sb["Rk_hbm"], new_group=False)
        kb.dma("sync", alr[0:1, 0:1], whbm["alpha"], new_group=False)
        _, vld0 = kb.dma("sync", identf[:], consts_sb["identf_hbm"], new_group=False)
        _, vld1 = kb.dma("gpsimd", rw2[:].rearrange("p (k c) -> p k c", c=8),
                         whbm["rw2T"].rearrange("(k p) c -> p k c", p=128))
        kb.wait("vector", kb.sem("dma_sync"), vld0)
        kb.wait("vector", kb.sem("dma_gpsimd"), vld1)
        kb.chain("vector", lambda e: e.tensor_add(out=comb[:], in0=comb[:], in1=bexp[:]), s_c, drain=True)
        kb.chain("tensor", lambda e: e.matmul(psW[0][0:8, 0:1], consts_sb["ones_row"][:, 0:8],
                                              alr[0:1, 0:1], start=True, stop=True), s_c)
        kb.chain("vector", lambda e: e.tensor_copy(out=acol[:, 0:1], in_=psW[0][0:8, 0:1]), s_c, drain=True)
        kb.barrier()

        def branch_emit(b, br):
            qsrc = qk_hbm[0 if br == 0 else 2]
            ksrc = qk_hbm[1 if br == 0 else 3]
            VT = "Vf_T" if br == 0 else "Vi_T"
            WT = "Wf_T" if br == 0 else "Wi_T"
            UF = "Uf_flat" if br == 0 else "Ui_flat"
            BK = "bf_k" if br == 0 else "bi_k"
            kb.dma("gpsimd", Vb[:].rearrange("p (k c) -> p k c", c=512),
                   whbm[VT].rearrange("(k p) c -> p k c", p=128))
            kb.dma("gpsimd", Wb[:].rearrange("p (k c) -> p k c", c=512),
                   whbm[WT].rearrange("(k p) c -> p k c", p=128), new_group=False)
            kb.dma("gpsimd", rw1[:].rearrange("p (k c) -> p k c", c=1024),
                   whbm["rw1T"].rearrange("(k p) c -> p k c", p=128), new_group=False)
            _, vwl = kb.dma("gpsimd", bk[:], whbm[BK], new_group=False)
            kb.wait("tensor", kb.sem("dma_gpsimd"), vwl)
            kb.barrier()
            evd = {2: 0, 3: 0}
            for j in range(NCH):
                pj = j % 2
                _, vq = kb.dma("sync", sb_chunk3(qsb[pj]), hbm_chunk_view(qsrc, b, j), sem=s_qs,
                               wait=[(s_mm, s_mm.n)])
                for m in range(4):
                    par = 2 + (m % 2)
                    wmm = [(s_qs, vq)] if m == 0 else []
                    if evd[par]:
                        wmm.append((s_e, evd[par]))
                    for k in range(8):
                        kb.op("tensor", lambda e, m=m, k=k, par=par, pj=pj: e.matmul(
                            psW[par][:], Vb[:, k * 512 + m * 128:k * 512 + (m + 1) * 128],
                            qsb[pj][:, k * CH:(k + 1) * CH],
                            start=(k == 0), stop=(k == 7)),
                            wait=(wmm if k == 0 else []),
                            inc=([(s_mm, 1)] if k == 7 else []))
                    vm = s_mm.n
                    kb.op("scalar", lambda e, m=m, j=j, par=par: e.copy(
                        out=xV[:, m * T + j * CH:m * T + (j + 1) * CH], in_=psW[par][:]),
                        wait=[(s_mm, vm)], inc=[(s_e, 1)])
                    evd[par] = s_e.n
                for m in range(8):
                    par = 2 + (m % 2)
                    wmm = [(s_e, evd[par])] + ([(s_c, s_c.n)] if m == 0 else [])
                    for k in range(8):
                        kb.op("tensor", lambda e, m=m, k=k, par=par, pj=pj: e.matmul(
                            psW[par][:], rw1[:, k * 1024 + m * 128:k * 1024 + (m + 1) * 128],
                            qsb[pj][:, k * CH:(k + 1) * CH],
                            start=(k == 0), stop=(k == 7)),
                            wait=(wmm if k == 0 else []),
                            inc=([(s_mm, 1)] if k == 7 else []))
                    vm = s_mm.n
                    kb.op("scalar", lambda e, m=m, par=par: e.activation(
                        out=hc[:, m * CH:(m + 1) * CH], in_=psW[par][:], func=AF.Gelu,
                        bias=rb1[:, m:m + 1]),
                        wait=[(s_mm, vm)], inc=[(s_e, 1)])
                    evd[par] = s_e.n
                vh = s_e.n
                for tt in range(4):
                    ti = j * 4 + tt
                    for k in range(8):
                        kb.chain("tensor", lambda e, tt=tt, k=k: e.matmul(
                            psW[0][:, 0:8], hc[:, k * CH + tt * 128:k * CH + (tt + 1) * 128],
                            rw2[:, k * 8:(k + 1) * 8], start=(k == 0), stop=(k == 7)), s_c,
                            extra_wait=([(s_e, vh)] if (tt == 0 and k == 0) else []))
                    def smax(e, ti=ti):
                        e.tensor_add(out=warr[:, 0:8], in0=psW[0][:, 0:8], in1=comb[:])
                        e.drain()
                        e.tensor_reduce(out=warr[:, 12:13], in_=warr[:, 0:8], axis=AXX,
                                        op=ALU.max, negate=True)
                        return e.drain()
                    kb.chain("vector", smax, s_c, drain=True)
                    kb.chain("scalar", lambda e: e.activation(
                        out=warr[:, 16:24], in_=warr[:, 0:8], func=AF.Exp,
                        bias=warr[:, 12:13]), s_c, drain=True)
                    def smax2(e, ti=ti):
                        e.tensor_reduce(out=warr[:, 24:25], in_=warr[:, 16:24], axis=AXX, op=ALU.add)
                        e.drain()
                        e.reciprocal(out=warr[:, 28:29], in_=warr[:, 24:25])
                        e.drain()
                        return e.tensor_scalar(out=warr[:, 32 + (ti % 32) * 8:32 + (ti % 32) * 8 + 8]
                                               if False else warr[:, 32:40],
                                               in0=warr[:, 16:24], scalar1=warr[:, 28:29],
                                               scalar2=None, op0=ALU.mult)
                    kb.chain("vector", smax2, s_c, drain=True)
                    kb.chain("tensor", lambda e, ti=ti: e.transpose(
                        psW[0][0:8, 0:128], warr[:, 32:40], identf[:]), s_c)
                    kb.chain("vector", lambda e, ti=ti: e.tensor_copy(
                        out=wT[:, ti * 128:(ti + 1) * 128], in_=psW[0][0:8, 0:128]), s_c, drain=True)
            evd = {2: 0, 3: 0}
            for j in range(NCH):
                pj = j % 2
                _, vk = kb.dma("sync", sb_chunk3(qsb[pj]), hbm_chunk_view(ksrc, b, j), sem=s_ks,
                               wait=[(s_mm, s_mm.n)])
                for m in range(4):
                    par = 2 + (m % 2)
                    wmm = [(s_ks, vk)] if m == 0 else []
                    if evd[par]:
                        wmm.append((s_e, evd[par]))
                    for k in range(8):
                        kb.op("tensor", lambda e, m=m, k=k, par=par, pj=pj: e.matmul(
                            psW[par][:], Wb[:, k * 512 + m * 128:k * 512 + (m + 1) * 128],
                            qsb[pj][:, k * CH:(k + 1) * CH],
                            start=(k == 0), stop=(k == 7)),
                            wait=(wmm if k == 0 else []),
                            inc=([(s_mm, 1)] if k == 7 else []))
                    vm = s_mm.n
                    kb.op("scalar", lambda e, m=m, j=j, par=par: e.copy(
                        out=yW[:, m * T + j * CH:m * T + (j + 1) * CH], in_=psW[par][:]),
                        wait=[(s_mm, vm)], inc=[(s_e, 1)])
                    evd[par] = s_e.n
            vyw = s_e.n
            for m in range(4):
                kb.chain("vector", lambda e, m=m: e.tensor_tensor_scan(
                    out=yW[:, m * T:(m + 1) * T], data0=yW[:, m * T:(m + 1) * T],
                    data1=yW[:, m * T:(m + 1) * T], initial=0.0,
                    op0=ALU.add, op1=ALU.bypass), s_c,
                    extra_wait=([(s_e, vyw)] if m == 0 else []))
            if br == 1:
                kb.chain("vector", lambda e: e.tensor_scalar(
                    out=wT[:], in0=wT[:], scalar1=acol[:, 0:1], scalar2=None,
                    op0=ALU.mult), s_c, drain=True)
            for m in range(4):
                for j in range(NCH):
                    par = 2 + (j % 2)
                    kb.chain("tensor", lambda e, m=m, j=j, par=par: e.matmul(
                        psW[par][:], Rk[:, m * 128:(m + 1) * 128],
                        wT[:, j * CH:(j + 1) * CH], start=True, stop=True), s_c)
                    def prod_tt(e, m=m, j=j, par=par):
                        e.tensor_mul(out=prodw[:, m * T + j * CH:m * T + (j + 1) * CH],
                                     in0=xV[:, m * T + j * CH:m * T + (j + 1) * CH],
                                     in1=yW[:, m * T + j * CH:m * T + (j + 1) * CH])
                        return e.tensor_mul(out=prodw[:, m * T + j * CH:m * T + (j + 1) * CH],
                                            in0=prodw[:, m * T + j * CH:m * T + (j + 1) * CH],
                                            in1=psW[par][:])
                    kb.chain("vector", prod_tt, s_c)
            vprod = s_c.n
            _, vub = kb.dma("gpsimd", Ub[:, 0:4 * 1024].rearrange("p (k c) -> p k c", c=1024),
                            whbm[UF].rearrange("(k p) c -> p k c", p=128),
                            wait=[(s_c, vprod), (s_mm, s_mm.n)])
            kb.wait("tensor", kb.sem("dma_gpsimd"), vub)
            evd = {2: 0, 3: 0}
            gfree = [0, 0, 0, 0]
            for j in range(NCH):
                pj = j % 2
                kb.chain("tensor", lambda e, j=j: e.matmul(
                    psW[0][:, 0:CH], consts_sb["ones_row"][:],
                    consts_sb["invn_sb"][0:1, j * CH:(j + 1) * CH], start=True, stop=True), s_c)
                kb.chain("vector", lambda e: e.tensor_copy(out=invn[:], in_=psW[0][:, 0:CH]), s_c,
                         extra_wait=[(s_v, s_v.n)])
                vin = s_c.n
                if br == 1:
                    _, vgf = kb.dma("sync", sb_chunk3(gfin[pj]), hbm_chunk_view(g_hbm, b, j),
                                    sem=s_go, wait=[(s_v, s_v.n)])
                    branch_emit.gf_dma[j % 2] = vgf
                for m in range(8):
                    par = 2 + (m % 2)
                    gb = (8 * j + m) % 4
                    wmm = [(s_c, vin)] if m == 0 else []
                    if evd[par]:
                        wmm.append((s_v, evd[par]))
                    for k in range(4):
                        kb.op("tensor", lambda e, m=m, k=k, j=j, par=par: e.matmul(
                            psW[par][:], Ub[:, k * 1024 + m * 128:k * 1024 + (m + 1) * 128],
                            prodw[:, k * T + j * CH:k * T + (j + 1) * CH],
                            start=(k == 0), stop=False),
                            wait=(wmm if k == 0 else []))
                    kb.op("tensor", lambda e, m=m, j=j, par=par: e.matmul(
                        psW[par][:], bk[:, m * 128:(m + 1) * 128],
                        wT[:, j * CH:(j + 1) * CH], start=False, stop=True),
                        inc=[(s_mm, 1)])
                    vm = s_mm.n
                    wtv = [(s_mm, vm)]
                    if gfree[gb]:
                        wtv.append((s_gi, gfree[gb]))
                    if br == 0:
                        kb.op("vector", lambda e, par=par, gb=gb: e.tensor_mul(
                            out=gst[gb][:], in0=psW[par][:], in1=invn[:]),
                            wait=wtv, inc=[(s_v, 1)])
                    else:
                        def gcomb(e, par=par, gb=gb, m=m, pj=pj):
                            e.tensor_mul(out=gst[gb][:], in0=psW[par][:], in1=invn[:])
                            return e.tensor_add(out=gst[gb][:], in0=gst[gb][:],
                                                in1=gfin[pj][:, m * CH:(m + 1) * CH])
                        wtv2 = wtv + ([(s_go, branch_emit.gf_dma[j % 2])] if m == 0 else [])
                        kb.op("vector", gcomb, wait=wtv2, inc=[(s_v, 1)])
                    evd[par] = s_v.n
                    _, vg = kb.dma("sync", g_hbm[b][m * 128:(m + 1) * 128, j * CH:(j + 1) * CH],
                                   gst[gb][:], sem=s_gi, wait=[(s_v, evd[par])])
                    gfree[gb] = vg
            kb.barrier()
        branch_emit.gf_dma = [0, 0]

        for b in range(NB):
            branch_emit.gf_dma = [0, 0]
            branch_emit(b, 0)
            branch_emit(b, 1)
    kb.barrier()

    WOb = ctx.enter_context(nc.sbuf_tensor(name + "_wo", [128, 8 * 1024], BF16))
    _, vwo = kb.dma("gpsimd", WOb[:].rearrange("p (k c) -> p k c", c=1024),
                    whbm["WOT"].rearrange("(k p) c -> p k c", p=128))
    bufs = make_ln_bufs(ctx, nc, name + "_ln", with_default_fill=BF16)
    bufs["ps"] = ps
    kb.wait("tensor", kb.sem("dma_gpsimd"), vwo)
    kb.barrier()
    RR = []
    for b in range(NB):
        fill = make_matmul_fill(kb, ctx, nc, ps, f"{name}_f{b}", WOb, g_hbm,
                                x3f_hbm, bufs, b)

        def out_view(j, b=b):
            return (hbm_chunk_view(out_f32_hbm, b, j), hbm_chunk_view(out_bf_hbm, b, j))

        RR.append(emit_ln_stream(kb, bufs, consts_sb, NCH, fill, out_view, f"{name}_ln{b}"))
        kb.barrier()
    return RR



W1 = ["WqT", "WkT", "WvT", "WoT"]
W3 = ["gate_wT", "gate_b_t32", "bi", "loraAT", "loraBT", "wiT", "woT", "bo_row"]
W4 = ["WdtpT", "WBT", "WCT", "WxsT", "WdtT", "bdt", "A_log", "Dp_col", "WoutT"]
W5 = ["WQT", "WKT", "WinvT", "WOT", "Vf_T", "Wf_T", "Uf_flat", "bf_k",
      "Vi_T", "Wi_T", "Ui_flat", "bi_k", "rw1T", "rb1", "rw2T", "rb2_t",
      "expb_t", "alpha"]
WALL = W1 + W3 + W4 + W5


def _setup_consts(nc, ctx, kb, hc):
    cs = {}
    c_onesb = ctx.enter_context(nc.sbuf_tensor("c_onesb", [128, 1], BF16))
    c_onesr = ctx.enter_context(nc.sbuf_tensor("c_onesr", [1, 128], F32))
    c_eps = ctx.enter_context(nc.sbuf_tensor("c_eps", [1, 1], F32))
    kb.dma("sync", c_onesb[:], nc.inline_tensor(hc["ones_col_bf"], "cc1").ap())
    kb.dma("sync", c_onesr[:], nc.inline_tensor(hc["ones_row"], "cc2").ap())
    kb.dma("sync", c_eps[:], nc.inline_tensor(np.full((1, 1), EPS, np.float32), "cc3").ap())
    cs["ones_col_bf"] = c_onesb
    cs["ones_row"] = c_onesr
    cs["eps11"] = c_eps
    cs["invn_hbm"] = nc.inline_tensor(hc["inv_n"], "cc4").ap()
    cs["tie_hbm"] = nc.inline_tensor(hc["tie_eps"], "ctie").ap()
    cs["tie32_hbm"] = nc.inline_tensor(hc["tie32"], "ctie32").ap()
    cs["Rl_hbm"] = nc.inline_tensor(hc["R_lora"].astype(BF), "cRl").ap()
    cs["Rk_hbm"] = nc.inline_tensor(hc["R_kr"].astype(BF), "cRk").ap()
    cs["ident_hbm"] = nc.inline_tensor(hc["identity_bf"], "cid").ap()
    cs["identf_hbm"] = nc.inline_tensor(np.eye(128, dtype=np.float32), "cidf").ap()
    return cs


def _build_fused(wshapes, hc):
    nc = bass.Bass("TRN2", target_bir_lowering=False, debug=False, detect_race_conditions=False)
    xres = nc.dram_tensor("xres", [NB, D, T], F32, kind="ExternalInput").ap()
    x4 = nc.dram_tensor("x4", [NB, D, T], F32, kind="ExternalOutput").ap()
    whbm = {n: nc.dram_tensor(n, list(wshapes[n]), F32, kind="ExternalInput").ap() for n in WALL}
    dr = lambda nm, sh, dt: nc.dram_tensor(nm, sh, dt, kind="Internal").ap()
    attnpre = dr("attnpre", [NB, D, T], BF16)
    x1f = dr("x1f", [NB, D, T], F32)
    x1b = dr("x1b", [NB, D, T], BF16)
    hsum = dr("hsum", [NB, DFF, T], BF16)
    ffnout = dr("ffnout", [NB, D, T], BF16)
    x2f = dr("x2f", [NB, D, T], F32)
    x2b = dr("x2b", [NB, D, T], BF16)
    x3f = dr("x3f", [NB, D, T], F32)
    x3b = dr("x3b", [NB, D, T], BF16)
    qk = [dr(f"qk{i}", [NB, D, T], BF16) for i in range(4)]
    g = dr("gcomb", [NB, D, T], BF16)
    x4b = dr("x4b", [NB, D, T], BF16)
    with ExitStack() as ctx:
        kb = KB(nc, ctx)
        cs = _setup_consts(nc, ctx, kb, hc)
        ps = make_psum_pool(ctx, nc)
        kb.barrier()
        with ExitStack() as pctx:
            emit_attn_phase(kb, pctx, nc, cs, ps, whbm, xres, attnpre, x1f, x1b)
            kb.barrier()
        with ExitStack() as pctx:
            emit_ffn0_phase(kb, pctx, nc, cs, ps, whbm, x1b, x1f, hsum, ffnout, x2f, x2b)
            kb.barrier()
        with ExitStack() as pctx:
            emit_ssm_phase(kb, pctx, nc, cs, ps, whbm, x2b, x2f, x3f, x3b)
            kb.barrier()
        with ExitStack() as pctx:
            emit_ffn1_phase(kb, pctx, nc, cs, ps, whbm, x3b, x3f, qk, g, x4, x4b)
            kb.barrier()
        kb.finalize()
    return nc


def kernel(**inputs):
    inputs = {k: np.asarray(v) for k, v in inputs.items()}
    w = prep_weights(inputs)
    hc = host_constants()
    NCORES = 8
    x = inputs["L0_attn_x"]
    shards = [np.ascontiguousarray(x[2 * c:2 * c + 2].transpose(0, 2, 1)) for c in range(NCORES)]
    wsh = {k: v.shape for k, v in w.items()}
    nc = _build_fused(wsh, hc)
    maps = [{"xres": shards[c], **{n: w[n] for n in WALL}} for c in range(NCORES)]
    rr = run_bass_kernel_spmd(nc, maps, core_ids=list(range(NCORES))).results
    out = np.concatenate([np.asarray(rr[c]["x4"]).transpose(0, 2, 1) for c in range(NCORES)], axis=0)
    return np.ascontiguousarray(out.astype(np.float32))


# version 13
# speedup vs baseline: 1.0550x; 1.0002x over previous
import numpy as np
import concourse.bass as bass
import concourse.mybir as mybir
from concourse.bass_utils import run_bass_kernel_spmd
from contextlib import ExitStack
import ml_dtypes

F32 = mybir.dt.float32
BF16 = mybir.dt.bfloat16
AF = mybir.ActivationFunctionType
ALU = mybir.AluOpType
AXX = mybir.AxisListType.X
BF = ml_dtypes.bfloat16

ENGINES = ("tensor", "vector", "scalar", "gpsimd", "sync")

D = 1024
T = 4096
NB = 2
NH = 16
HD = 64
DFF = 4096
EPS = 1e-5


class Sem:
    __slots__ = ("h", "n", "name")

    def __init__(self, h, name):
        self.h = h
        self.n = 0
        self.name = name

    def bump(self, k):
        self.n += k
        return self.n


class KB:

    def __init__(self, nc, ctx):
        self.nc = nc
        self.ctx = ctx
        self.q = {e: [] for e in ENGINES}
        self._sems = {}
        self.dma_sems = set()
        self.barrier_sem = self.sem("kb_barrier")

    def sem(self, name):
        if name not in self._sems:
            h = self.ctx.enter_context(self.nc.semaphore(name))
            self._sems[name] = Sem(h, name)
        return self._sems[name]

    def op(self, eng, fn, wait=(), inc=(), drain=False):
        wait = tuple((s, v) for s, v in wait)
        planned = []
        inc2 = []
        for s, k in inc:
            planned.append(s.bump(k))
            inc2.append((s, k))

        def thunk(e, fn=fn, wait=wait, inc2=tuple(inc2), drain=drain):
            for s, v in wait:
                e.wait_ge(s.h, v)
            inst = fn(e)
            if drain:
                e.drain()
                for s, k in inc2:
                    e.sem_inc(s.h, k)
            else:
                for s, k in inc2:
                    assert inst is not None, "inc attached to op returning None"
                    inst = inst.then_inc(s.h, k)
            return inst

        self.q[eng].append(thunk)
        return planned

    def wait(self, eng, sem, val):
        self.op(eng, lambda e: None, wait=[(sem, val)])

    def chain(self, eng, fn, s, extra_wait=(), drain=False):
        v = s.n
        self.op(eng, fn, wait=[(s, v)] + list(extra_wait), inc=[(s, 1)], drain=drain)
        return s.n

    def dma(self, queue, out, in_, sem=None, wait=(), new_group=True, **kw):
        s = sem if sem is not None else self.sem("dma_" + queue)
        self.dma_sems.add(s)
        w = list(wait)
        if new_group:
            w.append((s, s.n))
        v = self.op(queue, lambda e: e.dma_start(out=out, in_=in_, **kw),
                    wait=w, inc=[(s, 16)])
        return s, v[0]

    def barrier(self, engines=ENGINES, name=None):
        arr = self.barrier_sem
        active = [e for e in engines]
        dma_waits = [(s, s.n) for s in self.dma_sems if s.n > 0]
        for e in active:
            def arrive(eng, e=e):
                if e != "sync":
                    eng.drain()
                return eng.sem_inc(arr.h, 1)
            self.op(e, arrive)
            arr.bump(1)
        tgt = arr.n
        for e in active:
            def waitall(eng, tgt=tgt, dma_waits=tuple(dma_waits)):
                eng.wait_ge(arr.h, tgt)
                for s, v in dma_waits:
                    eng.wait_ge(s.h, v)
            self.q[e].append(waitall)

    def finalize(self):
        block = self.ctx.enter_context(self.nc.Block())
        for e in ENGINES:
            ops = self.q[e]
            if not ops:
                continue
            def runner(eng, ops=ops):
                for t in ops:
                    t(eng)
            getattr(block, e)(runner)



def prep_weights(inp):
    w = {}
    w["WqT"] = np.ascontiguousarray(inp["L0_attn_Wq"].T)
    w["WkT"] = np.ascontiguousarray(inp["L0_attn_Wk"].T)
    w["WvT"] = np.ascontiguousarray(inp["L0_attn_Wv"].T)
    w["WoT"] = np.ascontiguousarray(inp["L0_attn_Wo"].T)
    w["wiT"] = np.ascontiguousarray(inp["L0_ffn_wi"].T)
    w["woT"] = np.ascontiguousarray(inp["L0_ffn_wo"].T)
    w["bi"] = np.ascontiguousarray(inp["L0_ffn_bi"].reshape(32, 128))
    w["bo_row"] = np.ascontiguousarray(inp["L0_ffn_bo"].reshape(1, 1024))
    w["loraAT"] = np.ascontiguousarray(inp["L0_ffn_lora_A"].reshape(128, 1024).T)
    w["loraBT"] = np.ascontiguousarray(
        inp["L0_ffn_lora_B"].transpose(0, 2, 1).reshape(128, DFF))
    w["gate_wT"] = np.ascontiguousarray(inp["L0_ffn_gate_w"].T)
    w["gate_b_t"] = np.ascontiguousarray(np.tile(inp["L0_ffn_gate_b"].reshape(1, 8), (128, 1)))
    w["gate_b_t32"] = np.ascontiguousarray(np.tile(inp["L0_ffn_gate_b"].reshape(1, 8), (128, 32)))
    Wxp = inp["L1_attn_Wxproj"]
    w["WdtpT"] = np.ascontiguousarray(Wxp[:64].T)
    w["WBT"] = np.ascontiguousarray(Wxp[64:80].T)
    w["WCT"] = np.ascontiguousarray(Wxp[80:96].T)
    w["WxsT"] = np.ascontiguousarray(inp["L1_attn_Wxs"].T)
    w["WdtT"] = np.ascontiguousarray(inp["L1_attn_Wdt"].T)
    w["bdt"] = np.ascontiguousarray(inp["L1_attn_bdt"].reshape(1, 1))
    w["A_log"] = np.ascontiguousarray(inp["L1_attn_A_log"].reshape(1, 16))
    w["Dp_col"] = np.ascontiguousarray(inp["L1_attn_Dp"].reshape(8, 128))
    w["WoutT"] = np.ascontiguousarray(inp["L1_attn_Wout"].T)
    w["WQT"] = np.ascontiguousarray(inp["L1_ffn_W_Q"].T)
    w["WKT"] = np.ascontiguousarray(inp["L1_ffn_W_K"].T)
    w["WOT"] = np.ascontiguousarray(inp["L1_ffn_W_O"].T)
    w["WinvT"] = np.ascontiguousarray(inp["L1_ffn_W_inv"].T)
    w["Vf_T"] = np.ascontiguousarray(inp["L1_ffn_V_fwd"].transpose(1, 0, 2).reshape(D, 512))
    w["Wf_T"] = np.ascontiguousarray(inp["L1_ffn_W_fwd"].transpose(1, 0, 2).reshape(D, 512))
    w["Uf_flat"] = np.ascontiguousarray(inp["L1_ffn_U_fwd"].transpose(0, 2, 1).reshape(512, D))
    w["bf_k"] = np.ascontiguousarray(inp["L1_ffn_b_fwd"])
    w["Vi_T"] = np.ascontiguousarray(inp["L1_ffn_W_inv_exp"].transpose(1, 0, 2).reshape(D, 512))
    w["Wi_T"] = np.ascontiguousarray(inp["L1_ffn_V_inv"].transpose(1, 0, 2).reshape(D, 512))
    w["Ui_flat"] = np.ascontiguousarray(inp["L1_ffn_U_inv"].transpose(0, 2, 1).reshape(512, D))
    w["bi_k"] = np.ascontiguousarray(inp["L1_ffn_b_inv"])
    w["rw1T"] = np.ascontiguousarray(inp["L1_ffn_router_w1"].T)
    w["rb1"] = np.ascontiguousarray(inp["L1_ffn_router_b1"].reshape(8, 128))
    w["rw2T"] = np.ascontiguousarray(inp["L1_ffn_router_w2"].T)
    w["rb2"] = np.ascontiguousarray(inp["L1_ffn_router_b2"].reshape(1, 8))
    w["expb"] = np.ascontiguousarray(inp["L1_ffn_expert_bias"].reshape(1, 8))
    w["rb2_t"] = np.ascontiguousarray(np.tile(inp["L1_ffn_router_b2"].reshape(1, 8), (128, 1)))
    w["expb_t"] = np.ascontiguousarray(np.tile(inp["L1_ffn_expert_bias"].reshape(1, 8), (128, 1)))
    w["alpha"] = np.ascontiguousarray(np.asarray(inp["L1_ffn_alpha_bi"]).reshape(1, 1))
    return w


def host_constants():
    c = {}
    c["identity_bf"] = np.eye(128, dtype=BF)
    c["ones_col"] = np.ones((128, 1), np.float32)
    c["ones_col_bf"] = np.ones((128, 1), BF)
    c["ones_row"] = np.ones((1, 128), np.float32)
    c["inv_n"] = (1.0 / np.arange(1, T + 1, dtype=np.float64)).astype(np.float32).reshape(1, T)
    c["tie_eps"] = np.tile((np.arange(8, dtype=np.float32) * -1e-5).reshape(1, 8), (128, 1))
    c["tie32"] = np.tile((np.arange(8, dtype=np.float32) * -1e-5).reshape(1, 8), (128, 32))
    R = np.zeros((8, 128), np.float32)
    for e in range(8):
        R[e, e * 16:(e + 1) * 16] = 1.0
    c["R_lora"] = R
    Rk = np.zeros((8, 512), np.float32)
    for k in range(8):
        Rk[k, k * 64:(k + 1) * 64] = 1.0
    c["R_kr"] = Rk
    return c



LN_CH = 512


def make_ln_bufs(ctx, nc, name, with_default_fill=None):
    CH = LN_CH
    bufs = {"name": name}
    if with_default_fill is not None:
        bufs["rin"] = [ctx.enter_context(nc.sbuf_tensor(f"{name}_rin{i}", [128, 8 * CH], F32)) for i in range(2)]
        if with_default_fill != "none":
            bufs["ain"] = [ctx.enter_context(nc.sbuf_tensor(f"{name}_ain{i}", [128, 8 * CH], with_default_fill)) for i in range(2)]
    bufs["zb"] = [ctx.enter_context(nc.sbuf_tensor(f"{name}_z{i}", [128, 8 * CH], F32)) for i in range(2)]
    bufs["zbf"] = [ctx.enter_context(nc.sbuf_tensor(f"{name}_zbf{i}", [128, 8 * CH], BF16)) for i in range(2)]
    bufs["zsq"] = [ctx.enter_context(nc.sbuf_tensor(f"{name}_zsq{i}", [128, 8 * CH], BF16)) for i in range(2)]
    bufs["rows"] = [ctx.enter_context(nc.sbuf_tensor(f"{name}_rows{i}", [1, 5 * CH], F32)) for i in range(2)]
    bufs["xo"] = [ctx.enter_context(nc.sbuf_tensor(f"{name}_xo{i}", [128, 8 * CH], F32)) for i in range(2)]
    bufs["xob"] = [ctx.enter_context(nc.sbuf_tensor(f"{name}_xob{i}", [128, 8 * CH], BF16)) for i in range(1)]
    return bufs


def make_psum_pool(ctx, nc):
    ps = {}
    ps["ps_sz"] = ctx.enter_context(nc.psum_tensor("g_psz", [1, LN_CH], F32))
    ps["ps_sq"] = ctx.enter_context(nc.psum_tensor("g_psq", [1, LN_CH], F32))
    ps["psA"] = ctx.enter_context(nc.psum_tensor("g_psA", [128, LN_CH], F32))
    ps["psB"] = ctx.enter_context(nc.psum_tensor("g_psB", [128, LN_CH], F32))
    psP2 = ctx.enter_context(nc.psum_tensor("g_psP2", [128, 2 * LN_CH], F32))
    ps["psP2"] = psP2
    ps["psW"] = [psP2[:, 0:LN_CH], psP2[:, LN_CH:2 * LN_CH],
                 ctx.enter_context(nc.psum_tensor("g_psW2", [128, LN_CH], F32)),
                 ctx.enter_context(nc.psum_tensor("g_psW3", [128, LN_CH], F32))]
    return ps


def hbm_chunk_view(hbm, b, j, CH=LN_CH):
    return hbm[b].rearrange("(k p) t -> p k t", p=128)[:, :, j * CH:(j + 1) * CH]


def sb_chunk3(t):
    return t[:].rearrange("p (k t) -> p k t", k=8)


def emit_ln_stream(kb, bufs, consts_sb, nchunks, fill_z, out_view, name):
    CH = LN_CH
    zb = bufs["zb"]; zbf = bufs["zbf"]; zsq = bufs["zsq"]; rows = bufs["rows"]
    xo = bufs["xo"]; xob = bufs["xob"]
    ps = bufs["ps"]
    ps_sz = ps["ps_sz"]; ps_sq = ps["ps_sq"]; psA = ps["psA"]; psB = ps["psB"]
    s_out = kb.sem("LN_out")
    s_cg = kb.sem("LN_cg")
    s_cs = kb.sem("LN_cs")
    s_st = kb.sem("LN_st")
    s_rv = kb.sem("LN_rv")
    s_rs = kb.sem("LN_rs")
    s_bc = kb.sem("LN_bc")
    s_av = kb.sem("LN_av")
    s_ag = kb.sem("LN_ag")
    ones_bf = consts_sb["ones_col_bf"]
    ones_row = consts_sb["ones_row"]
    R = [dict() for _ in range(nchunks)]

    for j in range(nchunks):
        pj = j % 2
        r = R[j]
        wait_free = [(s_av, R[j - 2]["tt_done"])] if j >= 2 else []
        zsem, zval = fill_z(j, pj, wait_free)
        r["zready"] = zval

        wc = [(zsem, zval)]
        if j >= 2:
            wc = wc + [(s_st, R[j - 2]["stats"])]
        kb.op("gpsimd", lambda e, pj=pj: e.tensor_copy(out=zbf[pj][:], in_=zb[pj][:]),
              wait=wc, inc=[(s_cg, 1)])
        r["cg"] = s_cg.n
        kb.op("scalar", lambda e, pj=pj: e.activation(out=zsq[pj][:], in_=zb[pj][:], func=AF.Square),
              wait=wc, inc=[(s_cs, 1)])
        r["cs"] = s_cs.n

        wmm = [(s_cg, r["cg"]), (s_cs, r["cs"])]
        if j >= 1:
            wmm.append((s_rv, R[j - 1]["row1"]))
        for k in range(8):
            kb.op("tensor", lambda e, k=k, pj=pj: e.matmul(ps_sz[:], ones_bf[:],
                  zbf[pj][:, k * CH:(k + 1) * CH], start=(k == 0), stop=(k == 7)),
                  wait=(wmm if k == 0 else []))
        for k in range(8):
            kb.op("tensor", lambda e, k=k, pj=pj: e.matmul(ps_sq[:], ones_bf[:],
                  zsq[pj][:, k * CH:(k + 1) * CH], start=(k == 0), stop=(k == 7)),
                  inc=([(s_st, 1)] if k == 7 else []))
        r["stats"] = s_st.n

        mu = rows[pj][:, 0:CH]; var = rows[pj][:, CH:2 * CH]; a_r = rows[pj][:, 2 * CH:3 * CH]
        b_r = rows[pj][:, 3 * CH:4 * CH]; t0 = rows[pj][:, 4 * CH:5 * CH]
        wrow = [(s_st, r["stats"])]
        if j >= 2:
            wrow.append((s_bc, R[j - 2]["bc"]))

        def rowmath(e, mu=mu, var=var, t0=t0):
            e.tensor_scalar_mul(out=mu, in0=ps_sz[:], scalar1=1.0 / D)
            e.tensor_scalar_mul(out=var, in0=ps_sq[:], scalar1=1.0 / D)
            e.tensor_mul(out=t0, in0=mu, in1=mu)
            return e.tensor_sub(out=var, in0=var, in1=t0)
        kb.op("vector", rowmath, wait=wrow, inc=[(s_rv, 1)])
        r["row1"] = s_rv.n

        def rowexp(e, var=var, a_r=a_r, t0=t0):
            e.activation(out=t0, in_=var, func=AF.Ln, bias=consts_sb["eps11"][:, 0:1])
            return e.activation(out=a_r, in_=t0, func=AF.Exp, scale=-0.5)
        kb.op("scalar", rowexp, wait=[(s_rv, r["row1"])], inc=[(s_rs, 1)])
        r["row2"] = s_rs.n

        def rowb(e, b_r=b_r, mu=mu, a_r=a_r):
            e.tensor_mul(out=b_r, in0=mu, in1=a_r)
            return e.tensor_scalar_mul(out=b_r, in0=b_r, scalar1=-1.0)
        kb.op("vector", rowb, wait=[(s_rs, r["row2"])], inc=[(s_rv, 1)])
        r["row3"] = s_rv.n

        wbc = [(s_rv, r["row3"])]
        if j >= 1:
            wbc.append((s_av, R[j - 1]["tt_done"]))
        kb.op("tensor", lambda e, a_r=a_r: e.matmul(psA[:], ones_row[:], a_r, start=True, stop=True),
              wait=wbc)
        kb.op("tensor", lambda e, b_r=b_r: e.matmul(psB[:], ones_row[:], b_r, start=True, stop=True),
              inc=[(s_bc, 1)])
        r["bc"] = s_bc.n

        wap = [(s_bc, r["bc"])]
        if j >= 2:
            wap.append((s_out, R[j - 2]["outdma"]))
        for k in range(8):
            sl = slice(k * CH, (k + 1) * CH)
            kb.op("vector", lambda e, pj=pj, sl=sl: e.tensor_mul(out=xo[pj][:, sl], in0=zb[pj][:, sl], in1=psA[:]),
                  wait=(wap if k == 0 else []))
            kb.op("vector", lambda e, pj=pj, sl=sl: e.tensor_add(out=xo[pj][:, sl], in0=xo[pj][:, sl], in1=psB[:]),
                  inc=([(s_av, 1)] if k == 7 else []))
        r["tt_done"] = s_av.n
        wxob = [(s_av, r["tt_done"])]
        if j >= 1:
            wxob.append((s_out, R[j - 1]["outdma"]))
        kb.op("gpsimd", lambda e, pj=pj: e.tensor_copy(out=xob[0][:], in_=xo[pj][:]),
              wait=wxob, inc=[(s_ag, 1)])
        r["ag"] = s_ag.n

        of, ob = out_view(j)
        kb.dma("sync", of, sb_chunk3(xo[pj]), sem=s_out, wait=[(s_av, r["tt_done"])])
        _, vo = kb.dma("sync", ob, sb_chunk3(xob[0]), sem=s_out, new_group=False,
                       wait=[(s_ag, r["ag"])])
        r["outdma"] = vo
    return R


def emit_ln_phase(kb, ctx, nc, consts_sb, ps, res_hbm, addin_hbm, addin_dtype,
                  out_f32_hbm, out_bf_hbm, b_list, name, bufs=None):
    CH = LN_CH
    NCH = T // CH
    if bufs is None:
        bufs = make_ln_bufs(ctx, nc, name, with_default_fill=(addin_dtype if addin_hbm is not None else "none"))
        bufs["ps"] = ps
    rin = bufs["rin"]; ain = bufs.get("ain"); zb = bufs["zb"]
    s_in = [kb.sem("LN_in0"), kb.sem("LN_in1")]
    s_zc = kb.sem("LN_zc")
    nch_total = NCH * len(b_list)

    def fill_z(g, pj, wait_free):
        b = b_list[g // NCH]; j = g % NCH
        w = [(s_zc, fill_z.consumed[g - 2])] if g >= 2 else []
        sS = s_in[pj]
        _, v1 = kb.dma("sync", sb_chunk3(rin[pj]), hbm_chunk_view(res_hbm, b, j), sem=sS, wait=w)
        v2 = v1
        if ain is not None:
            _, v2 = kb.dma("sync", sb_chunk3(ain[pj]), hbm_chunk_view(addin_hbm, b, j), sem=sS, new_group=False)
        wz = [(sS, max(v1, v2))] + list(wait_free)
        if ain is not None:
            kb.op("vector", lambda e: e.tensor_add(out=zb[pj][:], in0=rin[pj][:], in1=ain[pj][:]),
                  wait=wz, inc=[(s_zc, 1)])
        else:
            kb.op("vector", lambda e: e.tensor_copy(out=zb[pj][:], in_=rin[pj][:]),
                  wait=wz, inc=[(s_zc, 1)])
        fill_z.consumed[g] = s_zc.n
        return (s_zc, s_zc.n)
    fill_z.consumed = [0] * nch_total

    def out_view(g):
        b = b_list[g // NCH]; j = g % NCH
        return (hbm_chunk_view(out_f32_hbm, b, j), hbm_chunk_view(out_bf_hbm, b, j))

    return emit_ln_stream(kb, bufs, consts_sb, nch_total, fill_z, out_view, name)



def emit_ssm_phase(kb, ctx, nc, consts_sb, ps, whbm, x2bf_hbm, x2f32_hbm,
                   out_f32_hbm, out_bf_hbm, name="ssm"):
    CH = LN_CH
    NCH = T // CH
    s = kb.sem(name + "_chain")
    s_xin = kb.sem(name + "_xin")
    s_rin = kb.sem(name + "_resin")
    s_gp = kb.sem("dma_gpsimd")
    psW = ps["psW"]

    Wdtp_bf = ctx.enter_context(nc.sbuf_tensor(name + "_wdtp", [128, 8 * 64], BF16))
    WB_bf = ctx.enter_context(nc.sbuf_tensor(name + "_wB", [128, 8 * 16], BF16))
    WC_bf = ctx.enter_context(nc.sbuf_tensor(name + "_wC", [128, 8 * 16], BF16))
    Wxs_bf = ctx.enter_context(nc.sbuf_tensor(name + "_wxs", [128, 8 * 16], BF16))
    WdtT_sb = ctx.enter_context(nc.sbuf_tensor(name + "_wdt", [64, 1], F32))
    sc = ctx.enter_context(nc.sbuf_tensor(name + "_sc", [1, 64], F32))
    Dp1 = ctx.enter_context(nc.sbuf_tensor(name + "_dp1", [128, 8], F32))
    WoutT_bf = ctx.enter_context(nc.sbuf_tensor(name + "_wout", [16, 1024], BF16))

    for wn, wsb, cc in (("WdtpT", Wdtp_bf, 64), ("WBT", WB_bf, 16), ("WCT", WC_bf, 16), ("WxsT", Wxs_bf, 16)):
        kb.dma("gpsimd", wsb[:].rearrange("p (k c) -> p k c", c=cc),
               whbm[wn].rearrange("(k p) c -> p k c", p=128))
    kb.dma("sync", WdtT_sb[:], whbm["WdtT"])
    kb.dma("sync", sc[0:1, 0:1], whbm["bdt"])
    kb.dma("sync", sc[0:1, 16:32], whbm["A_log"])
    kb.dma("sync", Dp1[:], whbm["Dp_col"].rearrange("k p -> p k"), allow_slow_non_contiguous=True)
    _, gv = kb.dma("gpsimd", WoutT_bf[:], whbm["WoutT"])
    ldv = kb.sem("dma_sync").n
    s_dma0 = kb.sem("dma_sync")
    kb.chain("vector", lambda e: e.tensor_scalar_mul(out=sc[0:1, 1:2], in0=sc[0:1, 0:1], scalar1=-1.0),
             s, extra_wait=[(s_dma0, ldv), (s_gp, gv)], drain=True)
    kb.chain("scalar", lambda e: e.activation(out=sc[0:1, 32:48], in_=sc[0:1, 16:32], func=AF.Exp), s, drain=True)
    kb.chain("vector", lambda e: e.tensor_scalar_add(out=Dp1[:], in0=Dp1[:], scalar1=1.0), s, drain=True)
    negbdt = sc[0:1, 1:2]
    E16 = sc[0:1, 32:48]

    x2bf = [ctx.enter_context(nc.sbuf_tensor(f"{name}_x2bf{i}", [128, 8 * CH], BF16)) for i in range(1)]
    dtp_sb = ctx.enter_context(nc.sbuf_tensor(name + "_dtp", [64, CH], F32))
    Bp = ctx.enter_context(nc.sbuf_tensor(name + "_Bp", [16, CH], F32))
    xsp = ctx.enter_context(nc.sbuf_tensor(name + "_xsp", [16, CH], F32))
    Cp = ctx.enter_context(nc.sbuf_tensor(name + "_Cp", [16, CH], F32))
    rowsS = ctx.enter_context(nc.sbuf_tensor(name + "_rows", [1, 2 * CH], F32))
    Ab = ctx.enter_context(nc.sbuf_tensor(name + "_Ab", [16, CH], F32))
    BxP = ctx.enter_context(nc.sbuf_tensor(name + "_Bx", [16, CH], F32))
    hs = ctx.enter_context(nc.sbuf_tensor(name + "_hs", [16, CH], F32))
    carry = ctx.enter_context(nc.sbuf_tensor(name + "_carry", [16, 4], F32))
    hcbf = ctx.enter_context(nc.sbuf_tensor(name + "_hc", [16, T], BF16))

    bufs = make_ln_bufs(ctx, nc, name + "_ln", with_default_fill="none")
    bufs["ps"] = ps
    rin = bufs["rin"]; zb = bufs["zb"]
    s_ymm = kb.sem(name + "_ymm")
    s_yst = kb.sem(name + "_yst")

    RR = []
    proj_done = {}
    for b in range(NB):
        for j in range(NCH):
            pj = 0
            g = b * NCH + j
            wx = [(s, proj_done[g - 1])] if g >= 1 else []
            _, vx = kb.dma("sync", x2bf[pj][:].rearrange("p (k t) -> p k t", k=8),
                           hbm_chunk_view(x2bf_hbm, b, j), sem=s_xin, wait=wx)
            fw = [(s_xin, vx)]
            s_pe = kb.sem(name + "_pe")
            s_pv = kb.sem(name + "_pev")
            for pi, (proj_sb, wbf, CC, pw) in enumerate(
                    ((dtp_sb, Wdtp_bf, 64, psW[0]), (Bp, WB_bf, 16, psW[1]),
                     (xsp, Wxs_bf, 16, psW[2]), (Cp, WC_bf, 16, psW[3]))):
                wmm = list(fw)
                fw = []
                if g >= 1:
                    wmm.append((s, s.n))
                for k in range(8):
                    kb.op("tensor", lambda e, pw=pw, k=k, wbf=wbf, CC=CC, pj=pj: e.matmul(
                        pw[0:CC, :], wbf[:, k * CC:(k + 1) * CC],
                        x2bf[pj][:, k * CH:(k + 1) * CH],
                        start=(k == 0), stop=(k == 7)),
                        wait=(wmm if k == 0 else []),
                        inc=([(s_pe, 1)] if k == 7 else []))
                vpe = s_pe.n
                kb.op("scalar", lambda e, pw=pw, proj_sb=proj_sb, CC=CC: e.copy(
                    out=proj_sb[:], in_=pw[0:CC, :]),
                    wait=[(s_pe, vpe)], inc=[(s_pv, 1)])
            kb.chain("vector", lambda e: e.engine_nop(), s, extra_wait=[(s_pv, s_pv.n)])
            proj_done[g] = s.n
            t1 = rowsS[:, 0:CH]; dtn = rowsS[:, CH:2 * CH]
            kb.chain("tensor", lambda e: e.matmul(psW[0][0:1, :], WdtT_sb[:], dtp_sb[:],
                                                  start=True, stop=True), s)
            kb.chain("scalar", lambda e: e.activation(out=t1, in_=psW[0][0:1, :], func=AF.Sigmoid,
                                                      scale=-1.0, bias=negbdt), s)
            kb.chain("scalar", lambda e: e.activation(out=dtn, in_=t1, func=AF.Ln), s)
            kb.chain("tensor", lambda e: e.matmul(psW[0][0:16, :], E16, dtn, start=True, stop=True), s)
            kb.chain("scalar", lambda e: e.activation(out=Ab[:], in_=psW[0][0:16, :], func=AF.Exp), s)
            kb.chain("vector", lambda e: e.tensor_scalar_mul(out=t1, in0=dtn, scalar1=-1.0), s)
            kb.chain("tensor", lambda e: e.matmul(psW[1][0:16, :], consts_sb["ones_row"][:, 0:16],
                                                  t1, start=True, stop=True), s)
            kb.chain("vector", lambda e: e.tensor_mul(out=BxP[:], in0=Bp[:], in1=xsp[:]), s)
            kb.chain("vector", lambda e: e.tensor_mul(out=BxP[:], in0=BxP[:], in1=psW[1][0:16, :]), s)
            if j == 0:
                kb.chain("vector", lambda e: e.tensor_tensor_scan(
                    out=hs[:], data0=Ab[:], data1=BxP[:],
                    initial=0.0, op0=ALU.mult, op1=ALU.add), s)
            else:
                kb.chain("vector", lambda e: e.tensor_tensor_scan(
                    out=hs[:], data0=Ab[:], data1=BxP[:],
                    initial=carry[:, 0:1], op0=ALU.mult, op1=ALU.add), s)
            kb.chain("vector", lambda e: e.tensor_copy(out=carry[:, 0:1], in_=hs[:, CH - 1:CH]),
                     s, drain=True)
            kb.chain("vector", lambda e: e.tensor_mul(
                out=Ab[:], in0=hs[:], in1=Cp[:]), s)
            kb.chain("scalar", lambda e, j=j: e.copy(out=hcbf[:, j * CH:(j + 1) * CH],
                                                     in_=Ab[:]), s)
        kb.barrier()

        def fill_z(j, pj, wait_free, b=b):
            w = [(s_yst, fill_z.consumed[j - 2])] if j >= 2 else []
            _, v1 = kb.dma("sync", sb_chunk3(rin[pj]), hbm_chunk_view(x2f32_hbm, b, j),
                           sem=s_rin, wait=w)
            for k in range(8):
                par = k % 4
                wmm = [(s_yst, fill_z.last_stt[par])] if fill_z.last_stt[par] else []
                kb.op("tensor", lambda e, k=k, j=j, par=par: e.matmul(
                    psW[par][:], WoutT_bf[:, k * 128:(k + 1) * 128],
                    hcbf[:, j * CH:(j + 1) * CH], start=True, stop=True),
                    wait=wmm, inc=[(s_ymm, 1)])
                vmm = s_ymm.n
                sl = slice(k * CH, (k + 1) * CH)
                wst = [(s_ymm, vmm), (s_rin, v1)] + (list(wait_free) if k == 0 else [])
                kb.op("vector", lambda e, pj=pj, sl=sl, k=k, par=par: e.scalar_tensor_tensor(
                    out=zb[pj][:, sl], in0=rin[pj][:, sl], scalar=Dp1[:, k:k + 1],
                    in1=psW[par][:], op0=ALU.mult, op1=ALU.add),
                    wait=wst, inc=[(s_yst, 1)])
                fill_z.last_stt[par] = s_yst.n
            fill_z.consumed[j] = s_yst.n
            return (s_yst, s_yst.n)
        fill_z.consumed = [0] * NCH
        fill_z.last_stt = {0: 0, 1: 0, 2: 0, 3: 0}

        def out_view(j, b=b):
            return (hbm_chunk_view(out_f32_hbm, b, j), hbm_chunk_view(out_bf_hbm, b, j))

        RR.append(emit_ln_stream(kb, bufs, consts_sb, NCH, fill_z, out_view, f"{name}_ln{b}"))
        kb.barrier()
    return RR


def sim_run(nc, in_map, out_names):
    from concourse.bass_interp import CoreSim
    nc.finalize()
    sim = CoreSim(nc, require_finite=False, require_nnan=False)
    for k, v in in_map.items():
        view = sim.tensor(k)
        view[:] = v
    sim.simulate()
    return {k: np.array(sim.tensor(k)) for k in out_names}



def make_matmul_fill(kb, ctx, nc, ps, name, WT_bf, act_hbm, res_hbm, bufs, b):
    CH = LN_CH
    psW = ps["psW"]
    rin = bufs["rin"]
    ain = bufs["ain"]
    zb = bufs["zb"]
    s_in = [kb.sem("FI_in0"), kb.sem("FI_in1")]
    s_mm = kb.sem("FI_mm")
    s_v = kb.sem("FI_v")

    def fill_z(j, pj, wait_free):
        w = [(s_v, fill_z.consumed[j - 2])] if j >= 2 else []
        sS = s_in[pj]
        _, v1 = kb.dma("sync", sb_chunk3(rin[pj]), hbm_chunk_view(res_hbm, b, j), sem=sS, wait=w)
        _, v2 = kb.dma("sync", sb_chunk3(ain[pj]), hbm_chunk_view(act_hbm, b, j), sem=sS, new_group=False)
        for m in range(8):
            par = m % 4
            wmm = [(sS, v2)] if m == 0 else []
            if fill_z.last_tt[par]:
                wmm.append((s_v, fill_z.last_tt[par]))
            for k in range(8):
                kb.op("tensor", lambda e, m=m, k=k, par=par, pj=pj: e.matmul(
                    psW[par][:], WT_bf[:, k * 1024 + m * 128:k * 1024 + (m + 1) * 128],
                    ain[pj][:, k * CH:(k + 1) * CH],
                    start=(k == 0), stop=(k == 7)),
                    wait=(wmm if k == 0 else []),
                    inc=([(s_mm, 1)] if k == 7 else []))
            vmm = s_mm.n
            sl = slice(m * CH, (m + 1) * CH)
            wst = [(s_mm, vmm)] + (list(wait_free) if m == 0 else [])
            kb.op("vector", lambda e, pj=pj, sl=sl, par=par: e.tensor_add(
                out=zb[pj][:, sl], in0=rin[pj][:, sl], in1=psW[par][:]),
                wait=wst, inc=[(s_v, 1)])
            fill_z.last_tt[par] = s_v.n
        fill_z.consumed[j] = s_v.n
        return (s_v, s_v.n)
    fill_z.consumed = [0] * (T // CH)
    fill_z.last_tt = {0: 0, 1: 0, 2: 0, 3: 0}
    return fill_z



def emit_attn_phase(kb, ctx, nc, consts_sb, ps, whbm, x_res_hbm, attnpre_hbm,
                    out_f32_hbm, out_bf_hbm, name="att"):
    CH = 512
    NCH = T // CH
    NTT = T // 128
    psW = ps["psW"]
    s_mm = kb.sem(name + "_mm")
    s_e = kb.sem(name + "_e")
    s_g = kb.sem(name + "_g")
    s_v = kb.sem(name + "_v")
    s_wq = kb.sem(name + "_wq")
    s_x = kb.sem(name + "_x")
    s_st = kb.sem(name + "_stdma")

    with ExitStack() as actx:
        x_bf = actx.enter_context(nc.sbuf_tensor(name + "_xbf", [128, 8 * T], BF16))
        V_aug = actx.enter_context(nc.sbuf_tensor(name + "_vaug", [128, NTT * NH * 65], BF16))
        WvT_bf = actx.enter_context(nc.sbuf_tensor(name + "_wv", [128, 8 * CH], BF16))
        wqk = actx.enter_context(nc.sbuf_tensor(name + "_wqk", [128, 2 * 8 * 64], BF16))
        Qa = actx.enter_context(nc.sbuf_tensor(name + "_qa", [66, T], BF16))
        Ka = actx.enter_context(nc.sbuf_tensor(name + "_ka", [66, T], BF16))
        qsq = actx.enter_context(nc.sbuf_tensor(name + "_qsq", [64, CH], BF16))
        Pt = actx.enter_context(nc.sbuf_tensor(name + "_pt", [128, NTT * CH], BF16))
        stage = [actx.enter_context(nc.sbuf_tensor(f"{name}_stage{i}", [64, CH], BF16)) for i in range(2)]
        arow = actx.enter_context(nc.sbuf_tensor(name + "_arow", [66, 16 + 2 * CH], F32))
        ones66 = actx.enter_context(nc.sbuf_tensor(name + "_o66", [66, 128], F32))
        recbc = actx.enter_context(nc.sbuf_tensor(name + "_recbc", [64, CH], F32))

        kn1 = arow[64:65, 0:8]
        mkcell = arow[64:65, 8:9]
        qnrow = arow[64:65, 16:16 + CH]
        rec = arow[64:65, 16 + CH:16 + 2 * CH]

        kb.op("vector", lambda e: e.memset(ones66[:], 1.0))
        kb.op("vector", lambda e: e.memset(Ka[64:65, :], 1.0))
        kb.op("vector", lambda e: e.memset(
            V_aug[:].rearrange("p (s c) -> p s c", c=65)[:, :, 64:65], 1.0), inc=[(s_v, 1)])

        last_STmm_h = 0
        last_PVmm = {}
        stage_dma = {0: 0, 1: 0}
        stage_tt_bank = {0: 0, 1: 0}
        rec_val = {0: 0, 1: 0}
        stage_tt = 0

        for b in range(NB):
            _, vx = kb.dma("gpsimd", x_bf[:].rearrange("p (k t) -> p k t", k=8),
                           hbm_chunk_bview(x_res_hbm, b), sem=s_x)
            vau_evict = [0] * (2 * NTT)
            for half in range(2):
                wload = [] if (b == 0 and half == 0) else [(s_mm, s_mm.n)]
                _, vwv = kb.dma("gpsimd", WvT_bf[:].rearrange("p (k c) -> p k c", c=CH),
                                whbm["WvT"].rearrange("(k p) c -> p k c", p=128)[:, :, half * CH:(half + 1) * CH],
                                sem=s_wq, wait=wload)
                for i in range(NTT):
                    u = half * NTT + i
                    par = 2 + (u % 2)
                    wv = [(s_x, vx), (s_wq, vwv)] if i == 0 else []
                    if i == 0 and half == 0:
                        wv = wv + [(s_v, s_v.n)]
                    if u >= 2:
                        wv = wv + [(s_e, vau_evict[u - 2])]
                    if b > 0 and u < 2:
                        wv = wv + [(s_v, stage_tt)]
                    for k in range(8):
                        kb.op("tensor", lambda e, i=i, k=k, par=par: e.matmul(
                            psW[par][:],
                            x_bf[:, k * T + i * 128:k * T + (i + 1) * 128],
                            WvT_bf[:, k * CH:(k + 1) * CH],
                            start=(k == 0), stop=(k == 7)),
                            wait=(wv if k == 0 else []),
                            inc=([(s_mm, 1)] if k == 7 else []))
                    vmi = s_mm.n
                    kb.op("scalar", lambda e, i=i, half=half, par=par: e.copy(
                        out=V_aug[:, i * NH * 65 + half * 8 * 65:i * NH * 65 + (half * 8 + 8) * 65].rearrange(
                            "p (h c) -> p h c", c=65)[:, :, 0:64],
                        in_=psW[par][:].rearrange("p (h c) -> p h c", c=64)),
                        wait=[(s_mm, vmi)], inc=[(s_e, 1)])
                    vau_evict[u] = s_e.n
            kb.barrier()
            for h in range(NH):
                _, vw = kb.dma("gpsimd", wqk[:, 0:512].rearrange("p (k c) -> p k c", c=64),
                               whbm["WqT"].rearrange("(k p) c -> p k c", p=128)[:, :, h * 64:(h + 1) * 64],
                               sem=s_wq)
                _, vw2 = kb.dma("gpsimd", wqk[:, 512:1024].rearrange("p (k c) -> p k c", c=64),
                                whbm["WkT"].rearrange("(k p) c -> p k c", p=128)[:, :, h * 64:(h + 1) * 64],
                                sem=s_wq, new_group=False)
                for (dst, off, scale) in ((Qa, 0, 0.125), (Ka, 512, 1.0)):
                    for j in range(NCH):
                        par = j % 2
                        wpr = []
                        if j == 0 and off == 0:
                            wpr = [(s_wq, vw2), (s_mm, last_STmm_h), (s_e, s_e.n)]
                        for k in range(8):
                            kb.op("tensor", lambda e, j=j, k=k, par=par, off=off: e.matmul(
                                psW[par][0:64, :], wqk[:, off + k * 64:off + (k + 1) * 64],
                                x_bf[:, k * T + j * CH:k * T + (j + 1) * CH],
                                start=(k == 0), stop=(k == 7)),
                                wait=(wpr if k == 0 else []),
                                inc=([(s_mm, 1)] if k == 7 else []))
                        vpj = s_mm.n
                        kb.op("scalar", lambda e, j=j, par=par, dst=dst, scale=scale: e.activation(
                            out=dst[0:64, j * CH:(j + 1) * CH], in_=psW[par][0:64, :],
                            func=AF.Identity, scale=scale),
                            wait=[(s_mm, vpj)], inc=[(s_e, 1)])
                vqk = s_e.n
                last_cons = ("v", s_v.n)
                prev_vns = 0
                for j in range(NCH):
                    wsq = [(s_e, vqk)] if j == 0 else [(s_mm, prev_vns)]
                    kb.op("scalar", lambda e, j=j: e.activation(
                        out=qsq[:], in_=Ka[0:64, j * CH:(j + 1) * CH], func=AF.Square),
                        wait=wsq, inc=[(s_e, 1)])
                    vsq = s_e.n
                    wnm = [(s_e, vsq), (s_v if last_cons[0] == "v" else s_e, last_cons[1])]
                    kb.op("tensor", lambda e: e.matmul(
                        psW[2][64:65, :], consts_sb["ones_col_bf"][0:64, :],
                        qsq[:], start=True, stop=True),
                        wait=wnm, inc=[(s_mm, 1)])
                    vns = s_mm.n
                    prev_vns = vns
                    kb.op("vector", lambda e, j=j: e.tensor_reduce(
                        out=kn1[:, j:j + 1], in_=psW[2][64:65, :], axis=AXX, op=ALU.max),
                        wait=[(s_mm, vns)], inc=[(s_v, 1)])
                    last_cons = ("v", s_v.n)
                kb.op("vector", lambda e: e.tensor_reduce(
                    out=mkcell, in_=kn1, axis=AXX, op=ALU.max), inc=[(s_v, 1)])
                vmk = s_v.n
                kb.op("scalar", lambda e: e.activation(out=mkcell, in_=mkcell, func=AF.Sqrt),
                      wait=[(s_v, vmk)], inc=[(s_e, 1)], drain=True)
                vmk2 = s_e.n
                for j in range(NCH):
                    kb.op("scalar", lambda e, j=j: e.activation(
                        out=qsq[:], in_=Qa[0:64, j * CH:(j + 1) * CH], func=AF.Square),
                        wait=([(s_v, s_v.n)] if j == 0 else []),
                        inc=[(s_e, 1)])
                    vsq = s_e.n
                    wnm = [(s_e, vsq), (s_v if last_cons[0] == "v" else s_e, last_cons[1])]
                    kb.op("tensor", lambda e: e.matmul(
                        psW[2][64:65, :], consts_sb["ones_col_bf"][0:64, :],
                        qsq[:], start=True, stop=True),
                        wait=wnm, inc=[(s_mm, 1)])
                    vns = s_mm.n
                    kb.op("scalar", lambda e: e.activation(
                        out=qnrow, in_=psW[2][64:65, :], func=AF.Sqrt),
                        wait=[(s_mm, vns)], inc=[(s_e, 1)])
                    vqn = s_e.n
                    kb.op("vector", lambda e, j=j: e.tensor_scalar(
                        out=Qa[64:65, j * CH:(j + 1) * CH], in0=qnrow, scalar1=mkcell, scalar2=-1.0,
                        op0=ALU.mult, op1=ALU.mult),
                        wait=[(s_e, vqn)], inc=[(s_v, 1)], drain=True)
                    last_cons = ("v", s_v.n)
                vaug = s_v.n

                psP2 = ps["psP2"]
                psA = ps["psA"]
                exp_pair_prev = 0
                pv_prev = []
                norm_pend = None

                def emit_pv(item):
                    tkt, wpv0, blk_, ntk_, pvbank_, first, last = item
                    wpv = list(wpv0) if first else []
                    if first and stage_tt_bank[blk_ % 2]:
                        wpv.append((s_v, stage_tt_bank[blk_ % 2]))
                    kb.op("tensor", lambda e, tkt=tkt, ntk_=ntk_, h=h, pvbank_=pvbank_: e.matmul(
                        pvbank_[0:65, :],
                        V_aug[:, tkt * NH * 65 + h * 65:tkt * NH * 65 + (h + 1) * 65],
                        Pt[:, tkt * CH:(tkt + 1) * CH],
                        start=(tkt == 0), stop=(tkt == ntk_ - 1)),
                        wait=wpv, inc=[(s_mm, 1)])
                    last_PVmm[tkt] = s_mm.n
                    if last:
                        kb.op("vector", lambda e, pvbank_=pvbank_: e.reciprocal(
                            out=rec, in_=pvbank_[64:65, :]),
                            wait=[(s_mm, s_mm.n)], inc=[(s_v, 1)], drain=True)
                        rec_val[blk_ % 2] = s_v.n

                def emit_norm(nblk, nbank):
                    kb.op("tensor", lambda e: e.matmul(
                        psA[0:64, :], ones66[64:65, 0:64], rec, start=True, stop=True),
                        wait=[(s_v, rec_val[nblk % 2])], inc=[(s_mm, 1)])
                    vbc = s_mm.n
                    kb.op("vector", lambda e: e.tensor_copy(out=recbc[:], in_=psA[0:64, :]),
                          wait=[(s_mm, vbc)], inc=[(s_v, 1)])
                    vrecbc = s_v.n
                    wstg = [(s_v, vrecbc)]
                    if stage_dma[nblk % 2]:
                        wstg.append((s_st, stage_dma[nblk % 2]))
                    kb.op("vector", lambda e, nbank=nbank: e.tensor_mul(
                        out=stage[nblk % 2][:], in0=nbank[0:64, :], in1=recbc[:]),
                        wait=wstg, inc=[(s_v, 1)])
                    stage_tt_bank[nblk % 2] = s_v.n
                    _, sdv = kb.dma("sync", attnpre_hbm[b][h * 64:(h + 1) * 64, nblk * CH:(nblk + 1) * CH],
                                    stage[nblk % 2][:], sem=s_st, wait=[(s_v, stage_tt_bank[nblk % 2])])
                    stage_dma[nblk % 2] = sdv

                for blk in range(8):
                    ntk = 4 * (blk + 1)
                    pvbank = psW[2] if blk % 2 == 0 else psW[3]
                    pvw = []
                    exp_done = [0] * NTT
                    for p in range(ntk // 2):
                        for _ in range(2):
                            if pv_prev:
                                emit_pv(pv_prev.pop(0))
                        tA, tB = 2 * p, 2 * p + 1
                        wst = [(s_v, vaug)] if (blk == 0 and p == 0) else []
                        if p >= 1:
                            wst.append((s_e, exp_done[2 * p - 2]))
                        elif blk > 0:
                            wst.append((s_e, exp_pair_prev))
                        kb.op("tensor", lambda e, tA=tA, blk=blk: e.matmul(
                            psP2[:, 0:CH], Ka[0:65, tA * 128:(tA + 1) * 128],
                            Qa[0:65, blk * CH:(blk + 1) * CH], start=True, stop=True),
                            wait=wst)
                        kb.op("tensor", lambda e, tB=tB, blk=blk: e.matmul(
                            psP2[:, CH:2 * CH], Ka[0:65, tB * 128:(tB + 1) * 128],
                            Qa[0:65, blk * CH:(blk + 1) * CH], start=True, stop=True),
                            inc=[(s_mm, 1)])
                        vstm = s_mm.n
                        wexp = [(s_mm, vstm)]
                        for tx in (tA, tB):
                            if last_PVmm.get(tx):
                                wexp.append((s_mm, last_PVmm[tx]))
                        kb.op("scalar", lambda e, tA=tA: e.activation(
                            out=Pt[:, tA * CH:(tA + 2) * CH], in_=psP2[:, 0:2 * CH], func=AF.Exp),
                            wait=wexp, inc=[(s_e, 1)])
                        exp_done[tA] = exp_done[tB] = s_e.n
                        for tx in (tA, tB):
                            if tx * 128 >= blk * CH:
                                def mask(e, tx=tx, q0=blk * CH):
                                    i0 = tx * CH
                                    lo = tx * 128 - q0
                                    if lo > 0:
                                        e.memset(Pt[:, i0:i0 + lo], 0.0)
                                    return e.affine_select(
                                        out=Pt[:, i0 + lo:i0 + lo + 128], in_=Pt[:, i0 + lo:i0 + lo + 128],
                                        pattern=[[1, 128]], compare_op=ALU.is_ge, fill=0.0,
                                        base=0, channel_multiplier=-1)
                                kb.op("gpsimd", mask, wait=[(s_e, exp_done[tx])], inc=[(s_g, 1)])
                                pvw.append((tx, None, blk, ntk, pvbank, False, tx == ntk - 1))
                            else:
                                pvw.append((tx, None, blk, ntk, pvbank, False, tx == ntk - 1))
                    exp_pair_prev = exp_done[ntk - 1]
                    blkwait = [(s_e, exp_done[ntk - 1]), (s_g, s_g.n)]
                    pvw[0] = (pvw[0][0], blkwait, blk, ntk, pvbank, True, pvw[0][6])
                    if blk == 7:
                        last_STmm_h = s_mm.n
                    while pv_prev:
                        emit_pv(pv_prev.pop(0))
                    if norm_pend is not None:
                        emit_norm(*norm_pend)
                    pv_prev = pvw
                    norm_pend = (blk, pvbank)
                while pv_prev:
                    emit_pv(pv_prev.pop(0))
                emit_norm(*norm_pend)
                stage_tt = stage_tt_bank[1]
            kb.barrier()
        kb.barrier()

    WoT_bf = ctx.enter_context(nc.sbuf_tensor(name + "_wo", [128, 8 * 1024], BF16))
    _, vwo = kb.dma("gpsimd", WoT_bf[:].rearrange("p (k c) -> p k c", c=1024),
                    whbm["WoT"].rearrange("(k p) c -> p k c", p=128))
    bufs = make_ln_bufs(ctx, nc, name + "_ln", with_default_fill=BF16)
    bufs["ps"] = ps
    kb.wait("tensor", kb.sem("dma_gpsimd"), vwo)
    kb.barrier()
    RR = []
    for b in range(NB):
        fill = make_matmul_fill(kb, ctx, nc, ps, f"{name}_f{b}", WoT_bf, attnpre_hbm,
                                x_res_hbm, bufs, b)

        def out_view(j, b=b):
            return (hbm_chunk_view(out_f32_hbm, b, j), hbm_chunk_view(out_bf_hbm, b, j))

        RR.append(emit_ln_stream(kb, bufs, consts_sb, NCH, fill, out_view, f"{name}_ln{b}"))
        kb.barrier()
    return RR


def hbm_chunk_bview(hbm, b):
    return hbm[b].rearrange("(k p) t -> p k t", p=128)



def emit_ffn0_phase(kb, ctx, nc, consts_sb, ps, whbm, x1bf_hbm, x1f_hbm,
                    hsum_hbm, ffnout_hbm, out_f32_hbm, out_bf_hbm, name="ffn0", cut=None):
    CH = 512
    NCH = T // CH
    NT = T // 128
    psW = ps["psW"]
    s_mm = kb.sem(name + "_mm")
    s_e = kb.sem(name + "_e")
    s_v = kb.sem(name + "_v")
    s_c = kb.sem(name + "_c")
    s_wi = kb.sem(name + "_wi")
    s_hs = kb.sem(name + "_hs")
    s_ho = kb.sem(name + "_ho")
    s_fo = kb.sem(name + "_fo")
    s_xb = kb.sem(name + "_xb")

    val_rows = ctx.enter_context(nc.sbuf_tensor(name + "_valr", [1, 2 * T], F32))

    def _subA(b, nb_):
        with ExitStack() as actx:
            x1_bf = actx.enter_context(nc.sbuf_tensor(nb_ + "_x1bf", [128, 8 * T], BF16))
            gwt = actx.enter_context(nc.sbuf_tensor(nb_ + "_gwt", [128, 8 * 8], BF16))
            gbt = actx.enter_context(nc.sbuf_tensor(nb_ + "_gbt", [128, 256], F32))
            tie = actx.enter_context(nc.sbuf_tensor(nb_ + "_tie", [128, 256], F32))
            laT = actx.enter_context(nc.sbuf_tensor(nb_ + "_laT", [128, 8 * 128], BF16))
            lbT = actx.enter_context(nc.sbuf_tensor(nb_ + "_lbT", [128, DFF], BF16))
            lbneg = actx.enter_context(nc.sbuf_tensor(nb_ + "_lbn", [128, 128], BF16))
            Rl = actx.enter_context(nc.sbuf_tensor(nb_ + "_Rl", [8, 128], BF16))
            bi_sb = actx.enter_context(nc.sbuf_tensor(nb_ + "_bi", [128, 32], F32))
            garr = actx.enter_context(nc.sbuf_tensor(nb_ + "_gr", [128, 1440], F32))
            indc = actx.enter_context(nc.sbuf_tensor(nb_ + "_indc", [128, 2 * NT * 8], BF16))
            valc = actx.enter_context(nc.sbuf_tensor(nb_ + "_valc", [128, NT], F32))
            ind1T = actx.enter_context(nc.sbuf_tensor(nb_ + "_i1T", [8, T], BF16))
            ind2T = actx.enter_context(nc.sbuf_tensor(nb_ + "_i2T", [8, T], BF16))
            vct = actx.enter_context(nc.sbuf_tensor(nb_ + "_vct", [32, 128], F32))
            z1 = actx.enter_context(nc.sbuf_tensor(nb_ + "_z1", [128, T], BF16))
            z2 = actx.enter_context(nc.sbuf_tensor(nb_ + "_z2", [128, T], BF16))
            iexp = actx.enter_context(nc.sbuf_tensor(nb_ + "_iexp", [128, CH], BF16))
            valbc = actx.enter_context(nc.sbuf_tensor(nb_ + "_vbc", [128, 8 * CH], F32))
            wic = [actx.enter_context(nc.sbuf_tensor(f"{nb_}_wic{i}", [128, 8 * 128], BF16)) for i in range(2)]
            hst = [actx.enter_context(nc.sbuf_tensor(f"{nb_}_hst{i}", [128, 2 * CH], BF16)) for i in range(4)]
            h2p = [actx.enter_context(nc.sbuf_tensor(f"{nb_}_h2p{i}", [128, CH], F32)) for i in range(2)]
            ident = actx.enter_context(nc.sbuf_tensor(nb_ + "_id", [128, 128], BF16))
            identf = actx.enter_context(nc.sbuf_tensor(nb_ + "_idf", [128, 128], F32))

            kb.dma("sync", gbt[:], whbm["gate_b_t32"])
            kb.dma("sync", tie[:], consts_sb["tie32_hbm"], new_group=False)
            kb.dma("sync", Rl[:], consts_sb["Rl_hbm"], new_group=False)
            kb.dma("sync", bi_sb[:], whbm["bi"].rearrange("m p -> p m"),
                   new_group=False, allow_slow_non_contiguous=True)
            kb.dma("sync", ident[:], consts_sb["ident_hbm"], new_group=False)
            _, vld = kb.dma("sync", identf[:], consts_sb["identf_hbm"], new_group=False)
            kb.dma("gpsimd", laT[:].rearrange("p (k c) -> p k c", c=128),
                   whbm["loraAT"].rearrange("(k p) c -> p k c", p=128))
            kb.dma("gpsimd", lbT[:], whbm["loraBT"], new_group=False)
            _, vld2 = kb.dma("gpsimd", gwt[:].rearrange("p (k c) -> p k c", c=8),
                             whbm["gate_wT"].rearrange("(k p) c -> p k c", p=128), new_group=False)
            _, vxb = kb.dma("sync", x1_bf[:].rearrange("p (k t) -> p k t", k=8),
                            hbm_chunk_bview(x1bf_hbm, b), sem=s_xb)
            kb.wait("tensor", kb.sem("dma_sync"), vld)
            kb.wait("tensor", kb.sem("dma_gpsimd"), vld2)
            kb.wait("tensor", s_xb, vxb)
            kb.barrier()

            kb.chain("vector", lambda e: e.memset(garr[:], 0.0), s_c)
            lgA = garr[:, 0:256]; lselA = garr[:, 256:512]; maskedA = garr[:, 512:768]
            emA = garr[:, 768:1024]; subT = garr[:, 1024:1280]
            m1nA = garr[:, 1280:1312]; mm1A = garr[:, 1312:1344]; m2A = garr[:, 1344:1376]
            esA = garr[:, 1376:1408]; rrA = garr[:, 1408:1440]
            vstart = s_c.n
            for i in range(NT):
                for k in range(8):
                    first = (i == 0 and k == 0)
                    last = (i == NT - 1 and k == 7)
                    kb.op("tensor", lambda e, i=i, k=k, first=first, last=last: e.matmul(
                        psW[0][:, i * 8:(i + 1) * 8], x1_bf[:, k * T + i * 128:k * T + (i + 1) * 128],
                        gwt[:, k * 8:(k + 1) * 8], start=first, stop=last),
                        wait=([(s_c, vstart)] if first else []),
                        inc=([(s_c, 1)] if last else []))
            def gate_v1(e):
                e.tensor_add(out=lgA, in0=psW[0][:, 0:256], in1=gbt[:])
                e.tensor_add(out=lselA, in0=lgA, in1=tie[:])
                e.drain()
                e.tensor_reduce(out=m1nA, in_=lselA.rearrange("p (i e) -> p i e", e=8),
                                axis=AXX, op=ALU.max, negate=True)
                e.drain()
                e.tensor_scalar_mul(out=mm1A, in0=m1nA, scalar1=-1.0)
                e.drain()
                mmb = mm1A.rearrange("p (i o) -> p i o", o=1).broadcast_to([128, NT, 8])
                e.tensor_tensor(out=indc[:, 0:256], in0=lselA.rearrange("p (i e) -> p i e", e=8),
                                in1=mmb, op=ALU.is_ge)
                e.tensor_tensor(out=subT.rearrange("p (i e) -> p i e", e=8),
                                in0=lselA.rearrange("p (i e) -> p i e", e=8), in1=mmb, op=ALU.is_lt)
                e.drain()
                e.tensor_scalar(out=subT, in0=subT, scalar1=1e30, scalar2=-1e30,
                                op0=ALU.mult, op1=ALU.add)
                e.drain()
                e.tensor_add(out=maskedA, in0=lselA, in1=subT)
                e.drain()
                e.tensor_reduce(out=m2A, in_=maskedA.rearrange("p (i e) -> p i e", e=8),
                                axis=AXX, op=ALU.max)
                e.drain()
                e.tensor_tensor(out=indc[:, 256:512], in0=maskedA.rearrange("p (i e) -> p i e", e=8),
                                in1=m2A.rearrange("p (i o) -> p i o", o=1).broadcast_to([128, NT, 8]),
                                op=ALU.is_ge)
                return e.tensor_sub(out=subT.rearrange("p (i e) -> p i e", e=8),
                                    in0=lgA.rearrange("p (i e) -> p i e", e=8),
                                    in1=mm1A.rearrange("p (i o) -> p i o", o=1).broadcast_to([128, NT, 8]))
            kb.chain("vector", gate_v1, s_c, drain=True)
            kb.chain("scalar", lambda e: e.activation(out=emA, in_=subT, func=AF.Exp), s_c, drain=True)
            def gate_v2(e):
                e.tensor_reduce(out=esA, in_=emA.rearrange("p (i e) -> p i e", e=8),
                                axis=AXX, op=ALU.add)
                e.drain()
                e.reciprocal(out=rrA, in_=esA)
                e.tensor_add(out=subT, in0=indc[:, 0:256], in1=indc[:, 256:512])
                e.drain()
                e.tensor_mul(out=subT, in0=emA, in1=subT)
                e.drain()
                e.tensor_reduce(out=valc[:], in_=subT.rearrange("p (i e) -> p i e", e=8),
                                axis=AXX, op=ALU.add)
                e.drain()
                return e.tensor_mul(out=valc[:], in0=valc[:], in1=rrA)
            kb.chain("vector", gate_v2, s_c, drain=True)
            for i in range(NT):
                kb.chain("tensor", lambda e, i=i: e.transpose(
                    psW[0][0:8, 0:128].bitcast(BF16)[:, 0:128],
                    indc[:, i * 8:(i + 1) * 8], ident[:]), s_c)
                kb.chain("vector", lambda e, i=i: e.tensor_copy(
                    out=ind1T[:, i * 128:(i + 1) * 128], in_=psW[0][0:8, 0:128].bitcast(BF16)[:, 0:128]), s_c)
                kb.chain("tensor", lambda e, i=i: e.transpose(
                    psW[0][0:8, 0:128].bitcast(BF16)[:, 0:128],
                    indc[:, NT * 8 + i * 8:NT * 8 + (i + 1) * 8], ident[:]), s_c)
                kb.chain("vector", lambda e, i=i: e.tensor_copy(
                    out=ind2T[:, i * 128:(i + 1) * 128], in_=psW[0][0:8, 0:128].bitcast(BF16)[:, 0:128]), s_c)
            kb.chain("tensor", lambda e: e.transpose(psW[0][0:32, 0:128], valc[:], identf[:]), s_c)
            kb.chain("vector", lambda e: e.tensor_copy(out=vct[:], in_=psW[0][0:32, 0:128]), s_c)
            vchain = s_c.n
            s_vr = kb.sem(name + "_vr")
            _, vvr = kb.dma("sync", val_rows[0:1, b * T:(b + 1) * T], vct[:], sem=s_vr,
                            wait=[(s_c, vchain)])
            for j in range(NCH):
                kb.chain("tensor", lambda e, j=j, b=b: e.matmul(
                    psW[0][:, 0:CH], consts_sb["ones_row"][:],
                    val_rows[0:1, b * T + j * CH:b * T + (j + 1) * CH], start=True, stop=True), s_c,
                    extra_wait=([(s_vr, vvr)] if j == 0 else []))
                kb.chain("vector", lambda e, j=j: e.tensor_copy(out=valbc[:, j * CH:(j + 1) * CH],
                                                                in_=psW[0][:, 0:CH]), s_c)
            if cut == "gate":
                if "dbg_garr" in whbm:
                    kb.dma("sync", whbm["dbg_garr"][:, 0:1440], garr[:, :], wait=[(s_c, s_c.n)])
                    kb.dma("gpsimd", whbm["dbg_indc"], indc[:, :], wait=[(s_c, s_c.n)])
                kb.barrier()
                return
            for j in range(NCH):
                for k in range(8):
                    kb.chain("tensor", lambda e, j=j, k=k: e.matmul(
                        psW[0][:, 0:CH], laT[:, k * 128:(k + 1) * 128],
                        x1_bf[:, k * T + j * CH:k * T + (j + 1) * CH],
                        start=(k == 0), stop=(k == 7)), s_c)
                kb.chain("scalar", lambda e, j=j: e.copy(out=z1[:, j * CH:(j + 1) * CH],
                                                         in_=psW[0][:, 0:CH]), s_c)
                kb.chain("tensor", lambda e, j=j: e.matmul(
                    psW[0][:, 0:CH], Rl[:], ind2T[:, j * CH:(j + 1) * CH], start=True, stop=True), s_c)
                kb.chain("vector", lambda e, j=j: e.tensor_copy(out=iexp[:], in_=psW[0][:, 0:CH]), s_c)
                kb.chain("vector", lambda e, j=j: e.tensor_mul(
                    out=z2[:, j * CH:(j + 1) * CH], in0=z1[:, j * CH:(j + 1) * CH], in1=iexp[:]), s_c)
                kb.chain("tensor", lambda e, j=j: e.matmul(
                    psW[0][:, 0:CH], Rl[:], ind1T[:, j * CH:(j + 1) * CH], start=True, stop=True), s_c)
                kb.chain("vector", lambda e, j=j: e.tensor_copy(out=iexp[:], in_=psW[0][:, 0:CH]), s_c)
                kb.chain("vector", lambda e, j=j: e.tensor_mul(
                    out=z1[:, j * CH:(j + 1) * CH], in0=z1[:, j * CH:(j + 1) * CH], in1=iexp[:]), s_c)
            kb.barrier()
            if cut == "z":
                return

            relu_done = {2: 0, 3: 0}
            uB_done = {0: 0, 1: 0}
            h2p_free = [0, 0]
            hst_free = [0, 0, 0, 0]
            wiv = [0, 0]
            for m in range(32):
                pw = m % 2
                wwi = [(s_mm, wiv[pw])] if wiv[pw] else []
                _, vwi = kb.dma("gpsimd", wic[pw][:].rearrange("p (k c) -> p k c", c=128),
                                whbm["wiT"].rearrange("(k p) c -> p k c", p=128)[:, :, m * 128:(m + 1) * 128],
                                sem=s_wi, wait=wwi)
                kb.op("vector", lambda e, pw=pw, m=m: e.tensor_scalar_mul(
                    out=lbneg[:], in0=lbT[:, m * 128:(m + 1) * 128], scalar1=-1.0),
                    wait=[(s_v, s_v.n), (s_mm, s_mm.n)], inc=[(s_v, 1)], drain=True)
                vneg = s_v.n
                for j in range(NCH):
                    par = 2 + (j % 2)
                    parB = j % 2
                    hb = (2 * m + j) % 4
                    wsh = [(s_wi, vwi), (s_v, vneg)] if j == 0 else []
                    if relu_done[par]:
                        wsh.append((s_e, relu_done[par]))
                    for k in range(8):
                        kb.op("tensor", lambda e, j=j, k=k, par=par, pw=pw: e.matmul(
                            psW[par][:], wic[pw][:, k * 128:(k + 1) * 128],
                            x1_bf[:, k * T + j * CH:k * T + (j + 1) * CH],
                            start=(k == 0), stop=False),
                            wait=(wsh if k == 0 else []))
                    kb.op("tensor", lambda e, j=j, par=par, m=m: e.matmul(
                        psW[par][:], lbT[:, m * 128:(m + 1) * 128],
                        z1[:, j * CH:(j + 1) * CH], start=False, stop=True),
                        inc=[(s_mm, 1)])
                    v1 = s_mm.n
                    wB = [(s_v, uB_done[parB])] if uB_done[parB] else []
                    kb.op("tensor", lambda e, j=j, parB=parB, m=m: e.matmul(
                        psW[parB][:], lbT[:, m * 128:(m + 1) * 128],
                        z2[:, j * CH:(j + 1) * CH], start=True, stop=False),
                        wait=wB)
                    kb.op("tensor", lambda e, j=j, parB=parB, m=m: e.matmul(
                        psW[parB][:], lbneg[:], z1[:, j * CH:(j + 1) * CH],
                        start=False, stop=True), inc=[(s_mm, 1)])
                    v2 = s_mm.n
                    whst = [(s_mm, v1)]
                    if hst_free[hb]:
                        whst.append((s_hs, hst_free[hb]))
                    kb.op("scalar", lambda e, par=par, m=m, hb=hb: e.activation(
                        out=hst[hb][:, 0:CH], in_=psW[par][:], func=AF.Relu,
                        bias=bi_sb[:, m:m + 1]),
                        wait=whst, inc=[(s_e, 1)])
                    ve1 = s_e.n
                    kb.op("scalar", lambda e, parB=parB, hb=hb: e.copy(
                        out=hst[hb][:, CH:2 * CH], in_=psW[parB][:]),
                        wait=[(s_mm, v2)], inc=[(s_e, 1)])
                    veB = s_e.n
                    kb.op("vector", lambda e, par=par, hb=hb: e.tensor_add(
                        out=h2p[hb % 2][:], in0=psW[par][:], in1=hst[hb][:, CH:2 * CH]),
                        wait=[(s_e, veB), (s_v, h2p_free[hb % 2])], inc=[(s_v, 1)])
                    vh2p = s_v.n
                    uB_done[parB] = vh2p
                    kb.op("scalar", lambda e, m=m, hb=hb: e.activation(
                        out=hst[hb][:, CH:2 * CH], in_=h2p[hb % 2][:], func=AF.Relu,
                        bias=bi_sb[:, m:m + 1]),
                        wait=[(s_v, vh2p)], inc=[(s_e, 1)])
                    ve2 = s_e.n
                    relu_done[par] = ve2
                    def hsum_tt(e, hb=hb, j=j):
                        e.tensor_add(out=hst[hb][:, 0:CH], in0=hst[hb][:, 0:CH], in1=hst[hb][:, CH:2 * CH])
                        return e.tensor_mul(out=hst[hb][:, 0:CH], in0=hst[hb][:, 0:CH],
                                            in1=valbc[:, j * CH:(j + 1) * CH])
                    kb.op("vector", hsum_tt, wait=[(s_e, ve2)], inc=[(s_v, 1)])
                    vtt = s_v.n
                    h2p_free[hb % 2] = vtt
                    _, vhs = kb.dma("sync", hsum_hbm[b][m * 128:(m + 1) * 128, j * CH:(j + 1) * CH],
                                    hst[hb][:, 0:CH], sem=s_hs, wait=[(s_v, vtt)])
                    hst_free[hb] = vhs
                wiv[pw] = s_mm.n
            kb.barrier()


    def _subB(b, nb_):
        with ExitStack() as bctx:
            woT_bf = bctx.enter_context(nc.sbuf_tensor(nb_ + "_wo", [128, 32 * 1024], BF16))
            hin = [bctx.enter_context(nc.sbuf_tensor(f"{nb_}_hin{i}", [128, 32 * CH], BF16)) for i in range(2)]
            bo_sb = bctx.enter_context(nc.sbuf_tensor(nb_ + "_bo", [1, 1024], F32))
            fst = [bctx.enter_context(nc.sbuf_tensor(f"{nb_}_fst{i}", [128, CH], BF16)) for i in range(4)]
            _, vwo = kb.dma("gpsimd", woT_bf[:].rearrange("p (k c) -> p k c", c=1024),
                            whbm["woT"].rearrange("(k p) c -> p k c", p=128))
            _, vbo = kb.dma("sync", bo_sb[:], whbm["bo_row"])
            kb.op("vector", lambda e, b=b: e.tensor_scalar_mul(
                out=val_rows[0:1, b * T:(b + 1) * T], in0=val_rows[0:1, b * T:(b + 1) * T],
                scalar1=2.0), wait=[(kb.sem("dma_sync"), kb.sem("dma_sync").n)], inc=[(s_v, 1)])
            vv2 = s_v.n
            kb.wait("tensor", kb.sem("dma_gpsimd"), vwo)
            kb.wait("tensor", kb.sem("dma_sync"), vbo)
            kb.wait("tensor", s_v, vv2)
            kb.barrier()
            evict_done = {0: 0, 1: 0, 2: 0, 3: 0}
            fst_free = [0, 0, 0, 0]
            hin_free = [0, 0]
            for j in range(NCH):
                pj = j % 2
                whi = [(s_mm, hin_free[pj])] if hin_free[pj] else []
                _, vhi = kb.dma("sync", hin[pj][:].rearrange("p (k t) -> p k t", k=32),
                                hsum_hbm[b].rearrange("(k p) t -> p k t", p=128)[:, :, j * CH:(j + 1) * CH],
                                sem=s_ho, wait=whi)
                for mo in range(8):
                    par = mo % 4
                    fb = (8 * j + mo) % 4
                    wmm = [(s_ho, vhi)] if mo == 0 else []
                    if evict_done[par]:
                        wmm.append((s_e, evict_done[par]))
                    for k in range(32):
                        kb.op("tensor", lambda e, mo=mo, k=k, par=par, pj=pj: e.matmul(
                            psW[par][:], woT_bf[:, k * 1024 + mo * 128:k * 1024 + (mo + 1) * 128],
                            hin[pj][:, k * CH:(k + 1) * CH],
                            start=(k == 0), stop=False),
                            wait=(wmm if k == 0 else []))
                    kb.op("tensor", lambda e, mo=mo, j=j, b=b, par=par: e.matmul(
                        psW[par][:], bo_sb[0:1, mo * 128:(mo + 1) * 128],
                        val_rows[0:1, b * T + j * CH:b * T + (j + 1) * CH],
                        start=False, stop=True), inc=[(s_mm, 1)])
                    vmm = s_mm.n
                    wev = [(s_mm, vmm)]
                    if fst_free[fb]:
                        wev.append((s_fo, fst_free[fb]))
                    kb.op("scalar", lambda e, par=par, fb=fb: e.copy(
                        out=fst[fb][:], in_=psW[par][:]),
                        wait=wev, inc=[(s_e, 1)])
                    evict_done[par] = s_e.n
                    _, vfo = kb.dma("sync", ffnout_hbm[b][mo * 128:(mo + 1) * 128, j * CH:(j + 1) * CH],
                                    fst[fb][:], sem=s_fo, wait=[(s_e, evict_done[par])])
                    fst_free[fb] = vfo
                hin_free[pj] = s_mm.n
            kb.barrier()


    for b in range(NB):
        _subA(b, f"{name}{b}")
        if cut is None:
            _subB(b, f"{name}{b}")

    if "dbg_val" in whbm:
        kb.dma("sync", whbm["dbg_val"], val_rows[0:1, :])
        kb.barrier()

    if cut is not None:
        return None
    return emit_ln_phase(kb, ctx, nc, consts_sb, ps, x1f_hbm, ffnout_hbm, BF16,
                         out_f32_hbm, out_bf_hbm, list(range(NB)), name + "_ln")



def emit_ffn1_phase(kb, ctx, nc, consts_sb, ps, whbm, x3bf_hbm, x3f_hbm,
                    qk_hbm, g_hbm, out_f32_hbm, out_bf_hbm, name="ffn1"):
    CH = 512
    NCH = T // CH
    NT = T // 128
    psW = ps["psW"]
    s_mm = kb.sem(name + "_mm")
    s_e = kb.sem(name + "_e")
    s_v = kb.sem(name + "_v")
    s_c = kb.sem(name + "_c")
    s_ld = kb.sem(name + "_ld")
    s_qs = kb.sem(name + "_qs")
    s_ks = kb.sem(name + "_ks")
    s_go = kb.sem(name + "_go")
    s_gi = kb.sem(name + "_gi")

    with ExitStack() as s1:
        x3bf = s1.enter_context(nc.sbuf_tensor(name + "_x3", [128, 8 * T], BF16))
        WQb = s1.enter_context(nc.sbuf_tensor(name + "_wq", [128, 8 * 1024], BF16))
        WKb = s1.enter_context(nc.sbuf_tensor(name + "_wk", [128, 8 * 1024], BF16))
        WIb = s1.enter_context(nc.sbuf_tensor(name + "_wi", [128, 8 * 1024], BF16))
        qc = [s1.enter_context(nc.sbuf_tensor(f"{name}_qc{i}", [128, 8 * CH], BF16)) for i in range(2)]
        for wn, wsb in (("WQT", WQb), ("WKT", WKb), ("WinvT", WIb)):
            kb.dma("gpsimd", wsb[:].rearrange("p (k c) -> p k c", c=1024),
                   whbm[wn].rearrange("(k p) c -> p k c", p=128))
        vw = kb.sem("dma_gpsimd").n
        kb.wait("tensor", kb.sem("dma_gpsimd"), vw)
        kb.barrier()

        def s1_emit(b):
            _, vx = kb.dma("sync", x3bf[:].rearrange("p (k t) -> p k t", k=8),
                           hbm_chunk_bview(x3bf_hbm, b), sem=s_ld)
            evd = {2: 0, 3: 0}
            qcfree = [0, 0]
            qc_dma = [0, 0]
            for src_i, (wsb, dst_i) in enumerate(((WQb, 0), (WKb, 1))):
                for j in range(NCH):
                    pj = j % 2
                    wq0 = [(s_ld, vx)] if (j == 0 and src_i == 0) else []
                    if qcfree[pj]:
                        wq0 = wq0 + [(s_mm, qcfree[pj])]
                    for m in range(8):
                        par = 2 + (m % 2)
                        wmm = list(wq0) if m == 0 else []
                        if evd[par]:
                            wmm.append((s_e, evd[par]))
                        for k in range(8):
                            kb.op("tensor", lambda e, m=m, k=k, j=j, par=par, wsb=wsb: e.matmul(
                                psW[par][:], wsb[:, k * 1024 + m * 128:k * 1024 + (m + 1) * 128],
                                x3bf[:, k * T + j * CH:k * T + (j + 1) * CH],
                                start=(k == 0), stop=(k == 7)),
                                wait=(wmm if k == 0 else []),
                                inc=([(s_mm, 1)] if k == 7 else []))
                        vm = s_mm.n
                        kb.op("scalar", lambda e, m=m, par=par, pj=pj: e.copy(
                            out=qc[pj][:, m * CH:(m + 1) * CH], in_=psW[par][:]),
                            wait=([(s_mm, vm), (s_qs, qc_dma[pj])] if m == 0 else [(s_mm, vm)]),
                            inc=[(s_e, 1)])
                        evd[par] = s_e.n
                    vch = s_e.n
                    _, vq = kb.dma("sync", hbm_chunk_view(qk_hbm[dst_i], b, j),
                                   sb_chunk3(qc[pj]), sem=s_qs, wait=[(s_e, vch)])
                    for m in range(8):
                        par = 2 + (m % 2)
                        wmm = [(s_e, evd[par])]
                        for k in range(8):
                            kb.op("tensor", lambda e, m=m, k=k, par=par, pj=pj: e.matmul(
                                psW[par][:], WIb[:, k * 1024 + m * 128:k * 1024 + (m + 1) * 128],
                                qc[pj][:, k * CH:(k + 1) * CH],
                                start=(k == 0), stop=(k == 7)),
                                wait=(wmm if k == 0 else []),
                                inc=([(s_mm, 1)] if k == 7 else []))
                        vm = s_mm.n
                        kb.op("scalar", lambda e, m=m, par=par, pj=pj: e.copy(
                            out=qc[pj][:, m * CH:(m + 1) * CH], in_=psW[par][:]),
                            wait=[(s_mm, vm), (s_qs, vq)] if m == 0 else [(s_mm, vm)],
                            inc=[(s_e, 1)])
                        evd[par] = s_e.n
                    vch = s_e.n
                    _, vq2 = kb.dma("sync", hbm_chunk_view(qk_hbm[2 + dst_i], b, j),
                                    sb_chunk3(qc[pj]), sem=s_qs, wait=[(s_e, vch)])
                    qc_dma[pj] = vq2
                    qcfree[pj] = s_mm.n
            kb.barrier()
        for b in range(NB):
            s1_emit(b)
    kb.barrier()

    with ExitStack() as s2:
        Vb = s2.enter_context(nc.sbuf_tensor(name + "_Vb", [128, 8 * 512], BF16))
        Wb = s2.enter_context(nc.sbuf_tensor(name + "_Wb", [128, 8 * 512], BF16))
        rwU = s2.enter_context(nc.sbuf_tensor(name + "_rwU", [128, 8 * 1024], BF16))
        rw1 = rwU
        Ub = rwU
        rw2 = s2.enter_context(nc.sbuf_tensor(name + "_rw2", [128, 8 * 8], BF16))
        rb1 = s2.enter_context(nc.sbuf_tensor(name + "_rb1", [128, 8], F32))
        comb = s2.enter_context(nc.sbuf_tensor(name + "_cmb", [128, 8], F32))
        bexp = s2.enter_context(nc.sbuf_tensor(name + "_bex", [128, 8], F32))
        bk = s2.enter_context(nc.sbuf_tensor(name + "_bk", [8, 1024], BF16))
        Rk = s2.enter_context(nc.sbuf_tensor(name + "_Rk", [8, 512], BF16))
        alr = s2.enter_context(nc.sbuf_tensor(name + "_al", [1, 4], F32))
        acol = s2.enter_context(nc.sbuf_tensor(name + "_ac", [8, 4], F32))
        identf = s2.enter_context(nc.sbuf_tensor(name + "_idf", [128, 128], F32))
        xV = s2.enter_context(nc.sbuf_tensor(name + "_xV", [128, 4 * T], BF16))
        yW = s2.enter_context(nc.sbuf_tensor(name + "_yW", [128, 4 * T], BF16))
        prodw = s2.enter_context(nc.sbuf_tensor(name + "_pw", [128, 4 * T], BF16))
        hc = s2.enter_context(nc.sbuf_tensor(name + "_hc", [128, 8 * CH], BF16))
        qsb = [s2.enter_context(nc.sbuf_tensor(f"{name}_qsb{i}", [128, 8 * CH], BF16)) for i in range(2)]
        warr = s2.enter_context(nc.sbuf_tensor(name + "_wa", [128, 512], F32))
        wT = s2.enter_context(nc.sbuf_tensor(name + "_wT", [8, T], BF16))
        invn = s2.enter_context(nc.sbuf_tensor(name + "_in", [128, CH], F32))
        if "invn_sb" not in consts_sb:
            c_invn = s2.enter_context(nc.sbuf_tensor(name + "_cinvn", [1, T], F32))
            kb.dma("sync", c_invn[:], consts_sb["invn_hbm"])
            consts_sb["invn_sb"] = c_invn
        gst = [s2.enter_context(nc.sbuf_tensor(f"{name}_gst{i}", [128, CH], BF16)) for i in range(4)]
        gfin = [s2.enter_context(nc.sbuf_tensor(f"{name}_gf{i}", [128, 8 * CH], BF16)) for i in range(2)]

        kb.dma("sync", rb1[:], whbm["rb1"].rearrange("k p -> p k"), allow_slow_non_contiguous=True)
        kb.dma("sync", comb[:], whbm["rb2_t"], new_group=False)
        kb.dma("sync", bexp[:], whbm["expb_t"], new_group=False)
        kb.dma("sync", Rk[:], consts_sb["Rk_hbm"], new_group=False)
        kb.dma("sync", alr[0:1, 0:1], whbm["alpha"], new_group=False)
        _, vld0 = kb.dma("sync", identf[:], consts_sb["identf_hbm"], new_group=False)
        _, vld1 = kb.dma("gpsimd", rw2[:].rearrange("p (k c) -> p k c", c=8),
                         whbm["rw2T"].rearrange("(k p) c -> p k c", p=128))
        kb.wait("vector", kb.sem("dma_sync"), vld0)
        kb.wait("vector", kb.sem("dma_gpsimd"), vld1)
        kb.chain("vector", lambda e: e.tensor_add(out=comb[:], in0=comb[:], in1=bexp[:]), s_c, drain=True)
        kb.chain("tensor", lambda e: e.matmul(psW[0][0:8, 0:1], consts_sb["ones_row"][:, 0:8],
                                              alr[0:1, 0:1], start=True, stop=True), s_c)
        kb.chain("vector", lambda e: e.tensor_copy(out=acol[:, 0:1], in_=psW[0][0:8, 0:1]), s_c, drain=True)
        kb.barrier()

        def branch_emit(b, br):
            qsrc = qk_hbm[0 if br == 0 else 2]
            ksrc = qk_hbm[1 if br == 0 else 3]
            VT = "Vf_T" if br == 0 else "Vi_T"
            WT = "Wf_T" if br == 0 else "Wi_T"
            UF = "Uf_flat" if br == 0 else "Ui_flat"
            BK = "bf_k" if br == 0 else "bi_k"
            kb.dma("gpsimd", Vb[:].rearrange("p (k c) -> p k c", c=512),
                   whbm[VT].rearrange("(k p) c -> p k c", p=128))
            kb.dma("gpsimd", Wb[:].rearrange("p (k c) -> p k c", c=512),
                   whbm[WT].rearrange("(k p) c -> p k c", p=128), new_group=False)
            kb.dma("gpsimd", rw1[:].rearrange("p (k c) -> p k c", c=1024),
                   whbm["rw1T"].rearrange("(k p) c -> p k c", p=128), new_group=False)
            _, vwl = kb.dma("gpsimd", bk[:], whbm[BK], new_group=False)
            kb.wait("tensor", kb.sem("dma_gpsimd"), vwl)
            kb.barrier()
            evd = {2: 0, 3: 0}
            for j in range(NCH):
                pj = j % 2
                _, vq = kb.dma("sync", sb_chunk3(qsb[pj]), hbm_chunk_view(qsrc, b, j), sem=s_qs,
                               wait=[(s_mm, s_mm.n)])
                for m in range(4):
                    par = 2 + (m % 2)
                    wmm = [(s_qs, vq)] if m == 0 else []
                    if evd[par]:
                        wmm.append((s_e, evd[par]))
                    for k in range(8):
                        kb.op("tensor", lambda e, m=m, k=k, par=par, pj=pj: e.matmul(
                            psW[par][:], Vb[:, k * 512 + m * 128:k * 512 + (m + 1) * 128],
                            qsb[pj][:, k * CH:(k + 1) * CH],
                            start=(k == 0), stop=(k == 7)),
                            wait=(wmm if k == 0 else []),
                            inc=([(s_mm, 1)] if k == 7 else []))
                    vm = s_mm.n
                    kb.op("scalar", lambda e, m=m, j=j, par=par: e.copy(
                        out=xV[:, m * T + j * CH:m * T + (j + 1) * CH], in_=psW[par][:]),
                        wait=[(s_mm, vm)], inc=[(s_e, 1)])
                    evd[par] = s_e.n
                for m in range(8):
                    par = 2 + (m % 2)
                    wmm = [(s_e, evd[par])] + ([(s_c, s_c.n)] if m == 0 else [])
                    for k in range(8):
                        kb.op("tensor", lambda e, m=m, k=k, par=par, pj=pj: e.matmul(
                            psW[par][:], rw1[:, k * 1024 + m * 128:k * 1024 + (m + 1) * 128],
                            qsb[pj][:, k * CH:(k + 1) * CH],
                            start=(k == 0), stop=(k == 7)),
                            wait=(wmm if k == 0 else []),
                            inc=([(s_mm, 1)] if k == 7 else []))
                    vm = s_mm.n
                    kb.op("scalar", lambda e, m=m, par=par: e.activation(
                        out=hc[:, m * CH:(m + 1) * CH], in_=psW[par][:], func=AF.Gelu,
                        bias=rb1[:, m:m + 1]),
                        wait=[(s_mm, vm)], inc=[(s_e, 1)])
                    evd[par] = s_e.n
                vh = s_e.n
                for tt in range(4):
                    ti = j * 4 + tt
                    for k in range(8):
                        kb.chain("tensor", lambda e, tt=tt, k=k: e.matmul(
                            psW[0][:, 0:8], hc[:, k * CH + tt * 128:k * CH + (tt + 1) * 128],
                            rw2[:, k * 8:(k + 1) * 8], start=(k == 0), stop=(k == 7)), s_c,
                            extra_wait=([(s_e, vh)] if (tt == 0 and k == 0) else []))
                    def smax(e, ti=ti):
                        e.tensor_add(out=warr[:, 0:8], in0=psW[0][:, 0:8], in1=comb[:])
                        e.drain()
                        e.tensor_reduce(out=warr[:, 12:13], in_=warr[:, 0:8], axis=AXX,
                                        op=ALU.max, negate=True)
                        return e.drain()
                    kb.chain("vector", smax, s_c, drain=True)
                    kb.chain("scalar", lambda e: e.activation(
                        out=warr[:, 16:24], in_=warr[:, 0:8], func=AF.Exp,
                        bias=warr[:, 12:13]), s_c, drain=True)
                    def smax2(e, ti=ti):
                        e.tensor_reduce(out=warr[:, 24:25], in_=warr[:, 16:24], axis=AXX, op=ALU.add)
                        e.drain()
                        e.reciprocal(out=warr[:, 28:29], in_=warr[:, 24:25])
                        e.drain()
                        return e.tensor_scalar(out=warr[:, 32 + (ti % 32) * 8:32 + (ti % 32) * 8 + 8]
                                               if False else warr[:, 32:40],
                                               in0=warr[:, 16:24], scalar1=warr[:, 28:29],
                                               scalar2=None, op0=ALU.mult)
                    kb.chain("vector", smax2, s_c, drain=True)
                    kb.chain("tensor", lambda e, ti=ti: e.transpose(
                        psW[0][0:8, 0:128], warr[:, 32:40], identf[:]), s_c)
                    kb.chain("vector", lambda e, ti=ti: e.tensor_copy(
                        out=wT[:, ti * 128:(ti + 1) * 128], in_=psW[0][0:8, 0:128]), s_c, drain=True)
            evd = {2: 0, 3: 0}
            for j in range(NCH):
                pj = j % 2
                _, vk = kb.dma("sync", sb_chunk3(qsb[pj]), hbm_chunk_view(ksrc, b, j), sem=s_ks,
                               wait=[(s_mm, s_mm.n)])
                for m in range(4):
                    par = 2 + (m % 2)
                    wmm = [(s_ks, vk)] if m == 0 else []
                    if evd[par]:
                        wmm.append((s_e, evd[par]))
                    for k in range(8):
                        kb.op("tensor", lambda e, m=m, k=k, par=par, pj=pj: e.matmul(
                            psW[par][:], Wb[:, k * 512 + m * 128:k * 512 + (m + 1) * 128],
                            qsb[pj][:, k * CH:(k + 1) * CH],
                            start=(k == 0), stop=(k == 7)),
                            wait=(wmm if k == 0 else []),
                            inc=([(s_mm, 1)] if k == 7 else []))
                    vm = s_mm.n
                    kb.op("scalar", lambda e, m=m, j=j, par=par: e.copy(
                        out=yW[:, m * T + j * CH:m * T + (j + 1) * CH], in_=psW[par][:]),
                        wait=[(s_mm, vm)], inc=[(s_e, 1)])
                    evd[par] = s_e.n
            vyw = s_e.n
            for m in range(4):
                kb.chain("vector", lambda e, m=m: e.tensor_tensor_scan(
                    out=yW[:, m * T:(m + 1) * T], data0=yW[:, m * T:(m + 1) * T],
                    data1=yW[:, m * T:(m + 1) * T], initial=0.0,
                    op0=ALU.add, op1=ALU.bypass), s_c,
                    extra_wait=([(s_e, vyw)] if m == 0 else []))
            if br == 1:
                kb.chain("vector", lambda e: e.tensor_scalar(
                    out=wT[:], in0=wT[:], scalar1=acol[:, 0:1], scalar2=None,
                    op0=ALU.mult), s_c, drain=True)
            for m in range(4):
                for j in range(NCH):
                    par = 2 + (j % 2)
                    kb.chain("tensor", lambda e, m=m, j=j, par=par: e.matmul(
                        psW[par][:], Rk[:, m * 128:(m + 1) * 128],
                        wT[:, j * CH:(j + 1) * CH], start=True, stop=True), s_c)
                    def prod_tt(e, m=m, j=j, par=par):
                        e.tensor_mul(out=prodw[:, m * T + j * CH:m * T + (j + 1) * CH],
                                     in0=xV[:, m * T + j * CH:m * T + (j + 1) * CH],
                                     in1=yW[:, m * T + j * CH:m * T + (j + 1) * CH])
                        return e.tensor_mul(out=prodw[:, m * T + j * CH:m * T + (j + 1) * CH],
                                            in0=prodw[:, m * T + j * CH:m * T + (j + 1) * CH],
                                            in1=psW[par][:])
                    kb.chain("vector", prod_tt, s_c)
            vprod = s_c.n
            _, vub = kb.dma("gpsimd", Ub[:, 0:4 * 1024].rearrange("p (k c) -> p k c", c=1024),
                            whbm[UF].rearrange("(k p) c -> p k c", p=128),
                            wait=[(s_c, vprod), (s_mm, s_mm.n)])
            kb.wait("tensor", kb.sem("dma_gpsimd"), vub)
            evd = {2: 0, 3: 0}
            gfree = [0, 0, 0, 0]
            for j in range(NCH):
                pj = j % 2
                kb.chain("tensor", lambda e, j=j: e.matmul(
                    psW[0][:, 0:CH], consts_sb["ones_row"][:],
                    consts_sb["invn_sb"][0:1, j * CH:(j + 1) * CH], start=True, stop=True), s_c)
                kb.chain("vector", lambda e: e.tensor_copy(out=invn[:], in_=psW[0][:, 0:CH]), s_c,
                         extra_wait=[(s_v, s_v.n)])
                vin = s_c.n
                if br == 1:
                    _, vgf = kb.dma("sync", sb_chunk3(gfin[pj]), hbm_chunk_view(g_hbm, b, j),
                                    sem=s_go, wait=[(s_v, s_v.n)])
                    branch_emit.gf_dma[j % 2] = vgf
                for m in range(8):
                    par = 2 + (m % 2)
                    gb = (8 * j + m) % 4
                    wmm = [(s_c, vin)] if m == 0 else []
                    if evd[par]:
                        wmm.append((s_v, evd[par]))
                    for k in range(4):
                        kb.op("tensor", lambda e, m=m, k=k, j=j, par=par: e.matmul(
                            psW[par][:], Ub[:, k * 1024 + m * 128:k * 1024 + (m + 1) * 128],
                            prodw[:, k * T + j * CH:k * T + (j + 1) * CH],
                            start=(k == 0), stop=False),
                            wait=(wmm if k == 0 else []))
                    kb.op("tensor", lambda e, m=m, j=j, par=par: e.matmul(
                        psW[par][:], bk[:, m * 128:(m + 1) * 128],
                        wT[:, j * CH:(j + 1) * CH], start=False, stop=True),
                        inc=[(s_mm, 1)])
                    vm = s_mm.n
                    wtv = [(s_mm, vm)]
                    if gfree[gb]:
                        wtv.append((s_gi, gfree[gb]))
                    if br == 0:
                        kb.op("vector", lambda e, par=par, gb=gb: e.tensor_mul(
                            out=gst[gb][:], in0=psW[par][:], in1=invn[:]),
                            wait=wtv, inc=[(s_v, 1)])
                    else:
                        def gcomb(e, par=par, gb=gb, m=m, pj=pj):
                            e.tensor_mul(out=gst[gb][:], in0=psW[par][:], in1=invn[:])
                            return e.tensor_add(out=gst[gb][:], in0=gst[gb][:],
                                                in1=gfin[pj][:, m * CH:(m + 1) * CH])
                        wtv2 = wtv + ([(s_go, branch_emit.gf_dma[j % 2])] if m == 0 else [])
                        kb.op("vector", gcomb, wait=wtv2, inc=[(s_v, 1)])
                    evd[par] = s_v.n
                    _, vg = kb.dma("sync", g_hbm[b][m * 128:(m + 1) * 128, j * CH:(j + 1) * CH],
                                   gst[gb][:], sem=s_gi, wait=[(s_v, evd[par])])
                    gfree[gb] = vg
            kb.barrier()
        branch_emit.gf_dma = [0, 0]

        for b in range(NB):
            branch_emit.gf_dma = [0, 0]
            branch_emit(b, 0)
            branch_emit(b, 1)
    kb.barrier()

    WOb = ctx.enter_context(nc.sbuf_tensor(name + "_wo", [128, 8 * 1024], BF16))
    _, vwo = kb.dma("gpsimd", WOb[:].rearrange("p (k c) -> p k c", c=1024),
                    whbm["WOT"].rearrange("(k p) c -> p k c", p=128))
    bufs = make_ln_bufs(ctx, nc, name + "_ln", with_default_fill=BF16)
    bufs["ps"] = ps
    kb.wait("tensor", kb.sem("dma_gpsimd"), vwo)
    kb.barrier()
    RR = []
    for b in range(NB):
        fill = make_matmul_fill(kb, ctx, nc, ps, f"{name}_f{b}", WOb, g_hbm,
                                x3f_hbm, bufs, b)

        def out_view(j, b=b):
            return (hbm_chunk_view(out_f32_hbm, b, j), hbm_chunk_view(out_bf_hbm, b, j))

        RR.append(emit_ln_stream(kb, bufs, consts_sb, NCH, fill, out_view, f"{name}_ln{b}"))
        kb.barrier()
    return RR



W1 = ["WqT", "WkT", "WvT", "WoT"]
W3 = ["gate_wT", "gate_b_t32", "bi", "loraAT", "loraBT", "wiT", "woT", "bo_row"]
W4 = ["WdtpT", "WBT", "WCT", "WxsT", "WdtT", "bdt", "A_log", "Dp_col", "WoutT"]
W5 = ["WQT", "WKT", "WinvT", "WOT", "Vf_T", "Wf_T", "Uf_flat", "bf_k",
      "Vi_T", "Wi_T", "Ui_flat", "bi_k", "rw1T", "rb1", "rw2T", "rb2_t",
      "expb_t", "alpha"]
WALL = W1 + W3 + W4 + W5


def _setup_consts(nc, ctx, kb, hc):
    cs = {}
    c_onesb = ctx.enter_context(nc.sbuf_tensor("c_onesb", [128, 1], BF16))
    c_onesr = ctx.enter_context(nc.sbuf_tensor("c_onesr", [1, 128], F32))
    c_eps = ctx.enter_context(nc.sbuf_tensor("c_eps", [1, 1], F32))
    kb.dma("sync", c_onesb[:], nc.inline_tensor(hc["ones_col_bf"], "cc1").ap())
    kb.dma("sync", c_onesr[:], nc.inline_tensor(hc["ones_row"], "cc2").ap())
    kb.dma("sync", c_eps[:], nc.inline_tensor(np.full((1, 1), EPS, np.float32), "cc3").ap())
    cs["ones_col_bf"] = c_onesb
    cs["ones_row"] = c_onesr
    cs["eps11"] = c_eps
    cs["invn_hbm"] = nc.inline_tensor(hc["inv_n"], "cc4").ap()
    cs["tie_hbm"] = nc.inline_tensor(hc["tie_eps"], "ctie").ap()
    cs["tie32_hbm"] = nc.inline_tensor(hc["tie32"], "ctie32").ap()
    cs["Rl_hbm"] = nc.inline_tensor(hc["R_lora"].astype(BF), "cRl").ap()
    cs["Rk_hbm"] = nc.inline_tensor(hc["R_kr"].astype(BF), "cRk").ap()
    cs["ident_hbm"] = nc.inline_tensor(hc["identity_bf"], "cid").ap()
    cs["identf_hbm"] = nc.inline_tensor(np.eye(128, dtype=np.float32), "cidf").ap()
    return cs


def _build_fused(wshapes, hc):
    nc = bass.Bass("TRN2", target_bir_lowering=False, debug=False, detect_race_conditions=False)
    xres = nc.dram_tensor("xres", [NB, D, T], F32, kind="ExternalInput").ap()
    x4 = nc.dram_tensor("x4", [NB, D, T], F32, kind="ExternalOutput").ap()
    whbm = {n: nc.dram_tensor(n, list(wshapes[n]), F32, kind="ExternalInput").ap() for n in WALL}
    dr = lambda nm, sh, dt: nc.dram_tensor(nm, sh, dt, kind="Internal").ap()
    attnpre = dr("attnpre", [NB, D, T], BF16)
    x1f = dr("x1f", [NB, D, T], F32)
    x1b = dr("x1b", [NB, D, T], BF16)
    hsum = dr("hsum", [NB, DFF, T], BF16)
    ffnout = dr("ffnout", [NB, D, T], BF16)
    x2f = dr("x2f", [NB, D, T], F32)
    x2b = dr("x2b", [NB, D, T], BF16)
    x3f = dr("x3f", [NB, D, T], F32)
    x3b = dr("x3b", [NB, D, T], BF16)
    qk = [dr(f"qk{i}", [NB, D, T], BF16) for i in range(4)]
    g = dr("gcomb", [NB, D, T], BF16)
    x4b = dr("x4b", [NB, D, T], BF16)
    with ExitStack() as ctx:
        kb = KB(nc, ctx)
        cs = _setup_consts(nc, ctx, kb, hc)
        ps = make_psum_pool(ctx, nc)
        kb.barrier()
        with ExitStack() as pctx:
            emit_attn_phase(kb, pctx, nc, cs, ps, whbm, xres, attnpre, x1f, x1b)
            kb.barrier()
        with ExitStack() as pctx:
            emit_ffn0_phase(kb, pctx, nc, cs, ps, whbm, x1b, x1f, hsum, ffnout, x2f, x2b)
            kb.barrier()
        with ExitStack() as pctx:
            emit_ssm_phase(kb, pctx, nc, cs, ps, whbm, x2b, x2f, x3f, x3b)
            kb.barrier()
        with ExitStack() as pctx:
            emit_ffn1_phase(kb, pctx, nc, cs, ps, whbm, x3b, x3f, qk, g, x4, x4b)
            kb.barrier()
        kb.finalize()
    return nc


def kernel(**inputs):
    inputs = {k: np.asarray(v) for k, v in inputs.items()}
    w = prep_weights(inputs)
    hc = host_constants()
    NCORES = 8
    x = inputs["L0_attn_x"]
    shards = [np.ascontiguousarray(x[2 * c:2 * c + 2].transpose(0, 2, 1)) for c in range(NCORES)]
    wsh = {k: v.shape for k, v in w.items()}
    nc = _build_fused(wsh, hc)
    maps = [{"xres": shards[c], **{n: w[n] for n in WALL}} for c in range(NCORES)]
    rr = run_bass_kernel_spmd(nc, maps, core_ids=list(range(NCORES))).results
    out = np.concatenate([np.asarray(rr[c]["x4"]).transpose(0, 2, 1) for c in range(NCORES)], axis=0)
    return np.ascontiguousarray(out.astype(np.float32))
